# version 13
# baseline (speedup 1.0000x reference)
import os
import sys
import math
import contextlib

import numpy as np
import ml_dtypes

if os.path.isdir("/opt/trn_rl_repo") and "/opt/trn_rl_repo" not in sys.path:
    sys.path.insert(0, "/opt/trn_rl_repo")

import concourse.bass as bass
import concourse.mybir as mybir
from concourse.bass_utils import run_bass_kernel_spmd

F32 = mybir.dt.float32
BF16 = mybir.dt.bfloat16
AF = mybir.ActivationFunctionType
ALU = mybir.AluOpType

SCALE = 1.0 / math.sqrt(128.0)
EPS = 1e-5


def make_cfg(D, H, G, S, N, NSB, n_slots):
    cfg = dict(D=D, H=H, G=G, S=S, N=N, NSB=NSB, n_slots=n_slots)
    cfg["Q"] = S // 2
    cfg["DC"] = D // 128
    cfg["KC"] = S // 128
    cfg["QBW"] = min(512, cfg["Q"])
    cfg["NQB"] = cfg["Q"] // cfg["QBW"]
    cfg["SBW"] = min(512, cfg["Q"])
    cfg["NSBL"] = S // cfg["SBW"]
    cfg["NS"] = N // NSB
    cfg["NCK"] = NSB // 128
    return cfg


CFG_FULL = make_cfg(D=2048, H=16, G=4, S=2048, N=8192, NSB=1024, n_slots=50)
CFG_MINI = make_cfg(D=512, H=4, G=2, S=512, N=1024, NSB=512, n_slots=30)
CFG_MINI2 = make_cfg(D=512, H=4, G=2, S=2048, N=1024, NSB=512, n_slots=34)



class Buf:
    __slots__ = ("name", "last_w", "readers")

    def __init__(self, name):
        self.name = name
        self.last_w = None
        self.readers = []


class Emitter:
    ENGINES = ("pe", "sc", "dv", "g", "s")

    def __init__(self, nc):
        self.nc = nc
        self.streams = {e: [] for e in self.ENGINES}
        self.counts = {}
        self.sem_names = []
        self.hiwater = {}

    def _sem(self, name):
        if name not in self.counts:
            self.counts[name] = 0
            self.sem_names.append(name)
        return name

    def raw_wait(self, eng, sem, val):
        if val <= 0:
            return
        assert sem in self.counts and val <= self.counts[sem], (
            f"wait on future: {sem} {val} > {self.counts.get(sem)}")
        if self.hiwater.get((eng, sem), 0) >= val:
            return
        self.hiwater[(eng, sem)] = val
        self.streams[eng].append(("wait", sem, val))

    def _collect(self, need, bufs, war):
        for b in bufs:
            if b.last_w is not None:
                s, v = b.last_w
                if need.get(s, 0) < v:
                    need[s] = v
            if war:
                for s, v in b.readers:
                    if need.get(s, 0) < v:
                        need[s] = v

    def _emit_waits(self, eng, need):
        for s, v in need.items():
            self.raw_wait(eng, s, v)

    def op(self, eng, emit, reads=(), writes=(), sem=None, amt=1):
        need = {}
        self._collect(need, reads, war=False)
        self._collect(need, writes, war=True)
        self._emit_waits(eng, need)
        if sem is None:
            sem = "e_" + eng
        self._sem(sem)
        self.counts[sem] += amt
        tok = (sem, self.counts[sem])
        self.streams[eng].append(("op", emit, sem, amt))
        for b in reads:
            b.readers.append(tok)
        for b in writes:
            b.last_w = tok
            b.readers = []
        return tok

    def dma_multi(self, eng, emits, buf, sem):
        need = {}
        self._collect(need, [buf], war=True)
        self._emit_waits(eng, need)
        self._sem(sem)
        for e in emits:
            self.counts[sem] += 16
            self.streams[eng].append(("op", e, sem, 16))
        buf.last_w = (sem, self.counts[sem])
        buf.readers = []

    def group(self, eng, writes, sem=None, amt=1):
        return _Group(self, eng, list(writes), sem, amt)

    def replay(self):
        nc = self.nc
        eng_obj = {"pe": nc.tensor, "sc": nc.scalar, "dv": nc.vector,
                   "g": nc.gpsimd, "s": nc.sync}
        blk_name = {"pe": "tensor", "sc": "scalar", "dv": "vector",
                    "g": "gpsimd", "s": "sync"}
        with contextlib.ExitStack() as stack:
            handles = {}
            for name in self.sem_names:
                handles[name] = stack.enter_context(nc.semaphore(name))
            block = stack.enter_context(nc.Block())

            def make_body(eng):
                def body(_e):
                    e = eng_obj[eng]
                    for item in self.streams[eng]:
                        if item[0] == "wait":
                            e.wait_ge(handles[item[1]], item[2])
                        else:
                            _, emit, sem, amt = item
                            inst = emit()
                            if sem is not None:
                                inst.then_inc(handles[sem], amt)
                return body

            for eng in self.ENGINES:
                getattr(block, blk_name[eng])(make_body(eng))


class _Group:

    def __init__(self, em, eng, writes, sem, amt):
        self.em = em
        self.eng = eng
        self.writes = writes
        self.sem = sem if sem is not None else "e_" + eng
        self.amt = amt
        self.reads = []
        self.first = True

    def add(self, emit, reads=(), last=False):
        em = self.em
        need = {}
        em._collect(need, reads, war=False)
        if self.first:
            em._collect(need, self.writes, war=True)
            self.first = False
        em._emit_waits(self.eng, need)
        em._sem(self.sem)
        em.counts[self.sem] += self.amt
        tok = (self.sem, em.counts[self.sem])
        em.streams[self.eng].append(("op", emit, self.sem, self.amt))
        for b in reads:
            b.readers.append(tok)
        if last:
            for b in self.writes:
                b.last_w = tok
                b.readers = []
        return tok


class Pool:

    def __init__(self, nc, n):
        self.slots = []
        self.free = []
        for i in range(n):
            t = nc.alloc_sbuf_tensor(f"slot{i}", [128, 2048], BF16)
            self.slots.append(dict(bf=t, f32=t.bitcast(F32),
                                   buf=Buf(f"slot{i}"), i=i))
            self.free.append(i)

    def take(self):
        assert self.free, "pool exhausted"
        return self.slots[self.free.pop(0)]

    def give(self, slot):
        self.free.append(slot["i"])



def build_program(cfg, dbg=False):
    D, H, G, S, N = cfg["D"], cfg["H"], cfg["G"], cfg["S"], cfg["N"]
    Q, DC, KC = cfg["Q"], cfg["DC"], cfg["KC"]
    QBW, NQB = cfg["QBW"], cfg["NQB"]
    SBW, NSBL = cfg["SBW"], cfg["NSBL"]
    NSB, NS, NCK = cfg["NSB"], cfg["NS"], cfg["NCK"]
    GW = G * 128

    nc = bass.Bass("TRN2", target_bir_lowering=False, debug=False)
    em = Emitter(nc)

    xT = nc.dram_tensor("xT", [D, S], F32, kind="ExternalInput").ap()
    wq = nc.dram_tensor("wq", [H, 128, DC, 128], F32, kind="ExternalInput").ap()
    wk = nc.dram_tensor("wk", [G, 128, DC, 128], F32, kind="ExternalInput").ap()
    wv = nc.dram_tensor("wv", [D, GW], F32, kind="ExternalInput").ap()
    wo = nc.dram_tensor("wo", [DC, 128, DC, 128], F32, kind="ExternalInput").ap()
    w0 = nc.dram_tensor("w0", [D, N], F32, kind="ExternalInput").ap()
    w1 = nc.dram_tensor("w1", [N, D], F32, kind="ExternalInput").ap()
    b0f = nc.dram_tensor("b0f", [128, N // 128], F32, kind="ExternalInput").ap()
    b1f = nc.dram_tensor("b1f", [128, DC], F32, kind="ExternalInput").ap()
    ctabd = nc.dram_tensor("ctab", [128, S], BF16, kind="ExternalInput").ap()
    stabd = nc.dram_tensor("stab", [128, S], BF16, kind="ExternalInput").ap()
    swmd = nc.dram_tensor("swm", [128, 128], BF16, kind="ExternalInput").ap()
    outT = nc.dram_tensor("outT", [D, Q], F32, kind="ExternalOutput").ap()

    PT = [nc.alloc_psum_tensor(f"ps{i}", [128, 1024], F32) for i in range(4)]
    PS = [Buf(f"ps{i}_{h}") for i in range(4) for h in range(2)]

    def ps_ap(i, lo, hi):
        return PT[i // 2][:, (i % 2) * 512 + lo:(i % 2) * 512 + hi]

    swap_sb = nc.alloc_sbuf_tensor("swap_sb", [128, 128], BF16)
    b0fm = nc.alloc_sbuf_tensor("b0fm", [128, N // 128], F32)
    b1fm = nc.alloc_sbuf_tensor("b1fm", [128, DC], F32)
    ones_col = nc.alloc_sbuf_tensor("ones_col", [128, 1], BF16)
    ones_rb = nc.alloc_sbuf_tensor("ones_rb", [1, 128], BF16)
    ones_rf = nc.alloc_sbuf_tensor("ones_rf", [1, 128], F32)
    eps_r = nc.alloc_sbuf_tensor("eps_r", [1, 1], F32)
    bswap, bb0, bb1 = Buf("swap"), Buf("b0fm"), Buf("b1fm")
    bones_c, bones_rb, bones_rf = Buf("onc"), Buf("onrb"), Buf("onrf")
    beps = Buf("eps")

    pool = Pool(nc, cfg["n_slots"])

    rowA = pool.take()
    rowB = pool.take()
    bA, bB = Buf("rowA"), Buf("rowB")

    T = nc.tensor
    SC = nc.scalar
    DV = nc.vector
    GP = nc.gpsimd
    SY = nc.sync

    dbg_n = [0]

    def dump(name, ap_fn, buf, shape, dtype=BF16):
        if not dbg:
            return
        t = nc.dram_tensor("dbg_" + name, shape, dtype, kind="ExternalOutput").ap()
        em.op("s", (lambda t=t, ap_fn=ap_fn: SY.dma_start(out=t, in_=ap_fn())),
              reads=[buf], sem=f"d_dbg{dbg_n[0]}", amt=16)
        dbg_n[0] += 1

    em.op("dv", lambda: DV.memset(ones_col.ap(), 1.0), writes=[bones_c])
    em.op("dv", lambda: DV.memset(ones_rb.ap(), 1.0), writes=[bones_rb])
    em.op("dv", lambda: DV.memset(ones_rf.ap(), 1.0), writes=[bones_rf])
    em.op("dv", lambda: DV.memset(eps_r.ap(), EPS), writes=[beps])

    em.dma_multi("s", [lambda: SY.dma_start(out=swap_sb.ap(), in_=swmd)], bswap, "d_sw")
    em.dma_multi("s", [lambda: SY.dma_start(out=b0fm.ap(), in_=b0f)], bb0, "d_b0")
    em.dma_multi("s", [lambda: SY.dma_start(out=b1fm.ap(), in_=b1f)], bb1, "d_b1")

    ct_s = pool.take()
    st_s = pool.take()
    em.dma_multi("s", [lambda: SY.dma_start(out=ct_s["bf"][:, 0:S], in_=ctabd)],
                 ct_s["buf"], "d_ct")
    em.dma_multi("s", [lambda: SY.dma_start(out=st_s["bf"][:, 0:S], in_=stabd)],
                 st_s["buf"], "d_st")

    xq_s = [pool.take() for _ in range((DC + 1) // 2)]
    xo_s = [pool.take() for _ in range((DC + 1) // 2)]

    def xq_ap(dc, lo, hi):
        return xq_s[dc // 2]["bf"][:, (dc % 2) * Q + lo:(dc % 2) * Q + hi]

    def xo_ap(dc, lo, hi):
        return xo_s[dc // 2]["bf"][:, (dc % 2) * Q + lo:(dc % 2) * Q + hi]

    def xs_ap(dc, lo, hi):
        if lo >= Q:
            return xo_ap(dc, lo - Q, hi - Q)
        return xq_ap(dc, lo, hi)

    def xs_buf(dc, lo):
        return (xo_s if lo >= Q else xq_s)[dc // 2]["buf"]

    for i in range((DC + 1) // 2):
        emits_q, emits_o = [], []
        for dc in (2 * i, 2 * i + 1):
            if dc < DC:
                emits_q.append(lambda dc=dc: GP.dma_start(
                    out=xq_ap(dc, 0, Q), in_=xT[dc * 128:(dc + 1) * 128, 0:Q]))
                emits_o.append(lambda dc=dc: GP.dma_start(
                    out=xo_ap(dc, 0, Q), in_=xT[dc * 128:(dc + 1) * 128, Q:S]))
        em.dma_multi("g", emits_q, xq_s[i]["buf"], f"d_xq{i}")
        em.dma_multi("g", emits_o, xo_s[i]["buf"], f"d_xo{i}")

    nvs = max(1, (DC + 3) // 4)
    wv_s = [pool.take() for _ in range(nvs)]
    for i in range(nvs):
        emits = []
        for dc in range(i * 4, min(DC, i * 4 + 4)):
            emits.append(lambda dc=dc, i=i: GP.dma_start(
                out=wv_s[i]["bf"][:, (dc % 4) * GW:(dc % 4 + 1) * GW],
                in_=wv[dc * 128:(dc + 1) * 128, :]))
        em.dma_multi("g", emits, wv_s[i]["buf"], f"d_wv{i}")

    def wv_ap(dc):
        return wv_s[dc // 4]["bf"][:, (dc % 4) * GW:(dc % 4 + 1) * GW]

    wk_s = [pool.take() for _ in range(G)]
    for g in range(G):
        em.dma_multi("g", [lambda g=g: GP.dma_start(
            out=wk_s[g]["bf"][:, 0:DC * 128].rearrange("p (dc c) -> p dc c", c=128),
            in_=wk[g])],
            wk_s[g]["buf"], f"d_wk{g}")

    wq_s = [pool.take(), pool.take()]

    def load_wq(h):
        sl = wq_s[h % 2]
        em.dma_multi("g", [lambda h=h, sl=sl: GP.dma_start(
            out=sl["bf"][:, 0:DC * 128].rearrange("p (dc c) -> p dc c", c=128),
            in_=wq[h])],
            sl["buf"], f"d_wq{h % 2}")

    load_wq(0)
    load_wq(1)

    scr1 = pool.take()
    scr2 = pool.take()
    scr1_h = [Buf("scr1a"), Buf("scr1b")]
    scr2_h = [Buf("scr2a"), Buf("scr2b")]

    vt_s = [pool.take() for _ in range(max(1, (KC + 3) // 4))]

    def v_ap(kc, g):
        base = (kc % 4) * GW
        return vt_s[kc // 4]["bf"][:, base + g * 128: base + (g + 1) * 128]

    for kc in range(KC):
        pb = kc % 2
        grp = em.group("pe", writes=[PS[pb]])
        for dc in range(DC):
            grp.add((lambda kc=kc, dc=dc, pb=pb: T.matmul(
                ps_ap(pb, 0, GW),
                xs_ap(dc, kc * 128, (kc + 1) * 128),
                wv_ap(dc),
                start=(dc == 0), stop=(dc == DC - 1))),
                reads=[xs_buf(dc, kc * 128), wv_s[dc // 4]["buf"]],
                last=(dc == DC - 1))
        sl = vt_s[kc // 4]
        em.op("sc", (lambda kc=kc, pb=pb, sl=sl: SC.activation(
            sl["bf"][:, (kc % 4) * GW:(kc % 4 + 1) * GW],
            ps_ap(pb, 0, GW), AF.Copy)),
            reads=[PS[pb]], writes=[sl["buf"]])

    dump("v0", lambda: vt_s[0]["bf"][:, 0:GW], vt_s[0]["buf"], [128, GW])

    kr_s = [pool.take() for _ in range(G)]

    NSBP = (NSBL + 1) // 2
    for g in range(G):
        for sbp in range(NSBP):
            sbs = [s for s in (2 * sbp, 2 * sbp + 1) if s < NSBL]
            grp = em.group("pe", writes=[PS[2 + (s % 2)] for s in sbs])
            for dc in range(DC):
                for sb in sbs:
                    grp.add((lambda g=g, sb=sb, dc=dc: T.matmul(
                        ps_ap(2 + (sb % 2), 0, SBW),
                        wk_s[g]["bf"][:, dc * 128:(dc + 1) * 128],
                        xs_ap(dc, sb * SBW, (sb + 1) * SBW),
                        start=(dc == 0), stop=(dc == DC - 1))),
                        reads=[wk_s[g]["buf"], xs_buf(dc, sb * SBW)],
                        last=(dc == DC - 1 and sb == sbs[-1]))
            for sb in sbs:
                pa = sb % 2
                pw = 6 + pa
                klo = pa * 1024
                pb = 2 + pa
                em.op("sc", (lambda pb=pb, klo=klo: SC.activation(
                    scr1["bf"][:, klo:klo + SBW], ps_ap(pb, 0, SBW), AF.Copy)),
                    reads=[PS[pb]], writes=[scr1_h[pa]])
                em.op("pe", (lambda pw=pw, klo=klo: T.matmul(
                    ps_ap(pw, 0, SBW), swap_sb.ap(), scr1["bf"][:, klo:klo + SBW],
                    start=True, stop=True)),
                    reads=[bswap, scr1_h[pa]], writes=[PS[pw]])
                em.op("dv", (lambda g=g, sb=sb, klo=klo: DV.tensor_mul(
                    kr_s[g]["bf"][:, sb * SBW:(sb + 1) * SBW],
                    scr1["bf"][:, klo:klo + SBW],
                    ct_s["bf"][:, sb * SBW:(sb + 1) * SBW])),
                    reads=[scr1_h[pa], ct_s["buf"]], writes=[kr_s[g]["buf"]])
                em.op("dv", (lambda sb=sb, pw=pw, klo=klo: DV.tensor_mul(
                    scr2["bf"][:, klo:klo + SBW], ps_ap(pw, 0, SBW),
                    st_s["bf"][:, sb * SBW:(sb + 1) * SBW])),
                    reads=[PS[pw], st_s["buf"]], writes=[scr2_h[pa]])
                em.op("dv", (lambda g=g, sb=sb, klo=klo: DV.tensor_add(
                    kr_s[g]["bf"][:, sb * SBW:(sb + 1) * SBW],
                    kr_s[g]["bf"][:, sb * SBW:(sb + 1) * SBW],
                    scr2["bf"][:, klo:klo + SBW])),
                    reads=[kr_s[g]["buf"], scr2_h[pa]], writes=[kr_s[g]["buf"]])

    dump("kr0", lambda: kr_s[0]["bf"][:, 0:S], kr_s[0]["buf"], [128, S])

    for sl in wv_s + wk_s + xo_s:
        pool.give(sl)

    qr_s = [pool.take(), pool.take()]
    qr_bufs = [Buf(f"qr{i}") for i in range(4)]

    def qr_ap(h, lo, hi):
        return qr_s[(h // 2) % 2]["bf"][:, (h % 2) * 1024 + lo:(h % 2) * 1024 + hi]

    def qr_buf(h):
        return qr_bufs[h % 4]

    pt_s = pool.take()
    pt_h = [Buf("ptA"), Buf("ptB")]
    den_s = [pool.take(), pool.take()]
    den_pl = [pool.take(), pool.take()]
    attn_s = [pool.take() for _ in range((H + 1) // 2)]
    wo_s = [pool.take(), pool.take()]
    xm_s = [pool.take(), pool.take()]
    if KC >= 8:
        POOLSET = frozenset(kc for kc in range(KC) if kc % 8 < 3)
    else:
        POOLSET = frozenset([KC - 1])

    def attn_ap(h, lo, hi):
        return attn_s[h // 2]["bf"][:, (h % 2) * 1024 + lo:(h % 2) * 1024 + hi]

    def load_wo(d):
        sl = wo_s[d % 2]
        em.dma_multi("g", [lambda d=d, sl=sl: GP.dma_start(
            out=sl["bf"][:, 0:DC * 128].rearrange("p (dc c) -> p dc c", c=128),
            in_=wo[d])],
            sl["buf"], f"d_wo{d % 2}")

    def load_xm(d):
        sl = xm_s[d % 2]
        em.dma_multi("s", [lambda d=d, sl=sl: SY.dma_start(
            out=sl["f32"][:, 0:Q], in_=xT[d * 128:(d + 1) * 128, 0:Q])],
            sl["buf"], f"d_xm{d % 2}")

    def qproj_rope(h, lo_bank):
        pt_i = lo_bank // 2
        grp = em.group("pe", writes=[PS[lo_bank], PS[lo_bank + 1]])
        for dc in range(DC):
            for qb in range(NQB):
                grp.add((lambda h=h, qb=qb, dc=dc, pt_i=pt_i: T.matmul(
                    PT[pt_i][:, qb * 512:qb * 512 + QBW],
                    wq_s[h % 2]["bf"][:, dc * 128:(dc + 1) * 128],
                    xq_ap(dc, qb * QBW, (qb + 1) * QBW),
                    start=(dc == 0), stop=(dc == DC - 1))),
                    reads=[wq_s[h % 2]["buf"], xq_s[dc // 2]["buf"]],
                    last=(dc == DC - 1 and qb == NQB - 1))
        for qb in range(NQB):
            pa = qb % 2
            qlo = pa * 1024
            em.op("sc", (lambda qb=qb, qlo=qlo, pt_i=pt_i: SC.activation(
                scr1["bf"][:, qlo:qlo + QBW],
                PT[pt_i][:, qb * 512:qb * 512 + QBW], AF.Copy)),
                reads=[PS[lo_bank + qb]], writes=[scr1_h[pa]])
            em.op("pe", (lambda qb=qb, qlo=qlo, pt_i=pt_i: T.matmul(
                PT[pt_i][:, qb * 512:qb * 512 + QBW], swap_sb.ap(),
                scr1["bf"][:, qlo:qlo + QBW],
                start=True, stop=True)),
                reads=[bswap, scr1_h[pa]], writes=[PS[lo_bank + qb]])
            em.op("dv", (lambda h=h, qb=qb, qlo=qlo: DV.tensor_mul(
                qr_ap(h, qb * QBW, (qb + 1) * QBW),
                scr1["bf"][:, qlo:qlo + QBW],
                ct_s["bf"][:, qb * QBW:(qb + 1) * QBW])),
                reads=[scr1_h[pa], ct_s["buf"]], writes=[qr_buf(h)])
            em.op("dv", (lambda qb=qb, qlo=qlo, pt_i=pt_i: DV.tensor_mul(
                scr2["bf"][:, qlo:qlo + QBW],
                PT[pt_i][:, qb * 512:qb * 512 + QBW],
                st_s["bf"][:, qb * QBW:(qb + 1) * QBW])),
                reads=[PS[lo_bank + qb], st_s["buf"]], writes=[scr2_h[pa]])
            em.op("dv", (lambda h=h, qb=qb, qlo=qlo: DV.tensor_add(
                qr_ap(h, qb * QBW, (qb + 1) * QBW),
                qr_ap(h, qb * QBW, (qb + 1) * QBW),
                scr2["bf"][:, qlo:qlo + QBW])),
                reads=[qr_buf(h), scr2_h[pa]],
                writes=[qr_buf(h)])

    def st_emit(h, kc):
        g = h % G
        sb = kc % 2
        grp = em.group("pe", writes=[PS[2 * sb], PS[2 * sb + 1]])
        for qb in range(NQB):
            grp.add((lambda h=h, g=g, kc=kc, qb=qb, sb=sb: T.matmul(
                PT[sb][:, qb * QBW:(qb + 1) * QBW],
                kr_s[g]["bf"][:, kc * 128:(kc + 1) * 128],
                qr_ap(h, qb * QBW, (qb + 1) * QBW),
                start=True, stop=True)),
                reads=[kr_s[g]["buf"], qr_buf(h)],
                last=(qb == NQB - 1))

    def tail_stage(h, stage):
        par = h % 2
        av_lo = 4 + 2 * par
        dlo = par * 1024
        if stage == 0:
            em.op("dv", (lambda h=h, par=par: DV.tensor_copy(
                attn_ap(h, 0, Q), PT[2 + par][:, 0:Q])),
                reads=[PS[av_lo], PS[av_lo + 1]], writes=[attn_s[h // 2]["buf"]])
            em.op("dv", (lambda dlo=dlo, par=par: DV.tensor_add(
                scr1["bf"][:, dlo:dlo + Q], den_s[par]["f32"][:, 0:Q],
                den_pl[par]["f32"][:, 0:Q])),
                reads=[den_s[par]["buf"], den_pl[par]["buf"]],
                writes=[scr1_h[par]])
        elif stage == 1:
            dgrp = em.group("pe", writes=[PS[av_lo], PS[av_lo + 1]])
            for qb in range(NQB):
                dgrp.add((lambda qb=qb, dlo=dlo, par=par: T.matmul(
                    PT[2 + par][0:1, qb * 512:qb * 512 + QBW], ones_col.ap(),
                    scr1["bf"][:, dlo + qb * QBW:dlo + (qb + 1) * QBW],
                    start=True, stop=True)),
                    reads=[bones_c, scr1_h[par]], last=(qb == NQB - 1))
        elif stage == 2:
            em.op("sc", (lambda par=par: SC.activation(
                rowB["f32"][0:1, 0:Q], PT[2 + par][0:1, 0:Q], AF.Ln)),
                reads=[PS[av_lo], PS[av_lo + 1]], writes=[bB])
        elif stage == 3:
            em.op("sc", (lambda: SC.activation(
                rowA["bf"][0:1, 0:Q], rowB["f32"][0:1, 0:Q], AF.Exp, scale=-1.0)),
                reads=[bB], writes=[bA])
        elif stage == 4:
            bgrp = em.group("pe", writes=[PS[av_lo], PS[av_lo + 1]])
            for qb in range(NQB):
                bgrp.add((lambda qb=qb, par=par: T.matmul(
                    PT[2 + par][:, qb * QBW:(qb + 1) * QBW], ones_rb.ap(),
                    rowA["bf"][0:1, qb * QBW:(qb + 1) * QBW],
                    start=True, stop=True)),
                    reads=[bones_rb, bA], last=(qb == NQB - 1))
        else:
            em.op("dv", (lambda h=h, par=par: DV.tensor_mul(
                attn_ap(h, 0, Q), attn_ap(h, 0, Q), PT[2 + par][:, 0:Q])),
                reads=[attn_s[h // 2]["buf"], PS[av_lo], PS[av_lo + 1]],
                writes=[attn_s[h // 2]["buf"]])

    qproj_rope(0, 4)
    if H > 1:
        qproj_rope(1, 6)
    if H > 2:
        load_wq(2)
    if H > 3:
        load_wq(3)

    TAIL_AT = (1, 2, 3, 4, 5, 6)
    QPROJ_AT = tuple(range(7, 13))
    for h in range(H):
        g = h % G
        par = h % 2
        av_lo = 4 + 2 * par
        other_lo = 4 + 2 * (1 - par)
        st_emit(h, 0)
        avgrp = None
        dv_first = True
        em.op("g", (lambda par=par: GP.memset(den_pl[par]["f32"][:, 0:Q], 0.0)),
              writes=[den_pl[par]["buf"]])
        qgrp = [None]
        qh = h + 2
        in_loop_tails = sum(1 for x in TAIL_AT if x < KC)
        for kc in range(KC):
            if kc + 1 < KC:
                st_emit(h, kc + 1)
            sb = kc % 2
            ptp = kc % 2
            ptlo = ptp * 1024
            em.op("sc", (lambda ptlo=ptlo, sb=sb: SC.activation(
                pt_s["bf"][:, ptlo:ptlo + Q], PT[sb][:, 0:Q], AF.Exp, scale=SCALE)),
                reads=[PS[2 * sb], PS[2 * sb + 1]], writes=[pt_h[ptp]])
            if kc in POOLSET:
                acc, bacc = den_pl[par], den_pl[par]["buf"]
                em.op("g", (lambda acc=acc, ptlo=ptlo: GP.tensor_add(
                    acc["f32"][:, 0:Q], acc["f32"][:, 0:Q],
                    pt_s["bf"][:, ptlo:ptlo + Q])),
                    reads=[pt_h[ptp], bacc], writes=[bacc])
            else:
                acc, bacc = den_s[par], den_s[par]["buf"]
                if dv_first:
                    em.op("dv", (lambda acc=acc, ptlo=ptlo: DV.tensor_copy(
                        acc["f32"][:, 0:Q], pt_s["bf"][:, ptlo:ptlo + Q])),
                        reads=[pt_h[ptp]], writes=[bacc])
                    dv_first = False
                else:
                    em.op("dv", (lambda acc=acc, ptlo=ptlo: DV.tensor_add(
                        acc["f32"][:, 0:Q], acc["f32"][:, 0:Q],
                        pt_s["bf"][:, ptlo:ptlo + Q])),
                        reads=[pt_h[ptp], bacc], writes=[bacc])
            if kc == 0:
                avgrp = em.group("pe", writes=[PS[av_lo], PS[av_lo + 1]])
            if qh < H and KC > QPROJ_AT[0]:
                if kc == QPROJ_AT[0]:
                    qgrp[0] = em.group("pe", writes=[PS[other_lo], PS[other_lo + 1]])
                if qgrp[0] is not None and kc >= QPROJ_AT[0]:
                    n_mem = DC * NQB
                    n_slots_q = KC - QPROJ_AT[0]
                    per = (n_mem + n_slots_q - 1) // n_slots_q
                    i0 = (kc - QPROJ_AT[0]) * per
                    for i in range(i0, min(i0 + per, n_mem)):
                        dc, qb = divmod(i, NQB)
                        qgrp[0].add((lambda qh=qh, qb=qb, dc=dc, other_lo=other_lo: T.matmul(
                            PT[other_lo // 2][:, qb * 512:qb * 512 + QBW],
                            wq_s[qh % 2]["bf"][:, dc * 128:(dc + 1) * 128],
                            xq_ap(dc, qb * QBW, (qb + 1) * QBW),
                            start=(dc == 0), stop=(dc == DC - 1))),
                            reads=[wq_s[qh % 2]["buf"], xq_s[dc // 2]["buf"]],
                            last=(i == n_mem - 1))
            for qb in range(NQB):
                avgrp.add((lambda kc=kc, g=g, qb=qb, par=par, ptlo=ptlo: T.matmul(
                    PT[2 + par][:, qb * QBW:(qb + 1) * QBW],
                    v_ap(kc, g),
                    pt_s["bf"][:, ptlo + qb * QBW:ptlo + (qb + 1) * QBW],
                    start=(kc == 0), stop=(kc == KC - 1))),
                    reads=[vt_s[kc // 4]["buf"], pt_h[ptp]],
                    last=(kc == KC - 1 and qb == NQB - 1))
            if h > 0 and kc in TAIL_AT:
                tail_stage(h - 1, TAIL_AT.index(kc))
        if h > 0:
            for stage in range(in_loop_tails, 6):
                tail_stage(h - 1, stage)
        if qh < H:
            if KC <= QPROJ_AT[0]:
                qproj_rope(qh, other_lo)
            else:
                for qb in range(NQB):
                    pa = qb % 2
                    qlo = pa * 1024
                    pt_i = other_lo // 2
                    em.op("dv", (lambda qb=qb, qlo=qlo, pt_i=pt_i: DV.tensor_copy(
                        scr2["bf"][:, qlo:qlo + QBW],
                        PT[pt_i][:, qb * 512:qb * 512 + QBW])),
                        reads=[PS[other_lo + qb]], writes=[scr2_h[pa]])
                    em.op("pe", (lambda qb=qb, qlo=qlo, pt_i=pt_i: T.matmul(
                        PT[pt_i][:, qb * 512:qb * 512 + QBW], swap_sb.ap(),
                        scr2["bf"][:, qlo:qlo + QBW],
                        start=True, stop=True)),
                        reads=[bswap, scr2_h[pa]], writes=[PS[other_lo + qb]])
                    em.op("dv", (lambda qh=qh, qb=qb, qlo=qlo: DV.tensor_mul(
                        qr_ap(qh, qb * QBW, (qb + 1) * QBW),
                        scr2["bf"][:, qlo:qlo + QBW],
                        ct_s["bf"][:, qb * QBW:(qb + 1) * QBW])),
                        reads=[scr2_h[pa], ct_s["buf"]],
                        writes=[qr_buf(qh)])
                    em.op("dv", (lambda qb=qb, qlo=qlo, pt_i=pt_i: DV.tensor_mul(
                        scr2["bf"][:, qlo:qlo + QBW],
                        PT[pt_i][:, qb * 512:qb * 512 + QBW],
                        st_s["bf"][:, qb * QBW:(qb + 1) * QBW])),
                        reads=[PS[other_lo + qb], st_s["buf"]], writes=[scr2_h[pa]])
                    em.op("dv", (lambda qh=qh, qb=qb, qlo=qlo: DV.tensor_add(
                        qr_ap(qh, qb * QBW, (qb + 1) * QBW),
                        qr_ap(qh, qb * QBW, (qb + 1) * QBW),
                        scr2["bf"][:, qlo:qlo + QBW])),
                        reads=[qr_buf(qh), scr2_h[pa]],
                        writes=[qr_buf(qh)])
            if qh + 2 < H:
                load_wq(qh + 2)
        if h == 0:
            load_wo(0)
            load_xm(0)
        if h == 1:
            load_wo(1)
            load_xm(1)
    for stage in range(6):
        tail_stage(H - 1, stage)

    dump("attn0", lambda: attn_ap(0, 0, Q), attn_s[0]["buf"], [128, Q])

    for b in pt_h:
        if b.last_w is not None:
            pt_s["buf"].readers.append(b.last_w)
        pt_s["buf"].readers.extend(b.readers)
    for i, b in enumerate(qr_bufs):
        sl = qr_s[i // 2]
        if b.last_w is not None:
            sl["buf"].readers.append(b.last_w)
        sl["buf"].readers.extend(b.readers)

    for sl in kr_s + vt_s + qr_s + [pt_s] + den_s + den_pl + xq_s + [ct_s, st_s] + wq_s:
        pool.give(sl)

    racc = [pool.take() for _ in range(DC)]
    w0_s = [pool.take() for _ in range((DC + 1) // 2)]
    ab_s = [pool.take(), pool.take()]
    bab = [Buf("ab0"), Buf("ab1")]
    x1b_s = []
    w1_s = []
    dg_s = []
    x1b_bufs = [Buf(f"x1b{dc}") for dc in range(DC)]

    def x1b_ap(dc, lo, hi):
        return x1b_s[dc // 2]["bf"][:, (dc % 2) * 1024 + lo:(dc % 2) * 1024 + hi]

    def w0_ap(dc, lo, hi):
        return w0_s[dc // 2]["bf"][:, (dc % 2) * NSB + lo:(dc % 2) * NSB + hi]

    def dg_ap(ncl, lo, hi):
        return dg_s[ncl // 2]["bf"][:, (ncl % 2) * 1024 + lo:(ncl % 2) * 1024 + hi]

    def load_w0(ns):
        for i in range((DC + 1) // 2):
            emits = []
            for dc in (2 * i, 2 * i + 1):
                if dc < DC:
                    emits.append(lambda ns=ns, dc=dc: GP.dma_start(
                        out=w0_ap(dc, 0, NSB),
                        in_=w0[dc * 128:(dc + 1) * 128, ns * NSB:(ns + 1) * NSB]))
            em.dma_multi("g", emits, w0_s[i]["buf"], f"d_w0_{i}")

    def load_w1(ns):
        for ncl in range(NCK):
            row = ns * NSB + ncl * 128
            em.dma_multi("g", [lambda row=row, ncl=ncl: GP.dma_start(
                out=w1_s[ncl]["bf"][:, 0:D], in_=w1[row:row + 128, :])],
                w1_s[ncl]["buf"], f"d_w1_{ncl}")

    load_w0(0)

    stat_groups = []
    for qb in range(NQB):
        stat_groups.append(em.group("pe", writes=[PS[4 + qb]]))
    for qb in range(NQB):
        stat_groups.append(em.group("pe", writes=[PS[6 + qb]]))

    def stats_add(stat_groups, dc, src_scr, src_bufs, which, last):
        for qb in range(NQB):
            stat_groups[which * NQB + qb].add(
                (lambda dc=dc, qb=qb, which=which, src_scr=src_scr: T.matmul(
                    PT[2 + which][0:1, qb * 512:qb * 512 + QBW], ones_col.ap(),
                    src_scr["bf"][:, (dc % 2) * 1024 + qb * QBW:
                                  (dc % 2) * 1024 + (qb + 1) * QBW],
                    start=(dc == 0), stop=last)),
                reads=[src_bufs[dc % 2], bones_c], last=last)

    for d in range(DC):
        par = d % 2
        st_halves = [PS[2 * par], PS[2 * par + 1]]
        grp = em.group("pe", writes=st_halves)
        for dvc in range(DC):
            for qb in range(NQB):
                grp.add((lambda d=d, qb=qb, dvc=dvc, par=par: T.matmul(
                    PT[par][:, qb * QBW:(qb + 1) * QBW],
                    wo_s[d % 2]["bf"][:, dvc * 128:(dvc + 1) * 128],
                    attn_ap(dvc, qb * QBW, (qb + 1) * QBW),
                    start=(dvc == 0), stop=(dvc == DC - 1))),
                    reads=[wo_s[d % 2]["buf"], attn_s[dvc // 2]["buf"]],
                    last=(qb == NQB - 1 and dvc == DC - 1))
        em.op("dv", (lambda d=d, par=par: DV.tensor_add(
            racc[d]["f32"][:, 0:Q], PT[par][:, 0:Q], xm_s[d % 2]["f32"][:, 0:Q])),
            reads=st_halves + [xm_s[d % 2]["buf"]], writes=[racc[d]["buf"]])
        if d == 0:
            dump("r1_0", lambda: racc[0]["f32"][:, 0:Q], racc[0]["buf"],
                 [128, Q], F32)
        if d + 2 < DC:
            load_wo(d + 2)
            load_xm(d + 2)
        em.op("sc", (lambda d=d, par=par: SC.activation(
            scr1["bf"][:, par * 1024:par * 1024 + Q], racc[d]["f32"][:, 0:Q], AF.Copy)),
            reads=[racc[d]["buf"]], writes=[scr1_h[par]])
        em.op("sc", (lambda par=par: SC.activation(
            scr2["bf"][:, par * 1024:par * 1024 + Q],
            scr1["bf"][:, par * 1024:par * 1024 + Q], AF.Square)),
            reads=[scr1_h[par]], writes=[scr2_h[par]])
        if d > 0:
            stats_add(stat_groups, d - 1, scr1, scr1_h, 0, last=False)
            stats_add(stat_groups, d - 1, scr2, scr2_h, 1, last=False)
    stats_add(stat_groups, DC - 1, scr1, scr1_h, 0, last=True)
    stats_add(stat_groups, DC - 1, scr2, scr2_h, 1, last=True)

    pool.give(wo_s[0])
    pool.give(wo_s[1])
    pool.give(xm_s[0])
    pool.give(xm_s[1])
    for sl in attn_s:
        pool.give(sl)
    x1b_s.extend(pool.take() for _ in range((DC + 1) // 2))
    w1_s.extend(pool.take() for _ in range(NCK))
    dg_s.extend(pool.take() for _ in range((NCK + 1) // 2))

    def ln_finalize():
        em.op("sc", (lambda: SC.activation(rowA["f32"][0:1, 0:Q], PT[2][0:1, 0:Q],
                                           AF.Copy, scale=1.0 / D)),
              reads=[PS[4], PS[5]], writes=[bA])
        em.op("sc", (lambda: SC.activation(rowB["f32"][0:1, 0:Q], PT[3][0:1, 0:Q],
                                           AF.Copy, scale=1.0 / D)),
              reads=[PS[6], PS[7]], writes=[bB])
        em.op("sc", (lambda: SC.activation(PT[0][0:1, 0:Q], rowA["f32"][0:1, 0:Q],
                                           AF.Square)),
              reads=[bA], writes=[PS[0], PS[1]])
        em.op("dv", (lambda: DV.tensor_sub(PT[1][0:1, 0:Q], rowB["f32"][0:1, 0:Q],
                                           PT[0][0:1, 0:Q])),
              reads=[bB, PS[0], PS[1]], writes=[PS[2], PS[3]])
        em.op("sc", (lambda: SC.activation(PT[0][0:1, 0:Q], PT[1][0:1, 0:Q],
                                           AF.Ln, bias=eps_r[0:1, 0:1])),
              reads=[PS[2], PS[3], beps], writes=[PS[0], PS[1]])
        em.op("sc", (lambda: SC.activation(rowB["f32"][0:1, 0:Q], PT[0][0:1, 0:Q],
                                           AF.Exp, scale=-0.5)),
              reads=[PS[0], PS[1]], writes=[bB])
        em.op("dv", (lambda: DV.scalar_tensor_tensor(
            rowA["f32"][0:1, 0:Q], rowA["f32"][0:1, 0:Q], -1.0,
            rowB["f32"][0:1, 0:Q], ALU.mult, ALU.mult)),
            reads=[bA, bB], writes=[bA])
        agrp = em.group("pe", writes=[PS[0], PS[1]])
        for qb in range(NQB):
            agrp.add((lambda qb=qb: T.matmul(
                PT[0][:, qb * QBW:(qb + 1) * QBW], ones_rf.ap(),
                rowB["f32"][0:1, qb * QBW:(qb + 1) * QBW], start=True, stop=True)),
                reads=[bones_rf, bB], last=(qb == NQB - 1))
        bgrp = em.group("pe", writes=[PS[2], PS[3]])
        for qb in range(NQB):
            bgrp.add((lambda qb=qb: T.matmul(
                PT[1][:, qb * QBW:(qb + 1) * QBW], ones_rf.ap(),
                rowA["f32"][0:1, qb * QBW:(qb + 1) * QBW], start=True, stop=True)),
                reads=[bones_rf, bA], last=(qb == NQB - 1))
        em.op("sc", (lambda: SC.activation(ab_s[0]["bf"][:, 0:Q], PT[0][:, 0:Q],
                                           AF.Copy)),
              reads=[PS[0], PS[1]], writes=[bab[0]])
        em.op("sc", (lambda: SC.activation(ab_s[1]["bf"][:, 0:Q], PT[1][:, 0:Q],
                                           AF.Copy)),
              reads=[PS[2], PS[3]], writes=[bab[1]])

    ln_finalize()

    for dc in range(DC):
        eng = "g" if dc % 3 == 2 else "dv"
        E = GP if eng == "g" else DV
        em.op(eng, (lambda E=E, dc=dc: E.tensor_mul(
            racc[dc]["f32"][:, 0:Q], racc[dc]["f32"][:, 0:Q],
            ab_s[0]["bf"][:, 0:Q])),
            reads=[racc[dc]["buf"], bab[0]], writes=[racc[dc]["buf"]])
        em.op(eng, (lambda E=E, dc=dc: E.tensor_add(
            racc[dc]["f32"][:, 0:Q], racc[dc]["f32"][:, 0:Q],
            ab_s[1]["bf"][:, 0:Q])),
            reads=[racc[dc]["buf"], bab[1]], writes=[racc[dc]["buf"]])
        em.op("sc", (lambda dc=dc: SC.activation(
            x1b_ap(dc, 0, Q), racc[dc]["f32"][:, 0:Q], AF.Copy)),
            reads=[racc[dc]["buf"]], writes=[x1b_bufs[dc]])

    dump("x1_0", lambda: racc[0]["f32"][:, 0:Q], racc[0]["buf"], [128, Q], F32)

    load_w1(0)

    for ns in range(NS):
        for ncl in range(NCK):
            par = ncl % 2
            st_halves = [PS[2 * par], PS[2 * par + 1]]
            grp = em.group("pe", writes=st_halves)
            for dc in range(DC):
                for qb in range(NQB):
                    grp.add((lambda ncl=ncl, qb=qb, dc=dc, par=par: T.matmul(
                        PT[par][:, qb * QBW:(qb + 1) * QBW],
                        w0_ap(dc, ncl * 128, (ncl + 1) * 128),
                        x1b_ap(dc, qb * QBW, (qb + 1) * QBW),
                        start=(dc == 0), stop=(dc == DC - 1))),
                        reads=[w0_s[dc // 2]["buf"], x1b_bufs[dc]],
                        last=(qb == NQB - 1 and dc == DC - 1))
            col = ns * NCK + ncl
            em.op("sc", (lambda ncl=ncl, par=par, col=col: SC.activation(
                dg_ap(ncl, 0, Q), PT[par][:, 0:Q], AF.Gelu,
                bias=b0fm[:, col:col + 1])),
                reads=st_halves + [bb0], writes=[dg_s[ncl // 2]["buf"]])
            if ns == 0 and ncl == 0:
                dump("dg0", lambda: dg_ap(0, 0, Q), dg_s[0]["buf"], [128, Q])
        if ns + 1 < NS:
            load_w0(ns + 1)
        for dc2 in range(DC):
            par = dc2 % 2
            av_halves = [PS[4 + 2 * par], PS[5 + 2 * par]]
            grp = em.group("pe", writes=av_halves)
            for ncl in range(NCK):
                for qb in range(NQB):
                    grp.add((lambda dc2=dc2, qb=qb, ncl=ncl, par=par: T.matmul(
                        PT[2 + par][:, qb * QBW:(qb + 1) * QBW],
                        w1_s[ncl]["bf"][:, dc2 * 128:(dc2 + 1) * 128],
                        dg_ap(ncl, qb * QBW, (qb + 1) * QBW),
                        start=(ncl == 0), stop=(ncl == NCK - 1))),
                        reads=[w1_s[ncl]["buf"], dg_s[ncl // 2]["buf"]],
                        last=(qb == NQB - 1 and ncl == NCK - 1))
            if ns < NS - 1:
                em.op("dv", (lambda dc2=dc2, par=par: DV.tensor_add(
                    racc[dc2]["f32"][:, 0:Q], racc[dc2]["f32"][:, 0:Q],
                    PT[2 + par][:, 0:Q])),
                    reads=av_halves + [racc[dc2]["buf"]], writes=[racc[dc2]["buf"]])
            else:
                em.op("dv", (lambda dc2=dc2, par=par: DV.scalar_tensor_tensor(
                    racc[dc2]["f32"][:, 0:Q], PT[2 + par][:, 0:Q],
                    b1fm[:, dc2:dc2 + 1], racc[dc2]["f32"][:, 0:Q],
                    ALU.add, ALU.add)),
                    reads=av_halves + [racc[dc2]["buf"], bb1],
                    writes=[racc[dc2]["buf"]])
        if ns + 1 < NS:
            load_w1(ns + 1)

    stat_groups = []
    for qb in range(NQB):
        stat_groups.append(em.group("pe", writes=[PS[4 + qb]]))
    for qb in range(NQB):
        stat_groups.append(em.group("pe", writes=[PS[6 + qb]]))
    for dc in range(DC):
        par = dc % 2
        em.op("sc", (lambda dc=dc, par=par: SC.activation(
            scr1["bf"][:, par * 1024:par * 1024 + Q], racc[dc]["f32"][:, 0:Q],
            AF.Copy)),
            reads=[racc[dc]["buf"]], writes=[scr1_h[par]])
        em.op("sc", (lambda par=par: SC.activation(
            scr2["bf"][:, par * 1024:par * 1024 + Q],
            scr1["bf"][:, par * 1024:par * 1024 + Q], AF.Square)),
            reads=[scr1_h[par]], writes=[scr2_h[par]])
        if dc > 0:
            stats_add(stat_groups, dc - 1, scr1, scr1_h, 0, last=False)
            stats_add(stat_groups, dc - 1, scr2, scr2_h, 1, last=False)
    stats_add(stat_groups, DC - 1, scr1, scr1_h, 0, last=True)
    stats_add(stat_groups, DC - 1, scr2, scr2_h, 1, last=True)

    ln_finalize()

    for dc in range(DC):
        eng = "g" if dc % 3 == 2 else "dv"
        E = GP if eng == "g" else DV
        em.op(eng, (lambda E=E, dc=dc: E.tensor_mul(
            racc[dc]["f32"][:, 0:Q], racc[dc]["f32"][:, 0:Q],
            ab_s[0]["bf"][:, 0:Q])),
            reads=[racc[dc]["buf"], bab[0]], writes=[racc[dc]["buf"]])
        em.op(eng, (lambda E=E, dc=dc: E.tensor_add(
            racc[dc]["f32"][:, 0:Q], racc[dc]["f32"][:, 0:Q],
            ab_s[1]["bf"][:, 0:Q])),
            reads=[racc[dc]["buf"], bab[1]], writes=[racc[dc]["buf"]])
        em.op("s", (lambda dc=dc: SY.dma_start(
            out=outT[dc * 128:(dc + 1) * 128, :], in_=racc[dc]["f32"][:, 0:Q])),
            reads=[racc[dc]["buf"]], sem="d_out", amt=16)

    em.raw_wait("s", "d_out", 16 * DC)
    for i in range(dbg_n[0]):
        em.raw_wait("s", f"d_dbg{i}", 16)

    em.replay()
    return nc



def rope_tables(positions):
    half = 64
    freqs = (1.0 / 10000.0 ** (np.arange(half, dtype=np.float64) / half))
    ang = np.outer(freqs, positions.astype(np.float64))
    c = np.cos(ang)
    s = np.sin(ang)
    C = np.ascontiguousarray(np.vstack([c, c]).astype(ml_dtypes.bfloat16))
    Ssn = np.ascontiguousarray(np.vstack([-s, s]).astype(ml_dtypes.bfloat16))
    return C, Ssn


def perm_idx_heads(nheads):
    hd = 128
    order = np.concatenate([np.arange(0, hd, 2), np.arange(1, hd, 2)])
    return np.concatenate([h * hd + order for h in range(nheads)])


def prep_shared(cfg, W_q, W_k, W_v, W_o, W0, b0, W1, b1):
    D, H, G, N, DC = cfg["D"], cfg["H"], cfg["G"], cfg["N"], cfg["DC"]
    sh = {}
    Wqp = W_q[:, perm_idx_heads(H)]
    Wkp = W_k[:, perm_idx_heads(G)]
    sh["wq"] = np.ascontiguousarray(
        Wqp.reshape(DC, 128, H, 128).transpose(2, 1, 0, 3)).astype(np.float32)
    sh["wk"] = np.ascontiguousarray(
        Wkp.reshape(DC, 128, G, 128).transpose(2, 1, 0, 3)).astype(np.float32)
    sh["wv"] = np.ascontiguousarray(W_v).astype(np.float32)
    sh["wo"] = np.ascontiguousarray(
        W_o.reshape(DC, 128, DC, 128).transpose(2, 1, 0, 3)).astype(np.float32)
    sh["w0"] = np.ascontiguousarray(W0.T).astype(np.float32)
    sh["w1"] = np.ascontiguousarray(W1.T).astype(np.float32)
    sh["b0f"] = np.ascontiguousarray(b0.reshape(N // 128, 128).T).astype(np.float32)
    sh["b1f"] = np.ascontiguousarray(b1.reshape(DC, 128).T).astype(np.float32)
    sh["swm"] = np.roll(np.eye(128, dtype=np.float32), 64, axis=0).astype(
        ml_dtypes.bfloat16)
    return sh


def prep_core(cfg, shared, xb, half):
    Q = cfg["Q"]
    my = np.arange(half * Q, (half + 1) * Q)
    other = np.arange((1 - half) * Q, (2 - half) * Q)
    cols = np.concatenate([my, other])
    d = dict(shared)
    d["xT"] = np.ascontiguousarray(xb[cols, :].T).astype(np.float32)
    C, Ssn = rope_tables(cols)
    d["ctab"] = C
    d["stab"] = Ssn
    return d


_PROG_CACHE = {}


def _get_prog(key, cfg):
    if key not in _PROG_CACHE:
        _PROG_CACHE[key] = build_program(cfg)
    return _PROG_CACHE[key]


def kernel(L0_attn_x, L0_attn_W_q, L0_attn_W_k, L0_attn_W_v, L0_attn_W_o,
           L0_ffn_W0, L0_ffn_bias0, L0_ffn_W1, L0_ffn_bias1):
    cfg = CFG_FULL
    x = np.asarray(L0_attn_x, dtype=np.float32)
    W_q = np.asarray(L0_attn_W_q, dtype=np.float32)
    W_k = np.asarray(L0_attn_W_k, dtype=np.float32)
    W_v = np.asarray(L0_attn_W_v, dtype=np.float32)
    W_o = np.asarray(L0_attn_W_o, dtype=np.float32)
    W0 = np.asarray(L0_ffn_W0, dtype=np.float32)
    b0 = np.asarray(L0_ffn_bias0, dtype=np.float32)
    W1 = np.asarray(L0_ffn_W1, dtype=np.float32)
    b1 = np.asarray(L0_ffn_bias1, dtype=np.float32)

    shared = prep_shared(cfg, W_q, W_k, W_v, W_o, W0, b0, W1, b1)
    in_maps = [prep_core(cfg, shared, x[c // 2], c % 2) for c in range(8)]
    nc = _get_prog("full", cfg)
    res = run_bass_kernel_spmd(nc, in_maps, core_ids=list(range(8)))
    Q = cfg["Q"]
    out = np.empty((4, 2048, 2048), dtype=np.float32)
    for c in range(8):
        b, half = c // 2, c % 2
        out[b, half * Q:(half + 1) * Q, :] = np.asarray(res.results[c]["outT"]).T
    return out


# version 14
# speedup vs baseline: 1.0152x; 1.0152x over previous
import os
import sys
import math
import contextlib

import numpy as np
import ml_dtypes

if os.path.isdir("/opt/trn_rl_repo") and "/opt/trn_rl_repo" not in sys.path:
    sys.path.insert(0, "/opt/trn_rl_repo")

import concourse.bass as bass
import concourse.mybir as mybir
from concourse.bass_utils import run_bass_kernel_spmd

F32 = mybir.dt.float32
BF16 = mybir.dt.bfloat16
AF = mybir.ActivationFunctionType
ALU = mybir.AluOpType

SCALE = 1.0 / math.sqrt(128.0)
EPS = 1e-5


def make_cfg(D, H, G, S, N, NSB, n_slots):
    cfg = dict(D=D, H=H, G=G, S=S, N=N, NSB=NSB, n_slots=n_slots)
    cfg["Q"] = S // 2
    cfg["DC"] = D // 128
    cfg["KC"] = S // 128
    cfg["QBW"] = min(512, cfg["Q"])
    cfg["NQB"] = cfg["Q"] // cfg["QBW"]
    cfg["SBW"] = min(512, cfg["Q"])
    cfg["NSBL"] = S // cfg["SBW"]
    cfg["NS"] = N // NSB
    cfg["NCK"] = NSB // 128
    return cfg


CFG_FULL = make_cfg(D=2048, H=16, G=4, S=2048, N=8192, NSB=1024, n_slots=50)
CFG_MINI = make_cfg(D=512, H=4, G=2, S=512, N=1024, NSB=512, n_slots=30)
CFG_MINI2 = make_cfg(D=512, H=4, G=2, S=2048, N=1024, NSB=512, n_slots=34)



class Buf:
    __slots__ = ("name", "last_w", "readers")

    def __init__(self, name):
        self.name = name
        self.last_w = None
        self.readers = []


class Emitter:
    ENGINES = ("pe", "sc", "dv", "g", "s")

    def __init__(self, nc):
        self.nc = nc
        self.streams = {e: [] for e in self.ENGINES}
        self.counts = {}
        self.sem_names = []
        self.hiwater = {}

    def _sem(self, name):
        if name not in self.counts:
            self.counts[name] = 0
            self.sem_names.append(name)
        return name

    def raw_wait(self, eng, sem, val):
        if val <= 0:
            return
        assert sem in self.counts and val <= self.counts[sem], (
            f"wait on future: {sem} {val} > {self.counts.get(sem)}")
        if self.hiwater.get((eng, sem), 0) >= val:
            return
        self.hiwater[(eng, sem)] = val
        self.streams[eng].append(("wait", sem, val))

    def _collect(self, need, bufs, war):
        for b in bufs:
            if b.last_w is not None:
                s, v = b.last_w
                if need.get(s, 0) < v:
                    need[s] = v
            if war:
                for s, v in b.readers:
                    if need.get(s, 0) < v:
                        need[s] = v

    def _emit_waits(self, eng, need):
        for s, v in need.items():
            self.raw_wait(eng, s, v)

    def op(self, eng, emit, reads=(), writes=(), sem=None, amt=1):
        need = {}
        self._collect(need, reads, war=False)
        self._collect(need, writes, war=True)
        self._emit_waits(eng, need)
        if sem is None:
            sem = "e_" + eng
        self._sem(sem)
        self.counts[sem] += amt
        tok = (sem, self.counts[sem])
        self.streams[eng].append(("op", emit, sem, amt))
        for b in reads:
            b.readers.append(tok)
        for b in writes:
            b.last_w = tok
            b.readers = []
        return tok

    def dma_multi(self, eng, emits, buf, sem):
        need = {}
        self._collect(need, [buf], war=True)
        self._emit_waits(eng, need)
        self._sem(sem)
        for e in emits:
            self.counts[sem] += 16
            self.streams[eng].append(("op", e, sem, 16))
        buf.last_w = (sem, self.counts[sem])
        buf.readers = []

    def group(self, eng, writes, sem=None, amt=1):
        return _Group(self, eng, list(writes), sem, amt)

    def replay(self):
        nc = self.nc
        eng_obj = {"pe": nc.tensor, "sc": nc.scalar, "dv": nc.vector,
                   "g": nc.gpsimd, "s": nc.sync}
        blk_name = {"pe": "tensor", "sc": "scalar", "dv": "vector",
                    "g": "gpsimd", "s": "sync"}
        with contextlib.ExitStack() as stack:
            handles = {}
            for name in self.sem_names:
                handles[name] = stack.enter_context(nc.semaphore(name))
            block = stack.enter_context(nc.Block())

            def make_body(eng):
                def body(_e):
                    e = eng_obj[eng]
                    for item in self.streams[eng]:
                        if item[0] == "wait":
                            e.wait_ge(handles[item[1]], item[2])
                        else:
                            _, emit, sem, amt = item
                            inst = emit()
                            if sem is not None:
                                inst.then_inc(handles[sem], amt)
                return body

            for eng in self.ENGINES:
                getattr(block, blk_name[eng])(make_body(eng))


class _Group:

    def __init__(self, em, eng, writes, sem, amt):
        self.em = em
        self.eng = eng
        self.writes = writes
        self.sem = sem if sem is not None else "e_" + eng
        self.amt = amt
        self.reads = []
        self.first = True

    def add(self, emit, reads=(), last=False):
        em = self.em
        need = {}
        em._collect(need, reads, war=False)
        if self.first:
            em._collect(need, self.writes, war=True)
            self.first = False
        em._emit_waits(self.eng, need)
        em._sem(self.sem)
        em.counts[self.sem] += self.amt
        tok = (self.sem, em.counts[self.sem])
        em.streams[self.eng].append(("op", emit, self.sem, self.amt))
        for b in reads:
            b.readers.append(tok)
        if last:
            for b in self.writes:
                b.last_w = tok
                b.readers = []
        return tok


class Pool:

    def __init__(self, nc, n):
        self.slots = []
        self.free = []
        for i in range(n):
            t = nc.alloc_sbuf_tensor(f"slot{i}", [128, 2048], BF16)
            self.slots.append(dict(bf=t, f32=t.bitcast(F32),
                                   buf=Buf(f"slot{i}"), i=i))
            self.free.append(i)

    def take(self):
        assert self.free, "pool exhausted"
        return self.slots[self.free.pop(0)]

    def give(self, slot):
        self.free.append(slot["i"])



def build_program(cfg, dbg=False):
    D, H, G, S, N = cfg["D"], cfg["H"], cfg["G"], cfg["S"], cfg["N"]
    Q, DC, KC = cfg["Q"], cfg["DC"], cfg["KC"]
    QBW, NQB = cfg["QBW"], cfg["NQB"]
    SBW, NSBL = cfg["SBW"], cfg["NSBL"]
    NSB, NS, NCK = cfg["NSB"], cfg["NS"], cfg["NCK"]
    GW = G * 128

    nc = bass.Bass("TRN2", target_bir_lowering=False, debug=False)
    em = Emitter(nc)

    xT = nc.dram_tensor("xT", [D, S], F32, kind="ExternalInput").ap()
    wq = nc.dram_tensor("wq", [H, 128, DC, 128], F32, kind="ExternalInput").ap()
    wk = nc.dram_tensor("wk", [G, 128, DC, 128], F32, kind="ExternalInput").ap()
    wv = nc.dram_tensor("wv", [D, GW], F32, kind="ExternalInput").ap()
    wo = nc.dram_tensor("wo", [DC, 128, DC, 128], F32, kind="ExternalInput").ap()
    w0 = nc.dram_tensor("w0", [D, N], F32, kind="ExternalInput").ap()
    w1 = nc.dram_tensor("w1", [N, D], F32, kind="ExternalInput").ap()
    b0f = nc.dram_tensor("b0f", [128, N // 128], F32, kind="ExternalInput").ap()
    b1f = nc.dram_tensor("b1f", [128, DC], F32, kind="ExternalInput").ap()
    ctabd = nc.dram_tensor("ctab", [128, S], BF16, kind="ExternalInput").ap()
    stabd = nc.dram_tensor("stab", [128, S], BF16, kind="ExternalInput").ap()
    swmd = nc.dram_tensor("swm", [128, 128], BF16, kind="ExternalInput").ap()
    outT = nc.dram_tensor("outT", [D, Q], F32, kind="ExternalOutput").ap()

    PT = [nc.alloc_psum_tensor(f"ps{i}", [128, 1024], F32) for i in range(4)]
    PS = [Buf(f"ps{i}_{h}") for i in range(4) for h in range(2)]

    def ps_ap(i, lo, hi):
        return PT[i // 2][:, (i % 2) * 512 + lo:(i % 2) * 512 + hi]

    swap_sb = nc.alloc_sbuf_tensor("swap_sb", [128, 128], BF16)
    b0fm = nc.alloc_sbuf_tensor("b0fm", [128, N // 128], F32)
    b1fm = nc.alloc_sbuf_tensor("b1fm", [128, DC], F32)
    ones_col = nc.alloc_sbuf_tensor("ones_col", [128, 1], BF16)
    ones_rb = nc.alloc_sbuf_tensor("ones_rb", [1, 128], BF16)
    ones_rf = nc.alloc_sbuf_tensor("ones_rf", [1, 128], F32)
    eps_r = nc.alloc_sbuf_tensor("eps_r", [1, 1], F32)
    bswap, bb0, bb1 = Buf("swap"), Buf("b0fm"), Buf("b1fm")
    bones_c, bones_rb, bones_rf = Buf("onc"), Buf("onrb"), Buf("onrf")
    beps = Buf("eps")

    pool = Pool(nc, cfg["n_slots"])

    rowA = pool.take()
    rowB = pool.take()
    bA, bB = Buf("rowA"), Buf("rowB")

    T = nc.tensor
    SC = nc.scalar
    DV = nc.vector
    GP = nc.gpsimd
    SY = nc.sync

    dbg_n = [0]

    def dump(name, ap_fn, buf, shape, dtype=BF16):
        if not dbg:
            return
        bufs = buf if isinstance(buf, list) else [buf]
        t = nc.dram_tensor("dbg_" + name, shape, dtype, kind="ExternalOutput").ap()
        em.op("s", (lambda t=t, ap_fn=ap_fn: SY.dma_start(out=t, in_=ap_fn())),
              reads=bufs, sem=f"d_dbg{dbg_n[0]}", amt=16)
        dbg_n[0] += 1

    em.op("dv", lambda: DV.memset(ones_col.ap(), 1.0), writes=[bones_c])
    em.op("dv", lambda: DV.memset(ones_rb.ap(), 1.0), writes=[bones_rb])
    em.op("dv", lambda: DV.memset(ones_rf.ap(), 1.0), writes=[bones_rf])
    em.op("dv", lambda: DV.memset(eps_r.ap(), EPS), writes=[beps])

    em.dma_multi("s", [lambda: SY.dma_start(out=swap_sb.ap(), in_=swmd)], bswap, "d_sw")
    em.dma_multi("s", [lambda: SY.dma_start(out=b0fm.ap(), in_=b0f)], bb0, "d_b0")
    em.dma_multi("s", [lambda: SY.dma_start(out=b1fm.ap(), in_=b1f)], bb1, "d_b1")

    ct_s = pool.take()
    st_s = pool.take()
    em.dma_multi("s", [lambda: SY.dma_start(out=ct_s["bf"][:, 0:S], in_=ctabd)],
                 ct_s["buf"], "d_ct")
    em.dma_multi("s", [lambda: SY.dma_start(out=st_s["bf"][:, 0:S], in_=stabd)],
                 st_s["buf"], "d_st")

    xq_s = [pool.take() for _ in range((DC + 1) // 2)]
    xo_s = [pool.take() for _ in range((DC + 1) // 2)]

    def xq_ap(dc, lo, hi):
        return xq_s[dc // 2]["bf"][:, (dc % 2) * Q + lo:(dc % 2) * Q + hi]

    def xo_ap(dc, lo, hi):
        return xo_s[dc // 2]["bf"][:, (dc % 2) * Q + lo:(dc % 2) * Q + hi]

    def xs_ap(dc, lo, hi):
        if lo >= Q:
            return xo_ap(dc, lo - Q, hi - Q)
        return xq_ap(dc, lo, hi)

    def xs_buf(dc, lo):
        return (xo_s if lo >= Q else xq_s)[dc // 2]["buf"]

    for i in range((DC + 1) // 2):
        emits_q, emits_o = [], []
        for dc in (2 * i, 2 * i + 1):
            if dc < DC:
                emits_q.append(lambda dc=dc: GP.dma_start(
                    out=xq_ap(dc, 0, Q), in_=xT[dc * 128:(dc + 1) * 128, 0:Q]))
                emits_o.append(lambda dc=dc: GP.dma_start(
                    out=xo_ap(dc, 0, Q), in_=xT[dc * 128:(dc + 1) * 128, Q:S]))
        em.dma_multi("g", emits_q, xq_s[i]["buf"], f"d_xq{i}")
        em.dma_multi("g", emits_o, xo_s[i]["buf"], f"d_xo{i}")

    nvs = max(1, (DC + 3) // 4)
    wv_s = [pool.take() for _ in range(nvs)]
    for i in range(nvs):
        emits = []
        for dc in range(i * 4, min(DC, i * 4 + 4)):
            emits.append(lambda dc=dc, i=i: GP.dma_start(
                out=wv_s[i]["bf"][:, (dc % 4) * GW:(dc % 4 + 1) * GW],
                in_=wv[dc * 128:(dc + 1) * 128, :]))
        em.dma_multi("g", emits, wv_s[i]["buf"], f"d_wv{i}")

    def wv_ap(dc):
        return wv_s[dc // 4]["bf"][:, (dc % 4) * GW:(dc % 4 + 1) * GW]

    wk_s = [pool.take() for _ in range(G)]
    for g in range(G):
        em.dma_multi("g", [lambda g=g: GP.dma_start(
            out=wk_s[g]["bf"][:, 0:DC * 128].rearrange("p (dc c) -> p dc c", c=128),
            in_=wk[g])],
            wk_s[g]["buf"], f"d_wk{g}")

    wq_s = [pool.take(), pool.take()]

    def load_wq(h):
        sl = wq_s[h % 2]
        em.dma_multi("g", [lambda h=h, sl=sl: GP.dma_start(
            out=sl["bf"][:, 0:DC * 128].rearrange("p (dc c) -> p dc c", c=128),
            in_=wq[h])],
            sl["buf"], f"d_wq{h % 2}")

    load_wq(0)
    load_wq(1)

    scr1 = pool.take()
    scr2 = pool.take()
    scr1_h = [Buf("scr1a"), Buf("scr1b")]
    scr2_h = [Buf("scr2a"), Buf("scr2b")]

    vt_s = [pool.take() for _ in range(max(1, (KC + 3) // 4))]

    def v_ap(kc, g):
        base = (kc % 4) * GW
        return vt_s[kc // 4]["bf"][:, base + g * 128: base + (g + 1) * 128]

    for kc in range(KC):
        pb = kc % 2
        grp = em.group("pe", writes=[PS[pb]])
        for dc in range(DC):
            grp.add((lambda kc=kc, dc=dc, pb=pb: T.matmul(
                ps_ap(pb, 0, GW),
                xs_ap(dc, kc * 128, (kc + 1) * 128),
                wv_ap(dc),
                start=(dc == 0), stop=(dc == DC - 1))),
                reads=[xs_buf(dc, kc * 128), wv_s[dc // 4]["buf"]],
                last=(dc == DC - 1))
        sl = vt_s[kc // 4]
        em.op("sc", (lambda kc=kc, pb=pb, sl=sl: SC.activation(
            sl["bf"][:, (kc % 4) * GW:(kc % 4 + 1) * GW],
            ps_ap(pb, 0, GW), AF.Copy)),
            reads=[PS[pb]], writes=[sl["buf"]])

    dump("v0", lambda: vt_s[0]["bf"][:, 0:GW], vt_s[0]["buf"], [128, GW])

    kr_s = [pool.take() for _ in range(G)]

    NSBP = (NSBL + 1) // 2
    for g in range(G):
        for sbp in range(NSBP):
            sbs = [s for s in (2 * sbp, 2 * sbp + 1) if s < NSBL]
            grp = em.group("pe", writes=[PS[2 + (s % 2)] for s in sbs])
            for dc in range(DC):
                for sb in sbs:
                    grp.add((lambda g=g, sb=sb, dc=dc: T.matmul(
                        ps_ap(2 + (sb % 2), 0, SBW),
                        wk_s[g]["bf"][:, dc * 128:(dc + 1) * 128],
                        xs_ap(dc, sb * SBW, (sb + 1) * SBW),
                        start=(dc == 0), stop=(dc == DC - 1))),
                        reads=[wk_s[g]["buf"], xs_buf(dc, sb * SBW)],
                        last=(dc == DC - 1 and sb == sbs[-1]))
            for sb in sbs:
                pa = sb % 2
                pw = 6 + pa
                klo = pa * 1024
                pb = 2 + pa
                em.op("sc", (lambda pb=pb, klo=klo: SC.activation(
                    scr1["bf"][:, klo:klo + SBW], ps_ap(pb, 0, SBW), AF.Copy)),
                    reads=[PS[pb]], writes=[scr1_h[pa]])
                em.op("pe", (lambda pw=pw, klo=klo: T.matmul(
                    ps_ap(pw, 0, SBW), swap_sb.ap(), scr1["bf"][:, klo:klo + SBW],
                    start=True, stop=True)),
                    reads=[bswap, scr1_h[pa]], writes=[PS[pw]])
                em.op("dv", (lambda g=g, sb=sb, klo=klo: DV.tensor_mul(
                    kr_s[g]["bf"][:, sb * SBW:(sb + 1) * SBW],
                    scr1["bf"][:, klo:klo + SBW],
                    ct_s["bf"][:, sb * SBW:(sb + 1) * SBW])),
                    reads=[scr1_h[pa], ct_s["buf"]], writes=[kr_s[g]["buf"]])
                em.op("dv", (lambda sb=sb, pw=pw, klo=klo: DV.tensor_mul(
                    scr2["bf"][:, klo:klo + SBW], ps_ap(pw, 0, SBW),
                    st_s["bf"][:, sb * SBW:(sb + 1) * SBW])),
                    reads=[PS[pw], st_s["buf"]], writes=[scr2_h[pa]])
                em.op("dv", (lambda g=g, sb=sb, klo=klo: DV.tensor_add(
                    kr_s[g]["bf"][:, sb * SBW:(sb + 1) * SBW],
                    kr_s[g]["bf"][:, sb * SBW:(sb + 1) * SBW],
                    scr2["bf"][:, klo:klo + SBW])),
                    reads=[kr_s[g]["buf"], scr2_h[pa]], writes=[kr_s[g]["buf"]])

    dump("kr0", lambda: kr_s[0]["bf"][:, 0:S], kr_s[0]["buf"], [128, S])

    for sl in wv_s + wk_s + xo_s:
        pool.give(sl)

    qr_s = [pool.take(), pool.take()]
    qr_bufs = [Buf(f"qr{i}") for i in range(4)]

    def qr_ap(h, lo, hi):
        return qr_s[(h // 2) % 2]["bf"][:, (h % 2) * 1024 + lo:(h % 2) * 1024 + hi]

    def qr_buf(h):
        return qr_bufs[h % 4]

    pt_s = pool.take()
    pt_h = [Buf("ptA"), Buf("ptB")]
    den_s = [pool.take(), pool.take()]
    den_pl = [pool.take(), pool.take()]
    attn_s = [pool.take() for _ in range((H + 1) // 2)]
    wo_s = [pool.take(), pool.take()]
    xm_s = [pool.take(), pool.take()]
    if KC >= 8:
        POOLSET = frozenset(kc for kc in range(KC) if kc % 8 < 3)
    else:
        POOLSET = frozenset([KC - 1])

    def attn_ap(h, lo, hi):
        return attn_s[h // 2]["bf"][:, (h % 2) * 1024 + lo:(h % 2) * 1024 + hi]

    def load_wo(d):
        sl = wo_s[d % 2]
        em.dma_multi("g", [lambda d=d, sl=sl: GP.dma_start(
            out=sl["bf"][:, 0:DC * 128].rearrange("p (dc c) -> p dc c", c=128),
            in_=wo[d])],
            sl["buf"], f"d_wo{d % 2}")

    def load_xm(d):
        sl = xm_s[d % 2]
        em.dma_multi("s", [lambda d=d, sl=sl: SY.dma_start(
            out=sl["f32"][:, 0:Q], in_=xT[d * 128:(d + 1) * 128, 0:Q])],
            sl["buf"], f"d_xm{d % 2}")

    def qproj_rope(h, lo_bank):
        pt_i = lo_bank // 2
        grp = em.group("pe", writes=[PS[lo_bank], PS[lo_bank + 1]])
        for dc in range(DC):
            for qb in range(NQB):
                grp.add((lambda h=h, qb=qb, dc=dc, pt_i=pt_i: T.matmul(
                    PT[pt_i][:, qb * 512:qb * 512 + QBW],
                    wq_s[h % 2]["bf"][:, dc * 128:(dc + 1) * 128],
                    xq_ap(dc, qb * QBW, (qb + 1) * QBW),
                    start=(dc == 0), stop=(dc == DC - 1))),
                    reads=[wq_s[h % 2]["buf"], xq_s[dc // 2]["buf"]],
                    last=(dc == DC - 1 and qb == NQB - 1))
        for qb in range(NQB):
            pa = qb % 2
            qlo = pa * 1024
            em.op("sc", (lambda qb=qb, qlo=qlo, pt_i=pt_i: SC.activation(
                scr1["bf"][:, qlo:qlo + QBW],
                PT[pt_i][:, qb * 512:qb * 512 + QBW], AF.Copy)),
                reads=[PS[lo_bank + qb]], writes=[scr1_h[pa]])
            em.op("pe", (lambda qb=qb, qlo=qlo, pt_i=pt_i: T.matmul(
                PT[pt_i][:, qb * 512:qb * 512 + QBW], swap_sb.ap(),
                scr1["bf"][:, qlo:qlo + QBW],
                start=True, stop=True)),
                reads=[bswap, scr1_h[pa]], writes=[PS[lo_bank + qb]])
            em.op("dv", (lambda h=h, qb=qb, qlo=qlo: DV.tensor_mul(
                qr_ap(h, qb * QBW, (qb + 1) * QBW),
                scr1["bf"][:, qlo:qlo + QBW],
                ct_s["bf"][:, qb * QBW:(qb + 1) * QBW])),
                reads=[scr1_h[pa], ct_s["buf"]], writes=[qr_buf(h)])
            em.op("dv", (lambda qb=qb, qlo=qlo, pt_i=pt_i: DV.tensor_mul(
                scr2["bf"][:, qlo:qlo + QBW],
                PT[pt_i][:, qb * 512:qb * 512 + QBW],
                st_s["bf"][:, qb * QBW:(qb + 1) * QBW])),
                reads=[PS[lo_bank + qb], st_s["buf"]], writes=[scr2_h[pa]])
            em.op("dv", (lambda h=h, qb=qb, qlo=qlo: DV.tensor_add(
                qr_ap(h, qb * QBW, (qb + 1) * QBW),
                qr_ap(h, qb * QBW, (qb + 1) * QBW),
                scr2["bf"][:, qlo:qlo + QBW])),
                reads=[qr_buf(h), scr2_h[pa]],
                writes=[qr_buf(h)])

    def st_emit(h, kc):
        g = h % G
        sb = kc % 2
        grp = em.group("pe", writes=[PS[2 * sb], PS[2 * sb + 1]])
        for qb in range(NQB):
            grp.add((lambda h=h, g=g, kc=kc, qb=qb, sb=sb: T.matmul(
                PT[sb][:, qb * QBW:(qb + 1) * QBW],
                kr_s[g]["bf"][:, kc * 128:(kc + 1) * 128],
                qr_ap(h, qb * QBW, (qb + 1) * QBW),
                start=True, stop=True)),
                reads=[kr_s[g]["buf"], qr_buf(h)],
                last=(qb == NQB - 1))

    def tail_stage(h, stage):
        par = h % 2
        av_lo = 4 + 2 * par
        dlo = par * 1024
        if stage == 0:
            em.op("dv", (lambda h=h, par=par: DV.tensor_copy(
                attn_ap(h, 0, Q), PT[2 + par][:, 0:Q])),
                reads=[PS[av_lo], PS[av_lo + 1]], writes=[attn_s[h // 2]["buf"]])
            em.op("dv", (lambda dlo=dlo, par=par: DV.tensor_add(
                scr1["bf"][:, dlo:dlo + Q], den_s[par]["f32"][:, 0:Q],
                den_pl[par]["f32"][:, 0:Q])),
                reads=[den_s[par]["buf"], den_pl[par]["buf"]],
                writes=[scr1_h[par]])
        elif stage == 1:
            dgrp = em.group("pe", writes=[PS[av_lo], PS[av_lo + 1]])
            for qb in range(NQB):
                dgrp.add((lambda qb=qb, dlo=dlo, par=par: T.matmul(
                    PT[2 + par][0:1, qb * 512:qb * 512 + QBW], ones_col.ap(),
                    scr1["bf"][:, dlo + qb * QBW:dlo + (qb + 1) * QBW],
                    start=True, stop=True)),
                    reads=[bones_c, scr1_h[par]], last=(qb == NQB - 1))
        elif stage == 2:
            em.op("sc", (lambda par=par: SC.activation(
                rowB["f32"][0:1, 0:Q], PT[2 + par][0:1, 0:Q], AF.Ln)),
                reads=[PS[av_lo], PS[av_lo + 1]], writes=[bB])
        elif stage == 3:
            em.op("sc", (lambda: SC.activation(
                rowA["bf"][0:1, 0:Q], rowB["f32"][0:1, 0:Q], AF.Exp, scale=-1.0)),
                reads=[bB], writes=[bA])
        elif stage == 4:
            bgrp = em.group("pe", writes=[PS[av_lo], PS[av_lo + 1]])
            for qb in range(NQB):
                bgrp.add((lambda qb=qb, par=par: T.matmul(
                    PT[2 + par][:, qb * QBW:(qb + 1) * QBW], ones_rb.ap(),
                    rowA["bf"][0:1, qb * QBW:(qb + 1) * QBW],
                    start=True, stop=True)),
                    reads=[bones_rb, bA], last=(qb == NQB - 1))
        else:
            em.op("dv", (lambda h=h, par=par: DV.tensor_mul(
                attn_ap(h, 0, Q), attn_ap(h, 0, Q), PT[2 + par][:, 0:Q])),
                reads=[attn_s[h // 2]["buf"], PS[av_lo], PS[av_lo + 1]],
                writes=[attn_s[h // 2]["buf"]])

    qproj_rope(0, 4)
    if H > 1:
        qproj_rope(1, 6)
    if H > 2:
        load_wq(2)
    if H > 3:
        load_wq(3)

    TAIL_AT = (1, 2, 3, 4, 5, 6)
    QPROJ_AT = tuple(range(7, 13))
    for h in range(H):
        g = h % G
        par = h % 2
        av_lo = 4 + 2 * par
        other_lo = 4 + 2 * (1 - par)
        st_emit(h, 0)
        avgrp = None
        dv_first = True
        em.op("g", (lambda par=par: GP.memset(den_pl[par]["f32"][:, 0:Q], 0.0)),
              writes=[den_pl[par]["buf"]])
        qgrp = [None]
        qh = h + 2
        in_loop_tails = sum(1 for x in TAIL_AT if x < KC)
        for kc in range(KC):
            if kc + 1 < KC:
                st_emit(h, kc + 1)
            sb = kc % 2
            ptp = kc % 2
            ptlo = ptp * 1024
            em.op("sc", (lambda ptlo=ptlo, sb=sb: SC.activation(
                pt_s["bf"][:, ptlo:ptlo + Q], PT[sb][:, 0:Q], AF.Exp, scale=SCALE)),
                reads=[PS[2 * sb], PS[2 * sb + 1]], writes=[pt_h[ptp]])
            if kc in POOLSET:
                acc, bacc = den_pl[par], den_pl[par]["buf"]
                em.op("g", (lambda acc=acc, ptlo=ptlo: GP.tensor_add(
                    acc["f32"][:, 0:Q], acc["f32"][:, 0:Q],
                    pt_s["bf"][:, ptlo:ptlo + Q])),
                    reads=[pt_h[ptp], bacc], writes=[bacc])
            else:
                acc, bacc = den_s[par], den_s[par]["buf"]
                if dv_first:
                    em.op("dv", (lambda acc=acc, ptlo=ptlo: DV.tensor_copy(
                        acc["f32"][:, 0:Q], pt_s["bf"][:, ptlo:ptlo + Q])),
                        reads=[pt_h[ptp]], writes=[bacc])
                    dv_first = False
                else:
                    em.op("dv", (lambda acc=acc, ptlo=ptlo: DV.tensor_add(
                        acc["f32"][:, 0:Q], acc["f32"][:, 0:Q],
                        pt_s["bf"][:, ptlo:ptlo + Q])),
                        reads=[pt_h[ptp], bacc], writes=[bacc])
            if kc == 0:
                avgrp = em.group("pe", writes=[PS[av_lo], PS[av_lo + 1]])
            if qh < H and KC > QPROJ_AT[0]:
                if kc == QPROJ_AT[0]:
                    qgrp[0] = em.group("pe", writes=[PS[other_lo], PS[other_lo + 1]])
                if qgrp[0] is not None and kc >= QPROJ_AT[0]:
                    n_mem = DC * NQB
                    n_slots_q = KC - QPROJ_AT[0]
                    per = (n_mem + n_slots_q - 1) // n_slots_q
                    i0 = (kc - QPROJ_AT[0]) * per
                    for i in range(i0, min(i0 + per, n_mem)):
                        dc, qb = divmod(i, NQB)
                        qgrp[0].add((lambda qh=qh, qb=qb, dc=dc, other_lo=other_lo: T.matmul(
                            PT[other_lo // 2][:, qb * 512:qb * 512 + QBW],
                            wq_s[qh % 2]["bf"][:, dc * 128:(dc + 1) * 128],
                            xq_ap(dc, qb * QBW, (qb + 1) * QBW),
                            start=(dc == 0), stop=(dc == DC - 1))),
                            reads=[wq_s[qh % 2]["buf"], xq_s[dc // 2]["buf"]],
                            last=(i == n_mem - 1))
            for qb in range(NQB):
                avgrp.add((lambda kc=kc, g=g, qb=qb, par=par, ptlo=ptlo: T.matmul(
                    PT[2 + par][:, qb * QBW:(qb + 1) * QBW],
                    v_ap(kc, g),
                    pt_s["bf"][:, ptlo + qb * QBW:ptlo + (qb + 1) * QBW],
                    start=(kc == 0), stop=(kc == KC - 1))),
                    reads=[vt_s[kc // 4]["buf"], pt_h[ptp]],
                    last=(kc == KC - 1 and qb == NQB - 1))
            if h > 0 and kc in TAIL_AT:
                tail_stage(h - 1, TAIL_AT.index(kc))
        if h > 0:
            for stage in range(in_loop_tails, 6):
                tail_stage(h - 1, stage)
        if qh < H:
            if KC <= QPROJ_AT[0]:
                qproj_rope(qh, other_lo)
            else:
                for qb in range(NQB):
                    pa = qb % 2
                    qlo = pa * 1024
                    pt_i = other_lo // 2
                    em.op("sc", (lambda qb=qb, qlo=qlo, pt_i=pt_i: SC.activation(
                        scr2["bf"][:, qlo:qlo + QBW],
                        PT[pt_i][:, qb * 512:qb * 512 + QBW], AF.Copy)),
                        reads=[PS[other_lo + qb]], writes=[scr2_h[pa]])
                    em.op("pe", (lambda qb=qb, qlo=qlo, pt_i=pt_i: T.matmul(
                        PT[pt_i][:, qb * 512:qb * 512 + QBW], swap_sb.ap(),
                        scr2["bf"][:, qlo:qlo + QBW],
                        start=True, stop=True)),
                        reads=[bswap, scr2_h[pa]], writes=[PS[other_lo + qb]])
                    em.op("dv", (lambda qh=qh, qb=qb, qlo=qlo: DV.tensor_mul(
                        qr_ap(qh, qb * QBW, (qb + 1) * QBW),
                        scr2["bf"][:, qlo:qlo + QBW],
                        ct_s["bf"][:, qb * QBW:(qb + 1) * QBW])),
                        reads=[scr2_h[pa], ct_s["buf"]],
                        writes=[qr_buf(qh)])
                    em.op("dv", (lambda qb=qb, qlo=qlo, pt_i=pt_i: DV.tensor_mul(
                        scr2["bf"][:, qlo:qlo + QBW],
                        PT[pt_i][:, qb * 512:qb * 512 + QBW],
                        st_s["bf"][:, qb * QBW:(qb + 1) * QBW])),
                        reads=[PS[other_lo + qb], st_s["buf"]], writes=[scr2_h[pa]])
                    em.op("dv", (lambda qh=qh, qb=qb, qlo=qlo: DV.tensor_add(
                        qr_ap(qh, qb * QBW, (qb + 1) * QBW),
                        qr_ap(qh, qb * QBW, (qb + 1) * QBW),
                        scr2["bf"][:, qlo:qlo + QBW])),
                        reads=[qr_buf(qh), scr2_h[pa]],
                        writes=[qr_buf(qh)])
            if qh + 2 < H:
                load_wq(qh + 2)
        if h == 0:
            load_wo(0)
            load_xm(0)
        if h == 1:
            load_wo(1)
            load_xm(1)
    for stage in range(6):
        tail_stage(H - 1, stage)

    dump("attn0", lambda: attn_ap(0, 0, Q), attn_s[0]["buf"], [128, Q])

    for b in pt_h:
        if b.last_w is not None:
            pt_s["buf"].readers.append(b.last_w)
        pt_s["buf"].readers.extend(b.readers)
    for i, b in enumerate(qr_bufs):
        sl = qr_s[i // 2]
        if b.last_w is not None:
            sl["buf"].readers.append(b.last_w)
        sl["buf"].readers.extend(b.readers)

    for sl in kr_s + vt_s + qr_s + [pt_s] + den_s + den_pl + xq_s + [ct_s, st_s] + wq_s:
        pool.give(sl)

    racc = [pool.take() for _ in range(DC)]
    racc_h = []
    for dc in range(DC):
        pair = [Buf(f"racc{dc}L"), Buf(f"racc{dc}R")]
        for b in pair:
            b.last_w = racc[dc]["buf"].last_w
            b.readers = list(racc[dc]["buf"].readers)
        racc_h.append(pair)
    w0_s = [pool.take() for _ in range((DC + 1) // 2)]
    ab_s = [pool.take(), pool.take()]
    bab = [Buf("ab0"), Buf("ab1")]
    x1b_s = []
    w1_s = []
    dg_s = []
    x1b_bufs = [Buf(f"x1b{dc}") for dc in range(DC)]

    def x1b_ap(dc, lo, hi):
        return x1b_s[dc // 2]["bf"][:, (dc % 2) * 1024 + lo:(dc % 2) * 1024 + hi]

    def w0_ap(dc, lo, hi):
        return w0_s[dc // 2]["bf"][:, (dc % 2) * NSB + lo:(dc % 2) * NSB + hi]

    def dg_ap(ncl, lo, hi):
        return dg_s[ncl // 2]["bf"][:, (ncl % 2) * 1024 + lo:(ncl % 2) * 1024 + hi]

    def load_w0(ns):
        for i in range((DC + 1) // 2):
            emits = []
            for dc in (2 * i, 2 * i + 1):
                if dc < DC:
                    emits.append(lambda ns=ns, dc=dc: GP.dma_start(
                        out=w0_ap(dc, 0, NSB),
                        in_=w0[dc * 128:(dc + 1) * 128, ns * NSB:(ns + 1) * NSB]))
            em.dma_multi("g", emits, w0_s[i]["buf"], f"d_w0_{i}")

    def load_w1(ns):
        for ncl in range(NCK):
            row = ns * NSB + ncl * 128
            em.dma_multi("g", [lambda row=row, ncl=ncl: GP.dma_start(
                out=w1_s[ncl]["bf"][:, 0:D], in_=w1[row:row + 128, :])],
                w1_s[ncl]["buf"], f"d_w1_{ncl}")

    load_w0(0)

    stat_groups = []
    for qb in range(NQB):
        stat_groups.append(em.group("pe", writes=[PS[4 + qb]]))
    for qb in range(NQB):
        stat_groups.append(em.group("pe", writes=[PS[6 + qb]]))

    def stats_add(stat_groups, dc, src_scr, src_bufs, which, last):
        for qb in range(NQB):
            stat_groups[which * NQB + qb].add(
                (lambda dc=dc, qb=qb, which=which, src_scr=src_scr: T.matmul(
                    PT[2 + which][0:1, qb * 512:qb * 512 + QBW], ones_col.ap(),
                    src_scr["bf"][:, (dc % 2) * 1024 + qb * QBW:
                                  (dc % 2) * 1024 + (qb + 1) * QBW],
                    start=(dc == 0), stop=last)),
                reads=[src_bufs[dc % 2], bones_c], last=last)

    for d in range(DC):
        par = d % 2
        st_halves = [PS[2 * par], PS[2 * par + 1]]
        grp = em.group("pe", writes=st_halves)
        for dvc in range(DC):
            for qb in range(NQB):
                grp.add((lambda d=d, qb=qb, dvc=dvc, par=par: T.matmul(
                    PT[par][:, qb * QBW:(qb + 1) * QBW],
                    wo_s[d % 2]["bf"][:, dvc * 128:(dvc + 1) * 128],
                    attn_ap(dvc, qb * QBW, (qb + 1) * QBW),
                    start=(dvc == 0), stop=(dvc == DC - 1))),
                    reads=[wo_s[d % 2]["buf"], attn_s[dvc // 2]["buf"]],
                    last=(qb == NQB - 1 and dvc == DC - 1))
        em.op("dv", (lambda d=d, par=par: DV.tensor_add(
            racc[d]["f32"][:, 0:Q], PT[par][:, 0:Q], xm_s[d % 2]["f32"][:, 0:Q])),
            reads=st_halves + [xm_s[d % 2]["buf"]], writes=racc_h[d])
        if d == 0:
            dump("r1_0", lambda: racc[0]["f32"][:, 0:Q], racc_h[0],
                 [128, Q], F32)
        if d + 2 < DC:
            load_wo(d + 2)
            load_xm(d + 2)
        em.op("sc", (lambda d=d, par=par: SC.activation(
            scr1["bf"][:, par * 1024:par * 1024 + Q], racc[d]["f32"][:, 0:Q], AF.Copy)),
            reads=racc_h[d], writes=[scr1_h[par]])
        em.op("sc", (lambda par=par: SC.activation(
            scr2["bf"][:, par * 1024:par * 1024 + Q],
            scr1["bf"][:, par * 1024:par * 1024 + Q], AF.Square)),
            reads=[scr1_h[par]], writes=[scr2_h[par]])
        if d > 0:
            stats_add(stat_groups, d - 1, scr1, scr1_h, 0, last=False)
            stats_add(stat_groups, d - 1, scr2, scr2_h, 1, last=False)
    stats_add(stat_groups, DC - 1, scr1, scr1_h, 0, last=True)
    stats_add(stat_groups, DC - 1, scr2, scr2_h, 1, last=True)

    pool.give(wo_s[0])
    pool.give(wo_s[1])
    pool.give(xm_s[0])
    pool.give(xm_s[1])
    for sl in attn_s:
        pool.give(sl)
    x1b_s.extend(pool.take() for _ in range((DC + 1) // 2))
    w1_s.extend(pool.take() for _ in range(NCK))
    dg_s.extend(pool.take() for _ in range((NCK + 1) // 2))

    def ln_finalize():
        em.op("sc", (lambda: SC.activation(rowA["f32"][0:1, 0:Q], PT[2][0:1, 0:Q],
                                           AF.Copy, scale=1.0 / D)),
              reads=[PS[4], PS[5]], writes=[bA])
        em.op("sc", (lambda: SC.activation(rowB["f32"][0:1, 0:Q], PT[3][0:1, 0:Q],
                                           AF.Copy, scale=1.0 / D)),
              reads=[PS[6], PS[7]], writes=[bB])
        em.op("sc", (lambda: SC.activation(PT[0][0:1, 0:Q], rowA["f32"][0:1, 0:Q],
                                           AF.Square)),
              reads=[bA], writes=[PS[0], PS[1]])
        em.op("dv", (lambda: DV.tensor_sub(PT[1][0:1, 0:Q], rowB["f32"][0:1, 0:Q],
                                           PT[0][0:1, 0:Q])),
              reads=[bB, PS[0], PS[1]], writes=[PS[2], PS[3]])
        em.op("sc", (lambda: SC.activation(PT[0][0:1, 0:Q], PT[1][0:1, 0:Q],
                                           AF.Ln, bias=eps_r[0:1, 0:1])),
              reads=[PS[2], PS[3], beps], writes=[PS[0], PS[1]])
        em.op("sc", (lambda: SC.activation(rowB["f32"][0:1, 0:Q], PT[0][0:1, 0:Q],
                                           AF.Exp, scale=-0.5)),
              reads=[PS[0], PS[1]], writes=[bB])
        em.op("dv", (lambda: DV.scalar_tensor_tensor(
            rowA["f32"][0:1, 0:Q], rowA["f32"][0:1, 0:Q], -1.0,
            rowB["f32"][0:1, 0:Q], ALU.mult, ALU.mult)),
            reads=[bA, bB], writes=[bA])
        agrp = em.group("pe", writes=[PS[0], PS[1]])
        for qb in range(NQB):
            agrp.add((lambda qb=qb: T.matmul(
                PT[0][:, qb * QBW:(qb + 1) * QBW], ones_rf.ap(),
                rowB["f32"][0:1, qb * QBW:(qb + 1) * QBW], start=True, stop=True)),
                reads=[bones_rf, bB], last=(qb == NQB - 1))
        bgrp = em.group("pe", writes=[PS[2], PS[3]])
        for qb in range(NQB):
            bgrp.add((lambda qb=qb: T.matmul(
                PT[1][:, qb * QBW:(qb + 1) * QBW], ones_rf.ap(),
                rowA["f32"][0:1, qb * QBW:(qb + 1) * QBW], start=True, stop=True)),
                reads=[bones_rf, bA], last=(qb == NQB - 1))
        em.op("sc", (lambda: SC.activation(ab_s[0]["bf"][:, 0:Q], PT[0][:, 0:Q],
                                           AF.Copy)),
              reads=[PS[0], PS[1]], writes=[bab[0]])
        em.op("sc", (lambda: SC.activation(ab_s[1]["bf"][:, 0:Q], PT[1][:, 0:Q],
                                           AF.Copy)),
              reads=[PS[2], PS[3]], writes=[bab[1]])

    ln_finalize()

    H1 = Q // 2
    for dc in range(DC):
        for eng, E, lo, hi in (("dv", DV, 0, H1), ("g", GP, H1, Q)):
            em.op(eng, (lambda E=E, dc=dc, lo=lo, hi=hi: E.tensor_mul(
                racc[dc]["f32"][:, lo:hi], racc[dc]["f32"][:, lo:hi],
                ab_s[0]["bf"][:, lo:hi])),
                reads=[racc_h[dc][0 if lo == 0 else 1], bab[0]],
                writes=[racc_h[dc][0 if lo == 0 else 1]])
            em.op(eng, (lambda E=E, dc=dc, lo=lo, hi=hi: E.tensor_add(
                racc[dc]["f32"][:, lo:hi], racc[dc]["f32"][:, lo:hi],
                ab_s[1]["bf"][:, lo:hi])),
                reads=[racc_h[dc][0 if lo == 0 else 1], bab[1]],
                writes=[racc_h[dc][0 if lo == 0 else 1]])
        em.op("sc", (lambda dc=dc: SC.activation(
            x1b_ap(dc, 0, Q), racc[dc]["f32"][:, 0:Q], AF.Copy)),
            reads=racc_h[dc], writes=[x1b_bufs[dc]])

    dump("x1_0", lambda: racc[0]["f32"][:, 0:Q], racc_h[0], [128, Q], F32)

    load_w1(0)

    for ns in range(NS):
        for ncl in range(NCK):
            par = ncl % 2
            st_halves = [PS[2 * par], PS[2 * par + 1]]
            grp = em.group("pe", writes=st_halves)
            for dc in range(DC):
                for qb in range(NQB):
                    grp.add((lambda ncl=ncl, qb=qb, dc=dc, par=par: T.matmul(
                        PT[par][:, qb * QBW:(qb + 1) * QBW],
                        w0_ap(dc, ncl * 128, (ncl + 1) * 128),
                        x1b_ap(dc, qb * QBW, (qb + 1) * QBW),
                        start=(dc == 0), stop=(dc == DC - 1))),
                        reads=[w0_s[dc // 2]["buf"], x1b_bufs[dc]],
                        last=(qb == NQB - 1 and dc == DC - 1))
            col = ns * NCK + ncl
            em.op("sc", (lambda ncl=ncl, par=par, col=col: SC.activation(
                dg_ap(ncl, 0, Q), PT[par][:, 0:Q], AF.Gelu,
                bias=b0fm[:, col:col + 1])),
                reads=st_halves + [bb0], writes=[dg_s[ncl // 2]["buf"]])
            if ns == 0 and ncl == 0:
                dump("dg0", lambda: dg_ap(0, 0, Q), dg_s[0]["buf"], [128, Q])
        if ns + 1 < NS:
            load_w0(ns + 1)
        for dc2 in range(DC):
            par = dc2 % 2
            av_halves = [PS[4 + 2 * par], PS[5 + 2 * par]]
            grp = em.group("pe", writes=av_halves)
            for ncl in range(NCK):
                for qb in range(NQB):
                    grp.add((lambda dc2=dc2, qb=qb, ncl=ncl, par=par: T.matmul(
                        PT[2 + par][:, qb * QBW:(qb + 1) * QBW],
                        w1_s[ncl]["bf"][:, dc2 * 128:(dc2 + 1) * 128],
                        dg_ap(ncl, qb * QBW, (qb + 1) * QBW),
                        start=(ncl == 0), stop=(ncl == NCK - 1))),
                        reads=[w1_s[ncl]["buf"], dg_s[ncl // 2]["buf"]],
                        last=(qb == NQB - 1 and ncl == NCK - 1))
            if ns < NS - 1:
                em.op("dv", (lambda dc2=dc2, par=par: DV.tensor_add(
                    racc[dc2]["f32"][:, 0:Q], racc[dc2]["f32"][:, 0:Q],
                    PT[2 + par][:, 0:Q])),
                    reads=av_halves + racc_h[dc2], writes=racc_h[dc2])
            else:
                em.op("dv", (lambda dc2=dc2, par=par: DV.scalar_tensor_tensor(
                    racc[dc2]["f32"][:, 0:Q], PT[2 + par][:, 0:Q],
                    b1fm[:, dc2:dc2 + 1], racc[dc2]["f32"][:, 0:Q],
                    ALU.add, ALU.add)),
                    reads=av_halves + racc_h[dc2] + [bb1],
                    writes=racc_h[dc2])
        if ns + 1 < NS:
            load_w1(ns + 1)

    stat_groups = []
    for qb in range(NQB):
        stat_groups.append(em.group("pe", writes=[PS[4 + qb]]))
    for qb in range(NQB):
        stat_groups.append(em.group("pe", writes=[PS[6 + qb]]))
    for dc in range(DC):
        par = dc % 2
        em.op("sc", (lambda dc=dc, par=par: SC.activation(
            scr1["bf"][:, par * 1024:par * 1024 + Q], racc[dc]["f32"][:, 0:Q],
            AF.Copy)),
            reads=racc_h[dc], writes=[scr1_h[par]])
        em.op("sc", (lambda par=par: SC.activation(
            scr2["bf"][:, par * 1024:par * 1024 + Q],
            scr1["bf"][:, par * 1024:par * 1024 + Q], AF.Square)),
            reads=[scr1_h[par]], writes=[scr2_h[par]])
        if dc > 0:
            stats_add(stat_groups, dc - 1, scr1, scr1_h, 0, last=False)
            stats_add(stat_groups, dc - 1, scr2, scr2_h, 1, last=False)
    stats_add(stat_groups, DC - 1, scr1, scr1_h, 0, last=True)
    stats_add(stat_groups, DC - 1, scr2, scr2_h, 1, last=True)

    ln_finalize()

    H1 = Q // 2
    for dc in range(DC):
        for eng, E, lo, hi in (("dv", DV, 0, H1), ("g", GP, H1, Q)):
            em.op(eng, (lambda E=E, dc=dc, lo=lo, hi=hi: E.tensor_mul(
                racc[dc]["f32"][:, lo:hi], racc[dc]["f32"][:, lo:hi],
                ab_s[0]["bf"][:, lo:hi])),
                reads=[racc_h[dc][0 if lo == 0 else 1], bab[0]],
                writes=[racc_h[dc][0 if lo == 0 else 1]])
            em.op(eng, (lambda E=E, dc=dc, lo=lo, hi=hi: E.tensor_add(
                racc[dc]["f32"][:, lo:hi], racc[dc]["f32"][:, lo:hi],
                ab_s[1]["bf"][:, lo:hi])),
                reads=[racc_h[dc][0 if lo == 0 else 1], bab[1]],
                writes=[racc_h[dc][0 if lo == 0 else 1]])
        em.op("s", (lambda dc=dc: SY.dma_start(
            out=outT[dc * 128:(dc + 1) * 128, :], in_=racc[dc]["f32"][:, 0:Q])),
            reads=racc_h[dc], sem="d_out", amt=16)

    em.raw_wait("s", "d_out", 16 * DC)
    for i in range(dbg_n[0]):
        em.raw_wait("s", f"d_dbg{i}", 16)

    em.replay()
    return nc



def rope_tables(positions):
    half = 64
    freqs = (1.0 / 10000.0 ** (np.arange(half, dtype=np.float64) / half))
    ang = np.outer(freqs, positions.astype(np.float64))
    c = np.cos(ang)
    s = np.sin(ang)
    C = np.ascontiguousarray(np.vstack([c, c]).astype(ml_dtypes.bfloat16))
    Ssn = np.ascontiguousarray(np.vstack([-s, s]).astype(ml_dtypes.bfloat16))
    return C, Ssn


def perm_idx_heads(nheads):
    hd = 128
    order = np.concatenate([np.arange(0, hd, 2), np.arange(1, hd, 2)])
    return np.concatenate([h * hd + order for h in range(nheads)])


def prep_shared(cfg, W_q, W_k, W_v, W_o, W0, b0, W1, b1):
    D, H, G, N, DC = cfg["D"], cfg["H"], cfg["G"], cfg["N"], cfg["DC"]
    sh = {}
    Wqp = W_q[:, perm_idx_heads(H)]
    Wkp = W_k[:, perm_idx_heads(G)]
    sh["wq"] = np.ascontiguousarray(
        Wqp.reshape(DC, 128, H, 128).transpose(2, 1, 0, 3)).astype(np.float32)
    sh["wk"] = np.ascontiguousarray(
        Wkp.reshape(DC, 128, G, 128).transpose(2, 1, 0, 3)).astype(np.float32)
    sh["wv"] = np.ascontiguousarray(W_v).astype(np.float32)
    sh["wo"] = np.ascontiguousarray(
        W_o.reshape(DC, 128, DC, 128).transpose(2, 1, 0, 3)).astype(np.float32)
    sh["w0"] = np.ascontiguousarray(W0.T).astype(np.float32)
    sh["w1"] = np.ascontiguousarray(W1.T).astype(np.float32)
    sh["b0f"] = np.ascontiguousarray(b0.reshape(N // 128, 128).T).astype(np.float32)
    sh["b1f"] = np.ascontiguousarray(b1.reshape(DC, 128).T).astype(np.float32)
    sh["swm"] = np.roll(np.eye(128, dtype=np.float32), 64, axis=0).astype(
        ml_dtypes.bfloat16)
    return sh


def prep_core(cfg, shared, xb, half):
    Q = cfg["Q"]
    my = np.arange(half * Q, (half + 1) * Q)
    other = np.arange((1 - half) * Q, (2 - half) * Q)
    cols = np.concatenate([my, other])
    d = dict(shared)
    d["xT"] = np.ascontiguousarray(xb[cols, :].T).astype(np.float32)
    C, Ssn = rope_tables(cols)
    d["ctab"] = C
    d["stab"] = Ssn
    return d


_PROG_CACHE = {}


def _get_prog(key, cfg):
    if key not in _PROG_CACHE:
        _PROG_CACHE[key] = build_program(cfg)
    return _PROG_CACHE[key]


def kernel(L0_attn_x, L0_attn_W_q, L0_attn_W_k, L0_attn_W_v, L0_attn_W_o,
           L0_ffn_W0, L0_ffn_bias0, L0_ffn_W1, L0_ffn_bias1):
    cfg = CFG_FULL
    x = np.asarray(L0_attn_x, dtype=np.float32)
    W_q = np.asarray(L0_attn_W_q, dtype=np.float32)
    W_k = np.asarray(L0_attn_W_k, dtype=np.float32)
    W_v = np.asarray(L0_attn_W_v, dtype=np.float32)
    W_o = np.asarray(L0_attn_W_o, dtype=np.float32)
    W0 = np.asarray(L0_ffn_W0, dtype=np.float32)
    b0 = np.asarray(L0_ffn_bias0, dtype=np.float32)
    W1 = np.asarray(L0_ffn_W1, dtype=np.float32)
    b1 = np.asarray(L0_ffn_bias1, dtype=np.float32)

    shared = prep_shared(cfg, W_q, W_k, W_v, W_o, W0, b0, W1, b1)
    in_maps = [prep_core(cfg, shared, x[c // 2], c % 2) for c in range(8)]
    nc = _get_prog("full", cfg)
    res = run_bass_kernel_spmd(nc, in_maps, core_ids=list(range(8)))
    Q = cfg["Q"]
    out = np.empty((4, 2048, 2048), dtype=np.float32)
    for c in range(8):
        b, half = c // 2, c % 2
        out[b, half * Q:(half + 1) * Q, :] = np.asarray(res.results[c]["outT"]).T
    return out


# version 15
# speedup vs baseline: 1.0288x; 1.0134x over previous
import os
import sys
import math
import contextlib

import numpy as np
import ml_dtypes

if os.path.isdir("/opt/trn_rl_repo") and "/opt/trn_rl_repo" not in sys.path:
    sys.path.insert(0, "/opt/trn_rl_repo")

import concourse.bass as bass
import concourse.mybir as mybir
from concourse.bass_utils import run_bass_kernel_spmd

F32 = mybir.dt.float32
BF16 = mybir.dt.bfloat16
AF = mybir.ActivationFunctionType
ALU = mybir.AluOpType

SCALE = 1.0 / math.sqrt(128.0)
EPS = 1e-5


def make_cfg(D, H, G, S, N, NSB, n_slots):
    cfg = dict(D=D, H=H, G=G, S=S, N=N, NSB=NSB, n_slots=n_slots)
    cfg["Q"] = S // 2
    cfg["DC"] = D // 128
    cfg["KC"] = S // 128
    cfg["QBW"] = min(512, cfg["Q"])
    cfg["NQB"] = cfg["Q"] // cfg["QBW"]
    cfg["SBW"] = min(512, cfg["Q"])
    cfg["NSBL"] = S // cfg["SBW"]
    cfg["NS"] = N // NSB
    cfg["NCK"] = NSB // 128
    return cfg


CFG_FULL = make_cfg(D=2048, H=16, G=4, S=2048, N=8192, NSB=1024, n_slots=50)
CFG_MINI = make_cfg(D=512, H=4, G=2, S=512, N=1024, NSB=512, n_slots=30)
CFG_MINI2 = make_cfg(D=512, H=4, G=2, S=2048, N=1024, NSB=512, n_slots=34)



class Buf:
    __slots__ = ("name", "last_w", "readers")

    def __init__(self, name):
        self.name = name
        self.last_w = None
        self.readers = []


class Emitter:
    ENGINES = ("pe", "sc", "dv", "g", "s")

    def __init__(self, nc):
        self.nc = nc
        self.streams = {e: [] for e in self.ENGINES}
        self.counts = {}
        self.sem_names = []
        self.hiwater = {}

    def _sem(self, name):
        if name not in self.counts:
            self.counts[name] = 0
            self.sem_names.append(name)
        return name

    def raw_wait(self, eng, sem, val):
        if val <= 0:
            return
        assert sem in self.counts and val <= self.counts[sem], (
            f"wait on future: {sem} {val} > {self.counts.get(sem)}")
        if self.hiwater.get((eng, sem), 0) >= val:
            return
        self.hiwater[(eng, sem)] = val
        self.streams[eng].append(("wait", sem, val))

    def _collect(self, need, bufs, war):
        for b in bufs:
            if b.last_w is not None:
                s, v = b.last_w
                if need.get(s, 0) < v:
                    need[s] = v
            if war:
                for s, v in b.readers:
                    if need.get(s, 0) < v:
                        need[s] = v

    def _emit_waits(self, eng, need):
        for s, v in need.items():
            self.raw_wait(eng, s, v)

    def op(self, eng, emit, reads=(), writes=(), sem=None, amt=1):
        need = {}
        self._collect(need, reads, war=False)
        self._collect(need, writes, war=True)
        self._emit_waits(eng, need)
        if sem is None:
            sem = "e_" + eng
        self._sem(sem)
        self.counts[sem] += amt
        tok = (sem, self.counts[sem])
        self.streams[eng].append(("op", emit, sem, amt))
        for b in reads:
            b.readers.append(tok)
        for b in writes:
            b.last_w = tok
            b.readers = []
        return tok

    def dma_multi(self, eng, emits, buf, sem):
        need = {}
        self._collect(need, [buf], war=True)
        self._emit_waits(eng, need)
        self._sem(sem)
        for e in emits:
            self.counts[sem] += 16
            self.streams[eng].append(("op", e, sem, 16))
        buf.last_w = (sem, self.counts[sem])
        buf.readers = []

    def group(self, eng, writes, sem=None, amt=1):
        return _Group(self, eng, list(writes), sem, amt)

    def replay(self):
        nc = self.nc
        eng_obj = {"pe": nc.tensor, "sc": nc.scalar, "dv": nc.vector,
                   "g": nc.gpsimd, "s": nc.sync}
        blk_name = {"pe": "tensor", "sc": "scalar", "dv": "vector",
                    "g": "gpsimd", "s": "sync"}
        with contextlib.ExitStack() as stack:
            handles = {}
            for name in self.sem_names:
                handles[name] = stack.enter_context(nc.semaphore(name))
            block = stack.enter_context(nc.Block())

            def make_body(eng):
                def body(_e):
                    e = eng_obj[eng]
                    for item in self.streams[eng]:
                        if item[0] == "wait":
                            e.wait_ge(handles[item[1]], item[2])
                        else:
                            _, emit, sem, amt = item
                            inst = emit()
                            if sem is not None:
                                inst.then_inc(handles[sem], amt)
                return body

            for eng in self.ENGINES:
                getattr(block, blk_name[eng])(make_body(eng))


class _Group:

    def __init__(self, em, eng, writes, sem, amt):
        self.em = em
        self.eng = eng
        self.writes = writes
        self.sem = sem if sem is not None else "e_" + eng
        self.amt = amt
        self.reads = []
        self.first = True

    def add(self, emit, reads=(), last=False):
        em = self.em
        need = {}
        em._collect(need, reads, war=False)
        if self.first:
            em._collect(need, self.writes, war=True)
            self.first = False
        em._emit_waits(self.eng, need)
        em._sem(self.sem)
        em.counts[self.sem] += self.amt
        tok = (self.sem, em.counts[self.sem])
        em.streams[self.eng].append(("op", emit, self.sem, self.amt))
        for b in reads:
            b.readers.append(tok)
        if last:
            for b in self.writes:
                b.last_w = tok
                b.readers = []
        return tok


class Pool:

    def __init__(self, nc, n):
        self.slots = []
        self.free = []
        for i in range(n):
            t = nc.alloc_sbuf_tensor(f"slot{i}", [128, 2048], BF16)
            self.slots.append(dict(bf=t, f32=t.bitcast(F32),
                                   buf=Buf(f"slot{i}"), i=i))
            self.free.append(i)

    def take(self):
        assert self.free, "pool exhausted"
        return self.slots[self.free.pop(0)]

    def give(self, slot):
        self.free.append(slot["i"])



def build_program(cfg, dbg=False):
    D, H, G, S, N = cfg["D"], cfg["H"], cfg["G"], cfg["S"], cfg["N"]
    Q, DC, KC = cfg["Q"], cfg["DC"], cfg["KC"]
    QBW, NQB = cfg["QBW"], cfg["NQB"]
    SBW, NSBL = cfg["SBW"], cfg["NSBL"]
    NSB, NS, NCK = cfg["NSB"], cfg["NS"], cfg["NCK"]
    GW = G * 128

    nc = bass.Bass("TRN2", target_bir_lowering=False, debug=False)
    em = Emitter(nc)

    xT = nc.dram_tensor("xT", [D, S], F32, kind="ExternalInput").ap()
    wq = nc.dram_tensor("wq", [H, 128, DC, 128], F32, kind="ExternalInput").ap()
    wk = nc.dram_tensor("wk", [G, 128, DC, 128], F32, kind="ExternalInput").ap()
    wv = nc.dram_tensor("wv", [D, GW], F32, kind="ExternalInput").ap()
    wo = nc.dram_tensor("wo", [DC, 128, DC, 128], F32, kind="ExternalInput").ap()
    w0 = nc.dram_tensor("w0", [D, N], F32, kind="ExternalInput").ap()
    w1 = nc.dram_tensor("w1", [N, D], F32, kind="ExternalInput").ap()
    b0f = nc.dram_tensor("b0f", [128, N // 128], F32, kind="ExternalInput").ap()
    b1f = nc.dram_tensor("b1f", [128, DC], F32, kind="ExternalInput").ap()
    ctabd = nc.dram_tensor("ctab", [128, S], BF16, kind="ExternalInput").ap()
    stabd = nc.dram_tensor("stab", [128, S], BF16, kind="ExternalInput").ap()
    swmd = nc.dram_tensor("swm", [128, 128], BF16, kind="ExternalInput").ap()
    outT = nc.dram_tensor("outT", [D, Q], F32, kind="ExternalOutput").ap()

    PT = [nc.alloc_psum_tensor(f"ps{i}", [128, 1024], F32) for i in range(4)]
    PS = [Buf(f"ps{i}_{h}") for i in range(4) for h in range(2)]

    def ps_ap(i, lo, hi):
        return PT[i // 2][:, (i % 2) * 512 + lo:(i % 2) * 512 + hi]

    swap_sb = nc.alloc_sbuf_tensor("swap_sb", [128, 128], BF16)
    b0fm = nc.alloc_sbuf_tensor("b0fm", [128, N // 128], F32)
    b1fm = nc.alloc_sbuf_tensor("b1fm", [128, DC], F32)
    ones_col = nc.alloc_sbuf_tensor("ones_col", [128, 1], BF16)
    ones_rb = nc.alloc_sbuf_tensor("ones_rb", [1, 128], BF16)
    ones_rf = nc.alloc_sbuf_tensor("ones_rf", [1, 128], F32)
    eps_r = nc.alloc_sbuf_tensor("eps_r", [1, 1], F32)
    bswap, bb0, bb1 = Buf("swap"), Buf("b0fm"), Buf("b1fm")
    bones_c, bones_rb, bones_rf = Buf("onc"), Buf("onrb"), Buf("onrf")
    beps = Buf("eps")

    pool = Pool(nc, cfg["n_slots"])

    rowA = pool.take()
    rowB = pool.take()
    bA, bB = Buf("rowA"), Buf("rowB")

    T = nc.tensor
    SC = nc.scalar
    DV = nc.vector
    GP = nc.gpsimd
    SY = nc.sync

    dbg_n = [0]

    def dump(name, ap_fn, buf, shape, dtype=BF16):
        if not dbg:
            return
        t = nc.dram_tensor("dbg_" + name, shape, dtype, kind="ExternalOutput").ap()
        em.op("s", (lambda t=t, ap_fn=ap_fn: SY.dma_start(out=t, in_=ap_fn())),
              reads=[buf], sem=f"d_dbg{dbg_n[0]}", amt=16)
        dbg_n[0] += 1

    em.op("dv", lambda: DV.memset(ones_col.ap(), 1.0), writes=[bones_c])
    em.op("dv", lambda: DV.memset(ones_rb.ap(), 1.0), writes=[bones_rb])
    em.op("dv", lambda: DV.memset(ones_rf.ap(), 1.0), writes=[bones_rf])
    em.op("dv", lambda: DV.memset(eps_r.ap(), EPS), writes=[beps])

    em.dma_multi("s", [lambda: SY.dma_start(out=swap_sb.ap(), in_=swmd)], bswap, "d_sw")
    em.dma_multi("s", [lambda: SY.dma_start(out=b0fm.ap(), in_=b0f)], bb0, "d_b0")
    em.dma_multi("s", [lambda: SY.dma_start(out=b1fm.ap(), in_=b1f)], bb1, "d_b1")

    ct_s = pool.take()
    st_s = pool.take()
    em.dma_multi("s", [lambda: SY.dma_start(out=ct_s["bf"][:, 0:S], in_=ctabd)],
                 ct_s["buf"], "d_ct")
    em.dma_multi("s", [lambda: SY.dma_start(out=st_s["bf"][:, 0:S], in_=stabd)],
                 st_s["buf"], "d_st")

    xq_s = [pool.take() for _ in range((DC + 1) // 2)]
    xo_s = [pool.take() for _ in range((DC + 1) // 2)]

    def xq_ap(dc, lo, hi):
        return xq_s[dc // 2]["bf"][:, (dc % 2) * Q + lo:(dc % 2) * Q + hi]

    def xo_ap(dc, lo, hi):
        return xo_s[dc // 2]["bf"][:, (dc % 2) * Q + lo:(dc % 2) * Q + hi]

    def xs_ap(dc, lo, hi):
        if lo >= Q:
            return xo_ap(dc, lo - Q, hi - Q)
        return xq_ap(dc, lo, hi)

    def xs_buf(dc, lo):
        return (xo_s if lo >= Q else xq_s)[dc // 2]["buf"]

    for i in range((DC + 1) // 2):
        emits_q, emits_o = [], []
        for dc in (2 * i, 2 * i + 1):
            if dc < DC:
                emits_q.append(lambda dc=dc: GP.dma_start(
                    out=xq_ap(dc, 0, Q), in_=xT[dc * 128:(dc + 1) * 128, 0:Q]))
                emits_o.append(lambda dc=dc: GP.dma_start(
                    out=xo_ap(dc, 0, Q), in_=xT[dc * 128:(dc + 1) * 128, Q:S]))
        em.dma_multi("g", emits_q, xq_s[i]["buf"], f"d_xq{i}")
        em.dma_multi("g", emits_o, xo_s[i]["buf"], f"d_xo{i}")

    nvs = max(1, (DC + 3) // 4)
    wv_s = [pool.take() for _ in range(nvs)]
    for i in range(nvs):
        emits = []
        for dc in range(i * 4, min(DC, i * 4 + 4)):
            emits.append(lambda dc=dc, i=i: GP.dma_start(
                out=wv_s[i]["bf"][:, (dc % 4) * GW:(dc % 4 + 1) * GW],
                in_=wv[dc * 128:(dc + 1) * 128, :]))
        em.dma_multi("g", emits, wv_s[i]["buf"], f"d_wv{i}")

    def wv_ap(dc):
        return wv_s[dc // 4]["bf"][:, (dc % 4) * GW:(dc % 4 + 1) * GW]

    wk_s = [pool.take() for _ in range(G)]
    for g in range(G):
        em.dma_multi("g", [lambda g=g: GP.dma_start(
            out=wk_s[g]["bf"][:, 0:DC * 128].rearrange("p (dc c) -> p dc c", c=128),
            in_=wk[g])],
            wk_s[g]["buf"], f"d_wk{g}")

    wq_s = [pool.take(), pool.take()]

    def load_wq(h):
        sl = wq_s[h % 2]
        em.dma_multi("g", [lambda h=h, sl=sl: GP.dma_start(
            out=sl["bf"][:, 0:DC * 128].rearrange("p (dc c) -> p dc c", c=128),
            in_=wq[h])],
            sl["buf"], f"d_wq{h % 2}")

    load_wq(0)
    load_wq(1)

    scr1 = pool.take()
    scr2 = pool.take()
    scr1_h = [Buf("scr1a"), Buf("scr1b")]
    scr2_h = [Buf("scr2a"), Buf("scr2b")]

    vt_s = [pool.take() for _ in range(max(1, (KC + 3) // 4))]

    def v_ap(kc, g):
        base = (kc % 4) * GW
        return vt_s[kc // 4]["bf"][:, base + g * 128: base + (g + 1) * 128]

    for kc in range(KC):
        pb = kc % 2
        grp = em.group("pe", writes=[PS[pb]])
        for dc in range(DC):
            grp.add((lambda kc=kc, dc=dc, pb=pb: T.matmul(
                ps_ap(pb, 0, GW),
                xs_ap(dc, kc * 128, (kc + 1) * 128),
                wv_ap(dc),
                start=(dc == 0), stop=(dc == DC - 1))),
                reads=[xs_buf(dc, kc * 128), wv_s[dc // 4]["buf"]],
                last=(dc == DC - 1))
        sl = vt_s[kc // 4]
        em.op("sc", (lambda kc=kc, pb=pb, sl=sl: SC.activation(
            sl["bf"][:, (kc % 4) * GW:(kc % 4 + 1) * GW],
            ps_ap(pb, 0, GW), AF.Copy)),
            reads=[PS[pb]], writes=[sl["buf"]])

    dump("v0", lambda: vt_s[0]["bf"][:, 0:GW], vt_s[0]["buf"], [128, GW])

    kr_s = [pool.take() for _ in range(G)]

    NSBP = (NSBL + 1) // 2
    for g in range(G):
        for sbp in range(NSBP):
            sbs = [s for s in (2 * sbp, 2 * sbp + 1) if s < NSBL]
            grp = em.group("pe", writes=[PS[2 + (s % 2)] for s in sbs])
            for dc in range(DC):
                for sb in sbs:
                    grp.add((lambda g=g, sb=sb, dc=dc: T.matmul(
                        ps_ap(2 + (sb % 2), 0, SBW),
                        wk_s[g]["bf"][:, dc * 128:(dc + 1) * 128],
                        xs_ap(dc, sb * SBW, (sb + 1) * SBW),
                        start=(dc == 0), stop=(dc == DC - 1))),
                        reads=[wk_s[g]["buf"], xs_buf(dc, sb * SBW)],
                        last=(dc == DC - 1 and sb == sbs[-1]))
            for sb in sbs:
                pa = sb % 2
                pw = 6 + pa
                klo = pa * 1024
                pb = 2 + pa
                em.op("sc", (lambda pb=pb, klo=klo: SC.activation(
                    scr1["bf"][:, klo:klo + SBW], ps_ap(pb, 0, SBW), AF.Copy)),
                    reads=[PS[pb]], writes=[scr1_h[pa]])
                em.op("pe", (lambda pw=pw, klo=klo: T.matmul(
                    ps_ap(pw, 0, SBW), swap_sb.ap(), scr1["bf"][:, klo:klo + SBW],
                    start=True, stop=True)),
                    reads=[bswap, scr1_h[pa]], writes=[PS[pw]])
                em.op("dv", (lambda g=g, sb=sb, klo=klo: DV.tensor_mul(
                    kr_s[g]["bf"][:, sb * SBW:(sb + 1) * SBW],
                    scr1["bf"][:, klo:klo + SBW],
                    ct_s["bf"][:, sb * SBW:(sb + 1) * SBW])),
                    reads=[scr1_h[pa], ct_s["buf"]], writes=[kr_s[g]["buf"]])
                em.op("dv", (lambda sb=sb, pw=pw, klo=klo: DV.tensor_mul(
                    scr2["bf"][:, klo:klo + SBW], ps_ap(pw, 0, SBW),
                    st_s["bf"][:, sb * SBW:(sb + 1) * SBW])),
                    reads=[PS[pw], st_s["buf"]], writes=[scr2_h[pa]])
                em.op("dv", (lambda g=g, sb=sb, klo=klo: DV.tensor_add(
                    kr_s[g]["bf"][:, sb * SBW:(sb + 1) * SBW],
                    kr_s[g]["bf"][:, sb * SBW:(sb + 1) * SBW],
                    scr2["bf"][:, klo:klo + SBW])),
                    reads=[kr_s[g]["buf"], scr2_h[pa]], writes=[kr_s[g]["buf"]])

    dump("kr0", lambda: kr_s[0]["bf"][:, 0:S], kr_s[0]["buf"], [128, S])

    for sl in wv_s + wk_s + xo_s:
        pool.give(sl)

    qr_s = [pool.take(), pool.take()]
    qr_bufs = [Buf(f"qr{i}") for i in range(4)]

    def qr_ap(h, lo, hi):
        return qr_s[(h // 2) % 2]["bf"][:, (h % 2) * 1024 + lo:(h % 2) * 1024 + hi]

    def qr_buf(h):
        return qr_bufs[h % 4]

    pt_s = pool.take()
    pt_h = [Buf("ptA"), Buf("ptB")]
    den_s = [pool.take(), pool.take()]
    den_pl = [pool.take(), pool.take()]
    attn_s = [pool.take() for _ in range((H + 1) // 2)]
    wo_s = [pool.take(), pool.take()]
    xm_s = [pool.take(), pool.take()]
    if KC >= 8:
        POOLSET = frozenset(kc for kc in range(KC) if kc % 8 < 3)
    else:
        POOLSET = frozenset([KC - 1])

    def attn_ap(h, lo, hi):
        return attn_s[h // 2]["bf"][:, (h % 2) * 1024 + lo:(h % 2) * 1024 + hi]

    def load_wo(d):
        sl = wo_s[d % 2]
        em.dma_multi("g", [lambda d=d, sl=sl: GP.dma_start(
            out=sl["bf"][:, 0:DC * 128].rearrange("p (dc c) -> p dc c", c=128),
            in_=wo[d])],
            sl["buf"], f"d_wo{d % 2}")

    def load_xm(d):
        sl = xm_s[d % 2]
        em.dma_multi("s", [lambda d=d, sl=sl: SY.dma_start(
            out=sl["f32"][:, 0:Q], in_=xT[d * 128:(d + 1) * 128, 0:Q])],
            sl["buf"], f"d_xm{d % 2}")

    def qproj_rope(h, lo_bank):
        pt_i = lo_bank // 2
        grp = em.group("pe", writes=[PS[lo_bank], PS[lo_bank + 1]])
        for dc in range(DC):
            for qb in range(NQB):
                grp.add((lambda h=h, qb=qb, dc=dc, pt_i=pt_i: T.matmul(
                    PT[pt_i][:, qb * 512:qb * 512 + QBW],
                    wq_s[h % 2]["bf"][:, dc * 128:(dc + 1) * 128],
                    xq_ap(dc, qb * QBW, (qb + 1) * QBW),
                    start=(dc == 0), stop=(dc == DC - 1))),
                    reads=[wq_s[h % 2]["buf"], xq_s[dc // 2]["buf"]],
                    last=(dc == DC - 1 and qb == NQB - 1))
        for qb in range(NQB):
            pa = qb % 2
            qlo = pa * 1024
            em.op("sc", (lambda qb=qb, qlo=qlo, pt_i=pt_i: SC.activation(
                scr1["bf"][:, qlo:qlo + QBW],
                PT[pt_i][:, qb * 512:qb * 512 + QBW], AF.Copy)),
                reads=[PS[lo_bank + qb]], writes=[scr1_h[pa]])
            em.op("pe", (lambda qb=qb, qlo=qlo, pt_i=pt_i: T.matmul(
                PT[pt_i][:, qb * 512:qb * 512 + QBW], swap_sb.ap(),
                scr1["bf"][:, qlo:qlo + QBW],
                start=True, stop=True)),
                reads=[bswap, scr1_h[pa]], writes=[PS[lo_bank + qb]])
            em.op("dv", (lambda h=h, qb=qb, qlo=qlo: DV.tensor_mul(
                qr_ap(h, qb * QBW, (qb + 1) * QBW),
                scr1["bf"][:, qlo:qlo + QBW],
                ct_s["bf"][:, qb * QBW:(qb + 1) * QBW])),
                reads=[scr1_h[pa], ct_s["buf"]], writes=[qr_buf(h)])
            em.op("dv", (lambda qb=qb, qlo=qlo, pt_i=pt_i: DV.tensor_mul(
                scr2["bf"][:, qlo:qlo + QBW],
                PT[pt_i][:, qb * 512:qb * 512 + QBW],
                st_s["bf"][:, qb * QBW:(qb + 1) * QBW])),
                reads=[PS[lo_bank + qb], st_s["buf"]], writes=[scr2_h[pa]])
            em.op("dv", (lambda h=h, qb=qb, qlo=qlo: DV.tensor_add(
                qr_ap(h, qb * QBW, (qb + 1) * QBW),
                qr_ap(h, qb * QBW, (qb + 1) * QBW),
                scr2["bf"][:, qlo:qlo + QBW])),
                reads=[qr_buf(h), scr2_h[pa]],
                writes=[qr_buf(h)])

    def st_emit(h, kc):
        g = h % G
        sb = kc % 2
        grp = em.group("pe", writes=[PS[2 * sb], PS[2 * sb + 1]])
        for qb in range(NQB):
            grp.add((lambda h=h, g=g, kc=kc, qb=qb, sb=sb: T.matmul(
                PT[sb][:, qb * QBW:(qb + 1) * QBW],
                kr_s[g]["bf"][:, kc * 128:(kc + 1) * 128],
                qr_ap(h, qb * QBW, (qb + 1) * QBW),
                start=True, stop=True)),
                reads=[kr_s[g]["buf"], qr_buf(h)],
                last=(qb == NQB - 1))

    def tail_stage(h, stage):
        par = h % 2
        av_lo = 4 + 2 * par
        dlo = par * 1024
        if stage == 0:
            em.op("dv", (lambda h=h, par=par: DV.tensor_copy(
                attn_ap(h, 0, Q), PT[2 + par][:, 0:Q])),
                reads=[PS[av_lo], PS[av_lo + 1]], writes=[attn_s[h // 2]["buf"]])
            em.op("dv", (lambda dlo=dlo, par=par: DV.tensor_add(
                scr1["bf"][:, dlo:dlo + Q], den_s[par]["f32"][:, 0:Q],
                den_pl[par]["f32"][:, 0:Q])),
                reads=[den_s[par]["buf"], den_pl[par]["buf"]],
                writes=[scr1_h[par]])
        elif stage == 1:
            dgrp = em.group("pe", writes=[PS[av_lo], PS[av_lo + 1]])
            for qb in range(NQB):
                dgrp.add((lambda qb=qb, dlo=dlo, par=par: T.matmul(
                    PT[2 + par][0:1, qb * 512:qb * 512 + QBW], ones_col.ap(),
                    scr1["bf"][:, dlo + qb * QBW:dlo + (qb + 1) * QBW],
                    start=True, stop=True)),
                    reads=[bones_c, scr1_h[par]], last=(qb == NQB - 1))
        elif stage == 2:
            em.op("sc", (lambda par=par: SC.activation(
                rowB["f32"][0:1, 0:Q], PT[2 + par][0:1, 0:Q], AF.Ln)),
                reads=[PS[av_lo], PS[av_lo + 1]], writes=[bB])
        elif stage == 3:
            em.op("sc", (lambda: SC.activation(
                rowA["bf"][0:1, 0:Q], rowB["f32"][0:1, 0:Q], AF.Exp, scale=-1.0)),
                reads=[bB], writes=[bA])
        elif stage == 4:
            bgrp = em.group("pe", writes=[PS[av_lo], PS[av_lo + 1]])
            for qb in range(NQB):
                bgrp.add((lambda qb=qb, par=par: T.matmul(
                    PT[2 + par][:, qb * QBW:(qb + 1) * QBW], ones_rb.ap(),
                    rowA["bf"][0:1, qb * QBW:(qb + 1) * QBW],
                    start=True, stop=True)),
                    reads=[bones_rb, bA], last=(qb == NQB - 1))
        else:
            em.op("dv", (lambda h=h, par=par: DV.tensor_mul(
                attn_ap(h, 0, Q), attn_ap(h, 0, Q), PT[2 + par][:, 0:Q])),
                reads=[attn_s[h // 2]["buf"], PS[av_lo], PS[av_lo + 1]],
                writes=[attn_s[h // 2]["buf"]])

    qproj_rope(0, 4)
    if H > 1:
        qproj_rope(1, 6)
    if H > 2:
        load_wq(2)
    if H > 3:
        load_wq(3)

    TAIL_AT = (1, 2, 3, 4, 5, 6)
    QPROJ_AT = tuple(range(7, 13))
    for h in range(H):
        g = h % G
        par = h % 2
        av_lo = 4 + 2 * par
        other_lo = 4 + 2 * (1 - par)
        st_emit(h, 0)
        avgrp = None
        dv_first = True
        em.op("g", (lambda par=par: GP.memset(den_pl[par]["f32"][:, 0:Q], 0.0)),
              writes=[den_pl[par]["buf"]])
        qgrp = [None]
        qh = h + 2
        in_loop_tails = sum(1 for x in TAIL_AT if x < KC)
        for kc in range(KC):
            if kc + 1 < KC:
                st_emit(h, kc + 1)
            sb = kc % 2
            ptp = kc % 2
            ptlo = ptp * 1024
            em.op("sc", (lambda ptlo=ptlo, sb=sb: SC.activation(
                pt_s["bf"][:, ptlo:ptlo + Q], PT[sb][:, 0:Q], AF.Exp, scale=SCALE)),
                reads=[PS[2 * sb], PS[2 * sb + 1]], writes=[pt_h[ptp]])
            if kc in POOLSET:
                acc, bacc = den_pl[par], den_pl[par]["buf"]
                em.op("g", (lambda acc=acc, ptlo=ptlo: GP.tensor_add(
                    acc["f32"][:, 0:Q], acc["f32"][:, 0:Q],
                    pt_s["bf"][:, ptlo:ptlo + Q])),
                    reads=[pt_h[ptp], bacc], writes=[bacc])
            else:
                acc, bacc = den_s[par], den_s[par]["buf"]
                if dv_first:
                    em.op("dv", (lambda acc=acc, ptlo=ptlo: DV.tensor_copy(
                        acc["f32"][:, 0:Q], pt_s["bf"][:, ptlo:ptlo + Q])),
                        reads=[pt_h[ptp]], writes=[bacc])
                    dv_first = False
                else:
                    em.op("dv", (lambda acc=acc, ptlo=ptlo: DV.tensor_add(
                        acc["f32"][:, 0:Q], acc["f32"][:, 0:Q],
                        pt_s["bf"][:, ptlo:ptlo + Q])),
                        reads=[pt_h[ptp], bacc], writes=[bacc])
            if kc == 0:
                avgrp = em.group("pe", writes=[PS[av_lo], PS[av_lo + 1]])
            if qh < H and KC > QPROJ_AT[0]:
                if kc == QPROJ_AT[0]:
                    qgrp[0] = em.group("pe", writes=[PS[other_lo], PS[other_lo + 1]])
                if qgrp[0] is not None and kc >= QPROJ_AT[0]:
                    n_mem = DC * NQB
                    n_slots_q = KC - QPROJ_AT[0]
                    per = (n_mem + n_slots_q - 1) // n_slots_q
                    i0 = (kc - QPROJ_AT[0]) * per
                    for i in range(i0, min(i0 + per, n_mem)):
                        dc, qb = divmod(i, NQB)
                        qgrp[0].add((lambda qh=qh, qb=qb, dc=dc, other_lo=other_lo: T.matmul(
                            PT[other_lo // 2][:, qb * 512:qb * 512 + QBW],
                            wq_s[qh % 2]["bf"][:, dc * 128:(dc + 1) * 128],
                            xq_ap(dc, qb * QBW, (qb + 1) * QBW),
                            start=(dc == 0), stop=(dc == DC - 1))),
                            reads=[wq_s[qh % 2]["buf"], xq_s[dc // 2]["buf"]],
                            last=(i == n_mem - 1))
            for qb in range(NQB):
                avgrp.add((lambda kc=kc, g=g, qb=qb, par=par, ptlo=ptlo: T.matmul(
                    PT[2 + par][:, qb * QBW:(qb + 1) * QBW],
                    v_ap(kc, g),
                    pt_s["bf"][:, ptlo + qb * QBW:ptlo + (qb + 1) * QBW],
                    start=(kc == 0), stop=(kc == KC - 1))),
                    reads=[vt_s[kc // 4]["buf"], pt_h[ptp]],
                    last=(kc == KC - 1 and qb == NQB - 1))
            if h > 0 and kc in TAIL_AT:
                tail_stage(h - 1, TAIL_AT.index(kc))
        if h > 0:
            for stage in range(in_loop_tails, 6):
                tail_stage(h - 1, stage)
        if qh < H:
            if KC <= QPROJ_AT[0]:
                qproj_rope(qh, other_lo)
            else:
                for qb in range(NQB):
                    pa = qb % 2
                    qlo = pa * 1024
                    pt_i = other_lo // 2
                    em.op("sc", (lambda qb=qb, qlo=qlo, pt_i=pt_i: SC.activation(
                        scr2["bf"][:, qlo:qlo + QBW],
                        PT[pt_i][:, qb * 512:qb * 512 + QBW], AF.Copy)),
                        reads=[PS[other_lo + qb]], writes=[scr2_h[pa]])
                    em.op("pe", (lambda qb=qb, qlo=qlo, pt_i=pt_i: T.matmul(
                        PT[pt_i][:, qb * 512:qb * 512 + QBW], swap_sb.ap(),
                        scr2["bf"][:, qlo:qlo + QBW],
                        start=True, stop=True)),
                        reads=[bswap, scr2_h[pa]], writes=[PS[other_lo + qb]])
                    em.op("dv", (lambda qh=qh, qb=qb, qlo=qlo: DV.tensor_mul(
                        qr_ap(qh, qb * QBW, (qb + 1) * QBW),
                        scr2["bf"][:, qlo:qlo + QBW],
                        ct_s["bf"][:, qb * QBW:(qb + 1) * QBW])),
                        reads=[scr2_h[pa], ct_s["buf"]],
                        writes=[qr_buf(qh)])
                    em.op("dv", (lambda qb=qb, qlo=qlo, pt_i=pt_i: DV.tensor_mul(
                        scr2["bf"][:, qlo:qlo + QBW],
                        PT[pt_i][:, qb * 512:qb * 512 + QBW],
                        st_s["bf"][:, qb * QBW:(qb + 1) * QBW])),
                        reads=[PS[other_lo + qb], st_s["buf"]], writes=[scr2_h[pa]])
                    em.op("dv", (lambda qh=qh, qb=qb, qlo=qlo: DV.tensor_add(
                        qr_ap(qh, qb * QBW, (qb + 1) * QBW),
                        qr_ap(qh, qb * QBW, (qb + 1) * QBW),
                        scr2["bf"][:, qlo:qlo + QBW])),
                        reads=[qr_buf(qh), scr2_h[pa]],
                        writes=[qr_buf(qh)])
            if qh + 2 < H:
                load_wq(qh + 2)
        if h == 0:
            load_wo(0)
            load_xm(0)
        if h == 1:
            load_wo(1)
            load_xm(1)
    for stage in range(6):
        tail_stage(H - 1, stage)

    dump("attn0", lambda: attn_ap(0, 0, Q), attn_s[0]["buf"], [128, Q])

    for b in pt_h:
        if b.last_w is not None:
            pt_s["buf"].readers.append(b.last_w)
        pt_s["buf"].readers.extend(b.readers)
    for i, b in enumerate(qr_bufs):
        sl = qr_s[i // 2]
        if b.last_w is not None:
            sl["buf"].readers.append(b.last_w)
        sl["buf"].readers.extend(b.readers)

    for sl in kr_s + vt_s + qr_s + [pt_s] + den_s + den_pl + xq_s + [ct_s, st_s] + wq_s:
        pool.give(sl)

    racc = [pool.take() for _ in range(DC)]
    w0_s = [pool.take() for _ in range((DC + 1) // 2)]
    ab_s = [pool.take(), pool.take()]
    bab = [Buf("ab0"), Buf("ab1")]
    x1b_s = []
    w1_s = []
    dg_s = []
    x1b_bufs = [Buf(f"x1b{dc}") for dc in range(DC)]

    def x1b_ap(dc, lo, hi):
        return x1b_s[dc // 2]["bf"][:, (dc % 2) * 1024 + lo:(dc % 2) * 1024 + hi]

    def w0_ap(dc, lo, hi):
        return w0_s[dc // 2]["bf"][:, (dc % 2) * NSB + lo:(dc % 2) * NSB + hi]

    def dg_ap(ncl, lo, hi):
        return dg_s[ncl // 2]["bf"][:, (ncl % 2) * 1024 + lo:(ncl % 2) * 1024 + hi]

    def load_w0(ns):
        for i in range((DC + 1) // 2):
            emits = []
            for dc in (2 * i, 2 * i + 1):
                if dc < DC:
                    emits.append(lambda ns=ns, dc=dc: GP.dma_start(
                        out=w0_ap(dc, 0, NSB),
                        in_=w0[dc * 128:(dc + 1) * 128, ns * NSB:(ns + 1) * NSB]))
            em.dma_multi("g", emits, w0_s[i]["buf"], f"d_w0_{i}")

    def load_w1(ns):
        for ncl in range(NCK):
            row = ns * NSB + ncl * 128
            em.dma_multi("g", [lambda row=row, ncl=ncl: GP.dma_start(
                out=w1_s[ncl]["bf"][:, 0:D], in_=w1[row:row + 128, :])],
                w1_s[ncl]["buf"], f"d_w1_{ncl}")

    load_w0(0)

    stat_groups = []
    for qb in range(NQB):
        stat_groups.append(em.group("pe", writes=[PS[4 + qb]]))
    for qb in range(NQB):
        stat_groups.append(em.group("pe", writes=[PS[6 + qb]]))

    def stats_add(stat_groups, dc, src_scr, src_bufs, which, last):
        for qb in range(NQB):
            stat_groups[which * NQB + qb].add(
                (lambda dc=dc, qb=qb, which=which, src_scr=src_scr: T.matmul(
                    PT[2 + which][0:1, qb * 512:qb * 512 + QBW], ones_col.ap(),
                    src_scr["bf"][:, (dc % 2) * 1024 + qb * QBW:
                                  (dc % 2) * 1024 + (qb + 1) * QBW],
                    start=(dc == 0), stop=last)),
                reads=[src_bufs[dc % 2], bones_c], last=last)

    for d in range(DC):
        par = d % 2
        st_halves = [PS[2 * par], PS[2 * par + 1]]
        grp = em.group("pe", writes=st_halves)
        for dvc in range(DC):
            for qb in range(NQB):
                grp.add((lambda d=d, qb=qb, dvc=dvc, par=par: T.matmul(
                    PT[par][:, qb * QBW:(qb + 1) * QBW],
                    wo_s[d % 2]["bf"][:, dvc * 128:(dvc + 1) * 128],
                    attn_ap(dvc, qb * QBW, (qb + 1) * QBW),
                    start=(dvc == 0), stop=(dvc == DC - 1))),
                    reads=[wo_s[d % 2]["buf"], attn_s[dvc // 2]["buf"]],
                    last=(qb == NQB - 1 and dvc == DC - 1))
        em.op("dv", (lambda d=d, par=par: DV.tensor_add(
            racc[d]["f32"][:, 0:Q], PT[par][:, 0:Q], xm_s[d % 2]["f32"][:, 0:Q])),
            reads=st_halves + [xm_s[d % 2]["buf"]], writes=[racc[d]["buf"]])
        if d == 0:
            dump("r1_0", lambda: racc[0]["f32"][:, 0:Q], racc[0]["buf"],
                 [128, Q], F32)
        if d + 2 < DC:
            load_wo(d + 2)
            load_xm(d + 2)
        em.op("sc", (lambda d=d, par=par: SC.activation(
            scr1["bf"][:, par * 1024:par * 1024 + Q], racc[d]["f32"][:, 0:Q], AF.Copy)),
            reads=[racc[d]["buf"]], writes=[scr1_h[par]])
        em.op("sc", (lambda par=par: SC.activation(
            scr2["bf"][:, par * 1024:par * 1024 + Q],
            scr1["bf"][:, par * 1024:par * 1024 + Q], AF.Square)),
            reads=[scr1_h[par]], writes=[scr2_h[par]])
        if d > 0:
            stats_add(stat_groups, d - 1, scr1, scr1_h, 0, last=False)
            stats_add(stat_groups, d - 1, scr2, scr2_h, 1, last=False)
    stats_add(stat_groups, DC - 1, scr1, scr1_h, 0, last=True)
    stats_add(stat_groups, DC - 1, scr2, scr2_h, 1, last=True)

    pool.give(wo_s[0])
    pool.give(wo_s[1])
    pool.give(xm_s[0])
    pool.give(xm_s[1])
    for sl in attn_s:
        pool.give(sl)
    x1b_s.extend(pool.take() for _ in range((DC + 1) // 2))
    w1_s.extend(pool.take() for _ in range(NCK))
    dg_s.extend(pool.take() for _ in range((NCK + 1) // 2))

    def ln_finalize():
        em.op("sc", (lambda: SC.activation(rowA["f32"][0:1, 0:Q], PT[2][0:1, 0:Q],
                                           AF.Copy, scale=1.0 / D)),
              reads=[PS[4], PS[5]], writes=[bA])
        em.op("sc", (lambda: SC.activation(rowB["f32"][0:1, 0:Q], PT[3][0:1, 0:Q],
                                           AF.Copy, scale=1.0 / D)),
              reads=[PS[6], PS[7]], writes=[bB])
        em.op("sc", (lambda: SC.activation(PT[0][0:1, 0:Q], rowA["f32"][0:1, 0:Q],
                                           AF.Square)),
              reads=[bA], writes=[PS[0], PS[1]])
        em.op("dv", (lambda: DV.tensor_sub(PT[1][0:1, 0:Q], rowB["f32"][0:1, 0:Q],
                                           PT[0][0:1, 0:Q])),
              reads=[bB, PS[0], PS[1]], writes=[PS[2], PS[3]])
        em.op("sc", (lambda: SC.activation(PT[0][0:1, 0:Q], PT[1][0:1, 0:Q],
                                           AF.Ln, bias=eps_r[0:1, 0:1])),
              reads=[PS[2], PS[3], beps], writes=[PS[0], PS[1]])
        em.op("sc", (lambda: SC.activation(rowB["f32"][0:1, 0:Q], PT[0][0:1, 0:Q],
                                           AF.Exp, scale=-0.5)),
              reads=[PS[0], PS[1]], writes=[bB])
        em.op("dv", (lambda: DV.scalar_tensor_tensor(
            rowA["f32"][0:1, 0:Q], rowA["f32"][0:1, 0:Q], -1.0,
            rowB["f32"][0:1, 0:Q], ALU.mult, ALU.mult)),
            reads=[bA, bB], writes=[bA])
        agrp = em.group("pe", writes=[PS[0], PS[1]])
        for qb in range(NQB):
            agrp.add((lambda qb=qb: T.matmul(
                PT[0][:, qb * QBW:(qb + 1) * QBW], ones_rf.ap(),
                rowB["f32"][0:1, qb * QBW:(qb + 1) * QBW], start=True, stop=True)),
                reads=[bones_rf, bB], last=(qb == NQB - 1))
        bgrp = em.group("pe", writes=[PS[2], PS[3]])
        for qb in range(NQB):
            bgrp.add((lambda qb=qb: T.matmul(
                PT[1][:, qb * QBW:(qb + 1) * QBW], ones_rf.ap(),
                rowA["f32"][0:1, qb * QBW:(qb + 1) * QBW], start=True, stop=True)),
                reads=[bones_rf, bA], last=(qb == NQB - 1))
        em.op("sc", (lambda: SC.activation(ab_s[0]["bf"][:, 0:Q], PT[0][:, 0:Q],
                                           AF.Copy)),
              reads=[PS[0], PS[1]], writes=[bab[0]])
        em.op("sc", (lambda: SC.activation(ab_s[1]["bf"][:, 0:Q], PT[1][:, 0:Q],
                                           AF.Copy)),
              reads=[PS[2], PS[3]], writes=[bab[1]])

    ln_finalize()

    for dc in range(DC):
        eng = "dv"
        E = GP if eng == "g" else DV
        em.op(eng, (lambda E=E, dc=dc: E.tensor_mul(
            racc[dc]["f32"][:, 0:Q], racc[dc]["f32"][:, 0:Q],
            ab_s[0]["bf"][:, 0:Q])),
            reads=[racc[dc]["buf"], bab[0]], writes=[racc[dc]["buf"]])
        em.op(eng, (lambda E=E, dc=dc: E.tensor_add(
            racc[dc]["f32"][:, 0:Q], racc[dc]["f32"][:, 0:Q],
            ab_s[1]["bf"][:, 0:Q])),
            reads=[racc[dc]["buf"], bab[1]], writes=[racc[dc]["buf"]])
        em.op("sc", (lambda dc=dc: SC.activation(
            x1b_ap(dc, 0, Q), racc[dc]["f32"][:, 0:Q], AF.Copy)),
            reads=[racc[dc]["buf"]], writes=[x1b_bufs[dc]])

    dump("x1_0", lambda: racc[0]["f32"][:, 0:Q], racc[0]["buf"], [128, Q], F32)

    load_w1(0)

    for ns in range(NS):
        for ncl in range(NCK):
            par = ncl % 2
            st_halves = [PS[2 * par], PS[2 * par + 1]]
            grp = em.group("pe", writes=st_halves)
            for dc in range(DC):
                for qb in range(NQB):
                    grp.add((lambda ncl=ncl, qb=qb, dc=dc, par=par: T.matmul(
                        PT[par][:, qb * QBW:(qb + 1) * QBW],
                        w0_ap(dc, ncl * 128, (ncl + 1) * 128),
                        x1b_ap(dc, qb * QBW, (qb + 1) * QBW),
                        start=(dc == 0), stop=(dc == DC - 1))),
                        reads=[w0_s[dc // 2]["buf"], x1b_bufs[dc]],
                        last=(qb == NQB - 1 and dc == DC - 1))
            col = ns * NCK + ncl
            em.op("sc", (lambda ncl=ncl, par=par, col=col: SC.activation(
                dg_ap(ncl, 0, Q), PT[par][:, 0:Q], AF.Gelu,
                bias=b0fm[:, col:col + 1])),
                reads=st_halves + [bb0], writes=[dg_s[ncl // 2]["buf"]])
            if ns == 0 and ncl == 0:
                dump("dg0", lambda: dg_ap(0, 0, Q), dg_s[0]["buf"], [128, Q])
        if ns + 1 < NS:
            load_w0(ns + 1)
        for dc2 in range(DC):
            par = dc2 % 2
            av_halves = [PS[4 + 2 * par], PS[5 + 2 * par]]
            grp = em.group("pe", writes=av_halves)
            for ncl in range(NCK):
                for qb in range(NQB):
                    grp.add((lambda dc2=dc2, qb=qb, ncl=ncl, par=par: T.matmul(
                        PT[2 + par][:, qb * QBW:(qb + 1) * QBW],
                        w1_s[ncl]["bf"][:, dc2 * 128:(dc2 + 1) * 128],
                        dg_ap(ncl, qb * QBW, (qb + 1) * QBW),
                        start=(ncl == 0), stop=(ncl == NCK - 1))),
                        reads=[w1_s[ncl]["buf"], dg_s[ncl // 2]["buf"]],
                        last=(qb == NQB - 1 and ncl == NCK - 1))
            if ns < NS - 1:
                em.op("dv", (lambda dc2=dc2, par=par: DV.tensor_add(
                    racc[dc2]["f32"][:, 0:Q], racc[dc2]["f32"][:, 0:Q],
                    PT[2 + par][:, 0:Q])),
                    reads=av_halves + [racc[dc2]["buf"]], writes=[racc[dc2]["buf"]])
            else:
                em.op("dv", (lambda dc2=dc2, par=par: DV.scalar_tensor_tensor(
                    racc[dc2]["f32"][:, 0:Q], PT[2 + par][:, 0:Q],
                    b1fm[:, dc2:dc2 + 1], racc[dc2]["f32"][:, 0:Q],
                    ALU.add, ALU.add)),
                    reads=av_halves + [racc[dc2]["buf"], bb1],
                    writes=[racc[dc2]["buf"]])
        if ns + 1 < NS:
            load_w1(ns + 1)

    stat_groups = []
    for qb in range(NQB):
        stat_groups.append(em.group("pe", writes=[PS[4 + qb]]))
    for qb in range(NQB):
        stat_groups.append(em.group("pe", writes=[PS[6 + qb]]))
    for dc in range(DC):
        par = dc % 2
        em.op("sc", (lambda dc=dc, par=par: SC.activation(
            scr1["bf"][:, par * 1024:par * 1024 + Q], racc[dc]["f32"][:, 0:Q],
            AF.Copy)),
            reads=[racc[dc]["buf"]], writes=[scr1_h[par]])
        em.op("sc", (lambda par=par: SC.activation(
            scr2["bf"][:, par * 1024:par * 1024 + Q],
            scr1["bf"][:, par * 1024:par * 1024 + Q], AF.Square)),
            reads=[scr1_h[par]], writes=[scr2_h[par]])
        if dc > 0:
            stats_add(stat_groups, dc - 1, scr1, scr1_h, 0, last=False)
            stats_add(stat_groups, dc - 1, scr2, scr2_h, 1, last=False)
    stats_add(stat_groups, DC - 1, scr1, scr1_h, 0, last=True)
    stats_add(stat_groups, DC - 1, scr2, scr2_h, 1, last=True)

    ln_finalize()

    for dc in range(DC):
        eng = "dv"
        E = GP if eng == "g" else DV
        em.op(eng, (lambda E=E, dc=dc: E.tensor_mul(
            racc[dc]["f32"][:, 0:Q], racc[dc]["f32"][:, 0:Q],
            ab_s[0]["bf"][:, 0:Q])),
            reads=[racc[dc]["buf"], bab[0]], writes=[racc[dc]["buf"]])
        em.op(eng, (lambda E=E, dc=dc: E.tensor_add(
            racc[dc]["f32"][:, 0:Q], racc[dc]["f32"][:, 0:Q],
            ab_s[1]["bf"][:, 0:Q])),
            reads=[racc[dc]["buf"], bab[1]], writes=[racc[dc]["buf"]])
        em.op("s", (lambda dc=dc: SY.dma_start(
            out=outT[dc * 128:(dc + 1) * 128, :], in_=racc[dc]["f32"][:, 0:Q])),
            reads=[racc[dc]["buf"]], sem="d_out", amt=16)

    em.raw_wait("s", "d_out", 16 * DC)
    for i in range(dbg_n[0]):
        em.raw_wait("s", f"d_dbg{i}", 16)

    em.replay()
    return nc



def rope_tables(positions):
    half = 64
    freqs = (1.0 / 10000.0 ** (np.arange(half, dtype=np.float64) / half))
    ang = np.outer(freqs, positions.astype(np.float64))
    c = np.cos(ang)
    s = np.sin(ang)
    C = np.ascontiguousarray(np.vstack([c, c]).astype(ml_dtypes.bfloat16))
    Ssn = np.ascontiguousarray(np.vstack([-s, s]).astype(ml_dtypes.bfloat16))
    return C, Ssn


def perm_idx_heads(nheads):
    hd = 128
    order = np.concatenate([np.arange(0, hd, 2), np.arange(1, hd, 2)])
    return np.concatenate([h * hd + order for h in range(nheads)])


def prep_shared(cfg, W_q, W_k, W_v, W_o, W0, b0, W1, b1):
    D, H, G, N, DC = cfg["D"], cfg["H"], cfg["G"], cfg["N"], cfg["DC"]
    sh = {}
    Wqp = W_q[:, perm_idx_heads(H)]
    Wkp = W_k[:, perm_idx_heads(G)]
    sh["wq"] = np.ascontiguousarray(
        Wqp.reshape(DC, 128, H, 128).transpose(2, 1, 0, 3)).astype(np.float32)
    sh["wk"] = np.ascontiguousarray(
        Wkp.reshape(DC, 128, G, 128).transpose(2, 1, 0, 3)).astype(np.float32)
    sh["wv"] = np.ascontiguousarray(W_v).astype(np.float32)
    sh["wo"] = np.ascontiguousarray(
        W_o.reshape(DC, 128, DC, 128).transpose(2, 1, 0, 3)).astype(np.float32)
    sh["w0"] = np.ascontiguousarray(W0.T).astype(np.float32)
    sh["w1"] = np.ascontiguousarray(W1.T).astype(np.float32)
    sh["b0f"] = np.ascontiguousarray(b0.reshape(N // 128, 128).T).astype(np.float32)
    sh["b1f"] = np.ascontiguousarray(b1.reshape(DC, 128).T).astype(np.float32)
    sh["swm"] = np.roll(np.eye(128, dtype=np.float32), 64, axis=0).astype(
        ml_dtypes.bfloat16)
    return sh


def prep_core(cfg, shared, xb, half):
    Q = cfg["Q"]
    my = np.arange(half * Q, (half + 1) * Q)
    other = np.arange((1 - half) * Q, (2 - half) * Q)
    cols = np.concatenate([my, other])
    d = dict(shared)
    d["xT"] = np.ascontiguousarray(xb[cols, :].T).astype(np.float32)
    C, Ssn = rope_tables(cols)
    d["ctab"] = C
    d["stab"] = Ssn
    return d


_PROG_CACHE = {}


def _get_prog(key, cfg):
    if key not in _PROG_CACHE:
        _PROG_CACHE[key] = build_program(cfg)
    return _PROG_CACHE[key]


def kernel(L0_attn_x, L0_attn_W_q, L0_attn_W_k, L0_attn_W_v, L0_attn_W_o,
           L0_ffn_W0, L0_ffn_bias0, L0_ffn_W1, L0_ffn_bias1):
    cfg = CFG_FULL
    x = np.asarray(L0_attn_x, dtype=np.float32)
    W_q = np.asarray(L0_attn_W_q, dtype=np.float32)
    W_k = np.asarray(L0_attn_W_k, dtype=np.float32)
    W_v = np.asarray(L0_attn_W_v, dtype=np.float32)
    W_o = np.asarray(L0_attn_W_o, dtype=np.float32)
    W0 = np.asarray(L0_ffn_W0, dtype=np.float32)
    b0 = np.asarray(L0_ffn_bias0, dtype=np.float32)
    W1 = np.asarray(L0_ffn_W1, dtype=np.float32)
    b1 = np.asarray(L0_ffn_bias1, dtype=np.float32)

    shared = prep_shared(cfg, W_q, W_k, W_v, W_o, W0, b0, W1, b1)
    in_maps = [prep_core(cfg, shared, x[c // 2], c % 2) for c in range(8)]
    nc = _get_prog("full", cfg)
    res = run_bass_kernel_spmd(nc, in_maps, core_ids=list(range(8)))
    Q = cfg["Q"]
    out = np.empty((4, 2048, 2048), dtype=np.float32)
    for c in range(8):
        b, half = c // 2, c % 2
        out[b, half * Q:(half + 1) * Q, :] = np.asarray(res.results[c]["outT"]).T
    return out


# version 16
# speedup vs baseline: 1.0413x; 1.0121x over previous
import os
import sys
import math
import contextlib

import numpy as np
import ml_dtypes

if os.path.isdir("/opt/trn_rl_repo") and "/opt/trn_rl_repo" not in sys.path:
    sys.path.insert(0, "/opt/trn_rl_repo")

import concourse.bass as bass
import concourse.mybir as mybir
from concourse.bass_utils import run_bass_kernel_spmd

F32 = mybir.dt.float32
BF16 = mybir.dt.bfloat16
AF = mybir.ActivationFunctionType
ALU = mybir.AluOpType

SCALE = 1.0 / math.sqrt(128.0)
EPS = 1e-5


def make_cfg(D, H, G, S, N, NSB, n_slots):
    cfg = dict(D=D, H=H, G=G, S=S, N=N, NSB=NSB, n_slots=n_slots)
    cfg["Q"] = S // 2
    cfg["DC"] = D // 128
    cfg["KC"] = S // 128
    cfg["QBW"] = min(512, cfg["Q"])
    cfg["NQB"] = cfg["Q"] // cfg["QBW"]
    cfg["SBW"] = min(512, cfg["Q"])
    cfg["NSBL"] = S // cfg["SBW"]
    cfg["NS"] = N // NSB
    cfg["NCK"] = NSB // 128
    return cfg


CFG_FULL = make_cfg(D=2048, H=16, G=4, S=2048, N=8192, NSB=1024, n_slots=50)
CFG_MINI = make_cfg(D=512, H=4, G=2, S=512, N=1024, NSB=512, n_slots=30)
CFG_MINI2 = make_cfg(D=512, H=4, G=2, S=2048, N=1024, NSB=512, n_slots=34)



class Buf:
    __slots__ = ("name", "last_w", "readers")

    def __init__(self, name):
        self.name = name
        self.last_w = None
        self.readers = []


class Emitter:
    ENGINES = ("pe", "sc", "dv", "g", "s")

    def __init__(self, nc):
        self.nc = nc
        self.streams = {e: [] for e in self.ENGINES}
        self.counts = {}
        self.sem_names = []
        self.hiwater = {}

    def _sem(self, name):
        if name not in self.counts:
            self.counts[name] = 0
            self.sem_names.append(name)
        return name

    def raw_wait(self, eng, sem, val):
        if val <= 0:
            return
        assert sem in self.counts and val <= self.counts[sem], (
            f"wait on future: {sem} {val} > {self.counts.get(sem)}")
        if self.hiwater.get((eng, sem), 0) >= val:
            return
        self.hiwater[(eng, sem)] = val
        self.streams[eng].append(("wait", sem, val))

    def _collect(self, need, bufs, war):
        for b in bufs:
            if b.last_w is not None:
                s, v = b.last_w
                if need.get(s, 0) < v:
                    need[s] = v
            if war:
                for s, v in b.readers:
                    if need.get(s, 0) < v:
                        need[s] = v

    def _emit_waits(self, eng, need):
        for s, v in need.items():
            self.raw_wait(eng, s, v)

    def op(self, eng, emit, reads=(), writes=(), sem=None, amt=1):
        need = {}
        self._collect(need, reads, war=False)
        self._collect(need, writes, war=True)
        self._emit_waits(eng, need)
        if sem is None:
            sem = "e_" + eng
        self._sem(sem)
        self.counts[sem] += amt
        tok = (sem, self.counts[sem])
        self.streams[eng].append(("op", emit, sem, amt))
        for b in reads:
            b.readers.append(tok)
        for b in writes:
            b.last_w = tok
            b.readers = []
        return tok

    def dma_multi(self, eng, emits, buf, sem):
        need = {}
        self._collect(need, [buf], war=True)
        self._emit_waits(eng, need)
        self._sem(sem)
        for e in emits:
            self.counts[sem] += 16
            self.streams[eng].append(("op", e, sem, 16))
        buf.last_w = (sem, self.counts[sem])
        buf.readers = []

    def group(self, eng, writes, sem=None, amt=1):
        return _Group(self, eng, list(writes), sem, amt)

    def replay(self):
        nc = self.nc
        eng_obj = {"pe": nc.tensor, "sc": nc.scalar, "dv": nc.vector,
                   "g": nc.gpsimd, "s": nc.sync}
        blk_name = {"pe": "tensor", "sc": "scalar", "dv": "vector",
                    "g": "gpsimd", "s": "sync"}
        with contextlib.ExitStack() as stack:
            handles = {}
            for name in self.sem_names:
                handles[name] = stack.enter_context(nc.semaphore(name))
            block = stack.enter_context(nc.Block())

            def make_body(eng):
                def body(_e):
                    e = eng_obj[eng]
                    for item in self.streams[eng]:
                        if item[0] == "wait":
                            e.wait_ge(handles[item[1]], item[2])
                        else:
                            _, emit, sem, amt = item
                            inst = emit()
                            if sem is not None:
                                inst.then_inc(handles[sem], amt)
                return body

            for eng in self.ENGINES:
                getattr(block, blk_name[eng])(make_body(eng))


class _Group:

    def __init__(self, em, eng, writes, sem, amt):
        self.em = em
        self.eng = eng
        self.writes = writes
        self.sem = sem if sem is not None else "e_" + eng
        self.amt = amt
        self.reads = []
        self.first = True

    def add(self, emit, reads=(), last=False):
        em = self.em
        need = {}
        em._collect(need, reads, war=False)
        if self.first:
            em._collect(need, self.writes, war=True)
            self.first = False
        em._emit_waits(self.eng, need)
        em._sem(self.sem)
        em.counts[self.sem] += self.amt
        tok = (self.sem, em.counts[self.sem])
        em.streams[self.eng].append(("op", emit, self.sem, self.amt))
        for b in reads:
            b.readers.append(tok)
        if last:
            for b in self.writes:
                b.last_w = tok
                b.readers = []
        return tok


class Pool:

    def __init__(self, nc, n):
        self.slots = []
        self.free = []
        for i in range(n):
            t = nc.alloc_sbuf_tensor(f"slot{i}", [128, 2048], BF16)
            self.slots.append(dict(bf=t, f32=t.bitcast(F32),
                                   buf=Buf(f"slot{i}"), i=i))
            self.free.append(i)

    def take(self):
        assert self.free, "pool exhausted"
        return self.slots[self.free.pop(0)]

    def give(self, slot):
        self.free.append(slot["i"])



def build_program(cfg, dbg=False):
    D, H, G, S, N = cfg["D"], cfg["H"], cfg["G"], cfg["S"], cfg["N"]
    Q, DC, KC = cfg["Q"], cfg["DC"], cfg["KC"]
    QBW, NQB = cfg["QBW"], cfg["NQB"]
    SBW, NSBL = cfg["SBW"], cfg["NSBL"]
    NSB, NS, NCK = cfg["NSB"], cfg["NS"], cfg["NCK"]
    GW = G * 128

    nc = bass.Bass("TRN2", target_bir_lowering=False, debug=False)
    em = Emitter(nc)

    xT = nc.dram_tensor("xT", [D, S], F32, kind="ExternalInput").ap()
    wq = nc.dram_tensor("wq", [H, 128, DC, 128], F32, kind="ExternalInput").ap()
    wk = nc.dram_tensor("wk", [G, 128, DC, 128], F32, kind="ExternalInput").ap()
    wv = nc.dram_tensor("wv", [D, GW], F32, kind="ExternalInput").ap()
    wo = nc.dram_tensor("wo", [DC, 128, DC, 128], F32, kind="ExternalInput").ap()
    w0 = nc.dram_tensor("w0", [D, N], F32, kind="ExternalInput").ap()
    w1 = nc.dram_tensor("w1", [N, D], F32, kind="ExternalInput").ap()
    b0f = nc.dram_tensor("b0f", [128, N // 128], F32, kind="ExternalInput").ap()
    b1f = nc.dram_tensor("b1f", [128, DC], F32, kind="ExternalInput").ap()
    ctabd = nc.dram_tensor("ctab", [128, S], BF16, kind="ExternalInput").ap()
    stabd = nc.dram_tensor("stab", [128, S], BF16, kind="ExternalInput").ap()
    swmd = nc.dram_tensor("swm", [128, 128], BF16, kind="ExternalInput").ap()
    outT = nc.dram_tensor("outT", [D, Q], F32, kind="ExternalOutput").ap()

    PT = [nc.alloc_psum_tensor(f"ps{i}", [128, 1024], F32) for i in range(4)]
    PS = [Buf(f"ps{i}_{h}") for i in range(4) for h in range(2)]

    def ps_ap(i, lo, hi):
        return PT[i // 2][:, (i % 2) * 512 + lo:(i % 2) * 512 + hi]

    swap_sb = nc.alloc_sbuf_tensor("swap_sb", [128, 128], BF16)
    b0fm = nc.alloc_sbuf_tensor("b0fm", [128, N // 128], F32)
    b1fm = nc.alloc_sbuf_tensor("b1fm", [128, DC], F32)
    ones_col = nc.alloc_sbuf_tensor("ones_col", [128, 1], BF16)
    ones_rb = nc.alloc_sbuf_tensor("ones_rb", [1, 128], BF16)
    ones_rf = nc.alloc_sbuf_tensor("ones_rf", [1, 128], F32)
    eps_r = nc.alloc_sbuf_tensor("eps_r", [1, 1], F32)
    bswap, bb0, bb1 = Buf("swap"), Buf("b0fm"), Buf("b1fm")
    bones_c, bones_rb, bones_rf = Buf("onc"), Buf("onrb"), Buf("onrf")
    beps = Buf("eps")

    pool = Pool(nc, cfg["n_slots"])

    rowA = pool.take()
    rowB = pool.take()
    bA, bB = Buf("rowA"), Buf("rowB")

    T = nc.tensor
    SC = nc.scalar
    DV = nc.vector
    GP = nc.gpsimd
    SY = nc.sync

    dbg_n = [0]

    def dump(name, ap_fn, buf, shape, dtype=BF16):
        if not dbg:
            return
        t = nc.dram_tensor("dbg_" + name, shape, dtype, kind="ExternalOutput").ap()
        em.op("s", (lambda t=t, ap_fn=ap_fn: SY.dma_start(out=t, in_=ap_fn())),
              reads=[buf], sem=f"d_dbg{dbg_n[0]}", amt=16)
        dbg_n[0] += 1

    em.op("dv", lambda: DV.memset(ones_col.ap(), 1.0), writes=[bones_c])
    em.op("dv", lambda: DV.memset(ones_rb.ap(), 1.0), writes=[bones_rb])
    em.op("dv", lambda: DV.memset(ones_rf.ap(), 1.0), writes=[bones_rf])
    em.op("dv", lambda: DV.memset(eps_r.ap(), EPS), writes=[beps])

    em.dma_multi("s", [lambda: SY.dma_start(out=swap_sb.ap(), in_=swmd)], bswap, "d_sw")
    em.dma_multi("s", [lambda: SY.dma_start(out=b0fm.ap(), in_=b0f)], bb0, "d_b0")
    em.dma_multi("s", [lambda: SY.dma_start(out=b1fm.ap(), in_=b1f)], bb1, "d_b1")

    ct_s = pool.take()
    st_s = pool.take()
    em.dma_multi("s", [lambda: SY.dma_start(out=ct_s["bf"][:, 0:S], in_=ctabd)],
                 ct_s["buf"], "d_ct")
    em.dma_multi("s", [lambda: SY.dma_start(out=st_s["bf"][:, 0:S], in_=stabd)],
                 st_s["buf"], "d_st")

    xq_s = [pool.take() for _ in range((DC + 1) // 2)]
    xo_s = [pool.take() for _ in range((DC + 1) // 2)]

    def xq_ap(dc, lo, hi):
        return xq_s[dc // 2]["bf"][:, (dc % 2) * Q + lo:(dc % 2) * Q + hi]

    def xo_ap(dc, lo, hi):
        return xo_s[dc // 2]["bf"][:, (dc % 2) * Q + lo:(dc % 2) * Q + hi]

    def xs_ap(dc, lo, hi):
        if lo >= Q:
            return xo_ap(dc, lo - Q, hi - Q)
        return xq_ap(dc, lo, hi)

    def xs_buf(dc, lo):
        return (xo_s if lo >= Q else xq_s)[dc // 2]["buf"]

    for i in range((DC + 1) // 2):
        emits_q, emits_o = [], []
        for dc in (2 * i, 2 * i + 1):
            if dc < DC:
                emits_q.append(lambda dc=dc: GP.dma_start(
                    out=xq_ap(dc, 0, Q), in_=xT[dc * 128:(dc + 1) * 128, 0:Q]))
                emits_o.append(lambda dc=dc: GP.dma_start(
                    out=xo_ap(dc, 0, Q), in_=xT[dc * 128:(dc + 1) * 128, Q:S]))
        em.dma_multi("g", emits_q, xq_s[i]["buf"], f"d_xq{i}")
        em.dma_multi("g", emits_o, xo_s[i]["buf"], f"d_xo{i}")

    nvs = max(1, (DC + 3) // 4)
    wv_s = [pool.take() for _ in range(nvs)]
    for i in range(nvs):
        emits = []
        for dc in range(i * 4, min(DC, i * 4 + 4)):
            emits.append(lambda dc=dc, i=i: GP.dma_start(
                out=wv_s[i]["bf"][:, (dc % 4) * GW:(dc % 4 + 1) * GW],
                in_=wv[dc * 128:(dc + 1) * 128, :]))
        em.dma_multi("g", emits, wv_s[i]["buf"], f"d_wv{i}")

    def wv_ap(dc):
        return wv_s[dc // 4]["bf"][:, (dc % 4) * GW:(dc % 4 + 1) * GW]

    wk_s = [pool.take() for _ in range(G)]
    for g in range(G):
        em.dma_multi("g", [lambda g=g: GP.dma_start(
            out=wk_s[g]["bf"][:, 0:DC * 128].rearrange("p (dc c) -> p dc c", c=128),
            in_=wk[g])],
            wk_s[g]["buf"], f"d_wk{g}")

    wq_s = [pool.take(), pool.take()]

    def load_wq(h):
        sl = wq_s[h % 2]
        em.dma_multi("g", [lambda h=h, sl=sl: GP.dma_start(
            out=sl["bf"][:, 0:DC * 128].rearrange("p (dc c) -> p dc c", c=128),
            in_=wq[h])],
            sl["buf"], f"d_wq{h % 2}")

    load_wq(0)
    load_wq(1)

    scr1 = pool.take()
    scr2 = pool.take()
    scr1_h = [Buf("scr1a"), Buf("scr1b")]
    scr2_h = [Buf("scr2a"), Buf("scr2b")]

    vt_s = [pool.take() for _ in range(max(1, (KC + 3) // 4))]

    def v_ap(kc, g):
        base = (kc % 4) * GW
        return vt_s[kc // 4]["bf"][:, base + g * 128: base + (g + 1) * 128]

    for kc in range(KC):
        pb = kc % 2
        grp = em.group("pe", writes=[PS[pb]])
        for dc in range(DC):
            grp.add((lambda kc=kc, dc=dc, pb=pb: T.matmul(
                ps_ap(pb, 0, GW),
                xs_ap(dc, kc * 128, (kc + 1) * 128),
                wv_ap(dc),
                start=(dc == 0), stop=(dc == DC - 1))),
                reads=[xs_buf(dc, kc * 128), wv_s[dc // 4]["buf"]],
                last=(dc == DC - 1))
        sl = vt_s[kc // 4]
        em.op("sc", (lambda kc=kc, pb=pb, sl=sl: SC.activation(
            sl["bf"][:, (kc % 4) * GW:(kc % 4 + 1) * GW],
            ps_ap(pb, 0, GW), AF.Copy)),
            reads=[PS[pb]], writes=[sl["buf"]])

    dump("v0", lambda: vt_s[0]["bf"][:, 0:GW], vt_s[0]["buf"], [128, GW])

    kr_s = [pool.take() for _ in range(G)]

    NSBP = (NSBL + 1) // 2
    for g in range(G):
        for sbp in range(NSBP):
            sbs = [s for s in (2 * sbp, 2 * sbp + 1) if s < NSBL]
            grp = em.group("pe", writes=[PS[2 + (s % 2)] for s in sbs])
            for dc in range(DC):
                for sb in sbs:
                    grp.add((lambda g=g, sb=sb, dc=dc: T.matmul(
                        ps_ap(2 + (sb % 2), 0, SBW),
                        wk_s[g]["bf"][:, dc * 128:(dc + 1) * 128],
                        xs_ap(dc, sb * SBW, (sb + 1) * SBW),
                        start=(dc == 0), stop=(dc == DC - 1))),
                        reads=[wk_s[g]["buf"], xs_buf(dc, sb * SBW)],
                        last=(dc == DC - 1 and sb == sbs[-1]))
            for sb in sbs:
                pa = sb % 2
                pw = 6 + pa
                klo = pa * 1024
                pb = 2 + pa
                em.op("sc", (lambda pb=pb, klo=klo: SC.activation(
                    scr1["bf"][:, klo:klo + SBW], ps_ap(pb, 0, SBW), AF.Copy)),
                    reads=[PS[pb]], writes=[scr1_h[pa]])
                em.op("pe", (lambda pw=pw, klo=klo: T.matmul(
                    ps_ap(pw, 0, SBW), swap_sb.ap(), scr1["bf"][:, klo:klo + SBW],
                    start=True, stop=True)),
                    reads=[bswap, scr1_h[pa]], writes=[PS[pw]])
                em.op("dv", (lambda g=g, sb=sb, klo=klo: DV.tensor_mul(
                    kr_s[g]["bf"][:, sb * SBW:(sb + 1) * SBW],
                    scr1["bf"][:, klo:klo + SBW],
                    ct_s["bf"][:, sb * SBW:(sb + 1) * SBW])),
                    reads=[scr1_h[pa], ct_s["buf"]], writes=[kr_s[g]["buf"]])
                em.op("dv", (lambda sb=sb, pw=pw, klo=klo: DV.tensor_mul(
                    scr2["bf"][:, klo:klo + SBW], ps_ap(pw, 0, SBW),
                    st_s["bf"][:, sb * SBW:(sb + 1) * SBW])),
                    reads=[PS[pw], st_s["buf"]], writes=[scr2_h[pa]])
                em.op("dv", (lambda g=g, sb=sb, klo=klo: DV.tensor_add(
                    kr_s[g]["bf"][:, sb * SBW:(sb + 1) * SBW],
                    kr_s[g]["bf"][:, sb * SBW:(sb + 1) * SBW],
                    scr2["bf"][:, klo:klo + SBW])),
                    reads=[kr_s[g]["buf"], scr2_h[pa]], writes=[kr_s[g]["buf"]])

    dump("kr0", lambda: kr_s[0]["bf"][:, 0:S], kr_s[0]["buf"], [128, S])

    for sl in wv_s + wk_s + xo_s:
        pool.give(sl)

    qr_s = [pool.take(), pool.take()]
    qr_bufs = [Buf(f"qr{i}") for i in range(4)]

    def qr_ap(h, lo, hi):
        return qr_s[(h // 2) % 2]["bf"][:, (h % 2) * 1024 + lo:(h % 2) * 1024 + hi]

    def qr_buf(h):
        return qr_bufs[h % 4]

    pt_s = pool.take()
    pt_h = [Buf("ptA"), Buf("ptB")]
    den_s = [pool.take(), pool.take()]
    den_pl = [pool.take(), pool.take()]
    attn_s = [pool.take() for _ in range((H + 1) // 2)]
    wo_s = [pool.take(), pool.take()]
    xm_s = [pool.take(), pool.take()]
    if KC >= 8:
        POOLSET = frozenset(kc for kc in range(KC) if kc % 8 < 3)
    else:
        POOLSET = frozenset([KC - 1])

    def attn_ap(h, lo, hi):
        return attn_s[h // 2]["bf"][:, (h % 2) * 1024 + lo:(h % 2) * 1024 + hi]

    def load_wo(d):
        sl = wo_s[d % 2]
        em.dma_multi("g", [lambda d=d, sl=sl: GP.dma_start(
            out=sl["bf"][:, 0:DC * 128].rearrange("p (dc c) -> p dc c", c=128),
            in_=wo[d])],
            sl["buf"], f"d_wo{d % 2}")

    def load_xm(d):
        sl = xm_s[d % 2]
        em.dma_multi("s", [lambda d=d, sl=sl: SY.dma_start(
            out=sl["f32"][:, 0:Q], in_=xT[d * 128:(d + 1) * 128, 0:Q])],
            sl["buf"], f"d_xm{d % 2}")

    def qproj_rope(h, lo_bank):
        pt_i = lo_bank // 2
        grp = em.group("pe", writes=[PS[lo_bank], PS[lo_bank + 1]])
        for dc in range(DC):
            for qb in range(NQB):
                grp.add((lambda h=h, qb=qb, dc=dc, pt_i=pt_i: T.matmul(
                    PT[pt_i][:, qb * 512:qb * 512 + QBW],
                    wq_s[h % 2]["bf"][:, dc * 128:(dc + 1) * 128],
                    xq_ap(dc, qb * QBW, (qb + 1) * QBW),
                    start=(dc == 0), stop=(dc == DC - 1))),
                    reads=[wq_s[h % 2]["buf"], xq_s[dc // 2]["buf"]],
                    last=(dc == DC - 1 and qb == NQB - 1))
        for qb in range(NQB):
            pa = qb % 2
            qlo = pa * 1024
            em.op("sc", (lambda qb=qb, qlo=qlo, pt_i=pt_i: SC.activation(
                scr1["bf"][:, qlo:qlo + QBW],
                PT[pt_i][:, qb * 512:qb * 512 + QBW], AF.Copy)),
                reads=[PS[lo_bank + qb]], writes=[scr1_h[pa]])
            em.op("pe", (lambda qb=qb, qlo=qlo, pt_i=pt_i: T.matmul(
                PT[pt_i][:, qb * 512:qb * 512 + QBW], swap_sb.ap(),
                scr1["bf"][:, qlo:qlo + QBW],
                start=True, stop=True)),
                reads=[bswap, scr1_h[pa]], writes=[PS[lo_bank + qb]])
            em.op("dv", (lambda h=h, qb=qb, qlo=qlo: DV.tensor_mul(
                qr_ap(h, qb * QBW, (qb + 1) * QBW),
                scr1["bf"][:, qlo:qlo + QBW],
                ct_s["bf"][:, qb * QBW:(qb + 1) * QBW])),
                reads=[scr1_h[pa], ct_s["buf"]], writes=[qr_buf(h)])
            em.op("dv", (lambda qb=qb, qlo=qlo, pt_i=pt_i: DV.tensor_mul(
                scr2["bf"][:, qlo:qlo + QBW],
                PT[pt_i][:, qb * 512:qb * 512 + QBW],
                st_s["bf"][:, qb * QBW:(qb + 1) * QBW])),
                reads=[PS[lo_bank + qb], st_s["buf"]], writes=[scr2_h[pa]])
            em.op("dv", (lambda h=h, qb=qb, qlo=qlo: DV.tensor_add(
                qr_ap(h, qb * QBW, (qb + 1) * QBW),
                qr_ap(h, qb * QBW, (qb + 1) * QBW),
                scr2["bf"][:, qlo:qlo + QBW])),
                reads=[qr_buf(h), scr2_h[pa]],
                writes=[qr_buf(h)])

    def st_emit(h, kc):
        g = h % G
        sb = kc % 2
        grp = em.group("pe", writes=[PS[2 * sb], PS[2 * sb + 1]])
        for qb in range(NQB):
            grp.add((lambda h=h, g=g, kc=kc, qb=qb, sb=sb: T.matmul(
                PT[sb][:, qb * QBW:(qb + 1) * QBW],
                kr_s[g]["bf"][:, kc * 128:(kc + 1) * 128],
                qr_ap(h, qb * QBW, (qb + 1) * QBW),
                start=True, stop=True)),
                reads=[kr_s[g]["buf"], qr_buf(h)],
                last=(qb == NQB - 1))

    def tail_stage(h, stage):
        par = h % 2
        av_lo = 4 + 2 * par
        dlo = par * 1024
        if stage == 0:
            em.op("dv", (lambda h=h, par=par: DV.tensor_copy(
                attn_ap(h, 0, Q), PT[2 + par][:, 0:Q])),
                reads=[PS[av_lo], PS[av_lo + 1]], writes=[attn_s[h // 2]["buf"]])
            em.op("dv", (lambda dlo=dlo, par=par: DV.tensor_add(
                scr1["bf"][:, dlo:dlo + Q], den_s[par]["f32"][:, 0:Q],
                den_pl[par]["f32"][:, 0:Q])),
                reads=[den_s[par]["buf"], den_pl[par]["buf"]],
                writes=[scr1_h[par]])
        elif stage == 1:
            dgrp = em.group("pe", writes=[PS[av_lo], PS[av_lo + 1]])
            for qb in range(NQB):
                dgrp.add((lambda qb=qb, dlo=dlo, par=par: T.matmul(
                    PT[2 + par][0:1, qb * 512:qb * 512 + QBW], ones_col.ap(),
                    scr1["bf"][:, dlo + qb * QBW:dlo + (qb + 1) * QBW],
                    start=True, stop=True)),
                    reads=[bones_c, scr1_h[par]], last=(qb == NQB - 1))
        elif stage == 2:
            em.op("sc", (lambda par=par: SC.activation(
                rowB["f32"][0:1, 0:Q], PT[2 + par][0:1, 0:Q], AF.Ln)),
                reads=[PS[av_lo], PS[av_lo + 1]], writes=[bB])
        elif stage == 3:
            em.op("sc", (lambda: SC.activation(
                rowA["bf"][0:1, 0:Q], rowB["f32"][0:1, 0:Q], AF.Exp, scale=-1.0)),
                reads=[bB], writes=[bA])
        elif stage == 4:
            bgrp = em.group("pe", writes=[PS[av_lo], PS[av_lo + 1]])
            for qb in range(NQB):
                bgrp.add((lambda qb=qb, par=par: T.matmul(
                    PT[2 + par][:, qb * QBW:(qb + 1) * QBW], ones_rb.ap(),
                    rowA["bf"][0:1, qb * QBW:(qb + 1) * QBW],
                    start=True, stop=True)),
                    reads=[bones_rb, bA], last=(qb == NQB - 1))
        else:
            em.op("dv", (lambda h=h, par=par: DV.tensor_mul(
                attn_ap(h, 0, Q), attn_ap(h, 0, Q), PT[2 + par][:, 0:Q])),
                reads=[attn_s[h // 2]["buf"], PS[av_lo], PS[av_lo + 1]],
                writes=[attn_s[h // 2]["buf"]])

    qproj_rope(0, 4)
    if H > 1:
        qproj_rope(1, 6)
    if H > 2:
        load_wq(2)
    if H > 3:
        load_wq(3)

    TAIL_AT = (1, 2, 3, 4, 5, 6)
    QPROJ_AT = tuple(range(7, 13))
    for h in range(H):
        g = h % G
        par = h % 2
        av_lo = 4 + 2 * par
        other_lo = 4 + 2 * (1 - par)
        st_emit(h, 0)
        avgrp = None
        dv_first = True
        em.op("g", (lambda par=par: GP.memset(den_pl[par]["f32"][:, 0:Q], 0.0)),
              writes=[den_pl[par]["buf"]])
        qgrp = [None]
        qh = h + 2
        in_loop_tails = sum(1 for x in TAIL_AT if x < KC)
        for kc in range(KC):
            if kc + 1 < KC:
                st_emit(h, kc + 1)
            sb = kc % 2
            ptp = kc % 2
            ptlo = ptp * 1024
            em.op("sc", (lambda ptlo=ptlo, sb=sb: SC.activation(
                pt_s["bf"][:, ptlo:ptlo + Q], PT[sb][:, 0:Q], AF.Exp, scale=SCALE)),
                reads=[PS[2 * sb], PS[2 * sb + 1]], writes=[pt_h[ptp]])
            if kc in POOLSET:
                acc, bacc = den_pl[par], den_pl[par]["buf"]
                em.op("g", (lambda acc=acc, ptlo=ptlo: GP.tensor_add(
                    acc["f32"][:, 0:Q], acc["f32"][:, 0:Q],
                    pt_s["bf"][:, ptlo:ptlo + Q])),
                    reads=[pt_h[ptp], bacc], writes=[bacc])
            else:
                acc, bacc = den_s[par], den_s[par]["buf"]
                if dv_first:
                    em.op("dv", (lambda acc=acc, ptlo=ptlo: DV.tensor_copy(
                        acc["f32"][:, 0:Q], pt_s["bf"][:, ptlo:ptlo + Q])),
                        reads=[pt_h[ptp]], writes=[bacc])
                    dv_first = False
                else:
                    em.op("dv", (lambda acc=acc, ptlo=ptlo: DV.tensor_add(
                        acc["f32"][:, 0:Q], acc["f32"][:, 0:Q],
                        pt_s["bf"][:, ptlo:ptlo + Q])),
                        reads=[pt_h[ptp], bacc], writes=[bacc])
            if kc == 0:
                avgrp = em.group("pe", writes=[PS[av_lo], PS[av_lo + 1]])
            if qh < H and KC > QPROJ_AT[0]:
                if kc == QPROJ_AT[0]:
                    qgrp[0] = em.group("pe", writes=[PS[other_lo], PS[other_lo + 1]])
                if qgrp[0] is not None and kc >= QPROJ_AT[0]:
                    n_mem = DC * NQB
                    n_slots_q = KC - QPROJ_AT[0]
                    per = (n_mem + n_slots_q - 1) // n_slots_q
                    i0 = (kc - QPROJ_AT[0]) * per
                    for i in range(i0, min(i0 + per, n_mem)):
                        dc, qb = divmod(i, NQB)
                        qgrp[0].add((lambda qh=qh, qb=qb, dc=dc, other_lo=other_lo: T.matmul(
                            PT[other_lo // 2][:, qb * 512:qb * 512 + QBW],
                            wq_s[qh % 2]["bf"][:, dc * 128:(dc + 1) * 128],
                            xq_ap(dc, qb * QBW, (qb + 1) * QBW),
                            start=(dc == 0), stop=(dc == DC - 1))),
                            reads=[wq_s[qh % 2]["buf"], xq_s[dc // 2]["buf"]],
                            last=(i == n_mem - 1))
            for qb in range(NQB):
                avgrp.add((lambda kc=kc, g=g, qb=qb, par=par, ptlo=ptlo: T.matmul(
                    PT[2 + par][:, qb * QBW:(qb + 1) * QBW],
                    v_ap(kc, g),
                    pt_s["bf"][:, ptlo + qb * QBW:ptlo + (qb + 1) * QBW],
                    start=(kc == 0), stop=(kc == KC - 1))),
                    reads=[vt_s[kc // 4]["buf"], pt_h[ptp]],
                    last=(kc == KC - 1 and qb == NQB - 1))
            if h > 0 and kc in TAIL_AT:
                tail_stage(h - 1, TAIL_AT.index(kc))
        if h > 0:
            for stage in range(in_loop_tails, 6):
                tail_stage(h - 1, stage)
        if qh < H:
            if KC <= QPROJ_AT[0]:
                qproj_rope(qh, other_lo)
            else:
                for qb in range(NQB):
                    pa = qb % 2
                    qlo = pa * 1024
                    pt_i = other_lo // 2
                    em.op("sc", (lambda qb=qb, qlo=qlo, pt_i=pt_i: SC.activation(
                        scr2["bf"][:, qlo:qlo + QBW],
                        PT[pt_i][:, qb * 512:qb * 512 + QBW], AF.Copy)),
                        reads=[PS[other_lo + qb]], writes=[scr2_h[pa]])
                    em.op("pe", (lambda qb=qb, qlo=qlo, pt_i=pt_i: T.matmul(
                        PT[pt_i][:, qb * 512:qb * 512 + QBW], swap_sb.ap(),
                        scr2["bf"][:, qlo:qlo + QBW],
                        start=True, stop=True)),
                        reads=[bswap, scr2_h[pa]], writes=[PS[other_lo + qb]])
                    em.op("dv", (lambda qh=qh, qb=qb, qlo=qlo: DV.tensor_mul(
                        qr_ap(qh, qb * QBW, (qb + 1) * QBW),
                        scr2["bf"][:, qlo:qlo + QBW],
                        ct_s["bf"][:, qb * QBW:(qb + 1) * QBW])),
                        reads=[scr2_h[pa], ct_s["buf"]],
                        writes=[qr_buf(qh)])
                    em.op("dv", (lambda qb=qb, qlo=qlo, pt_i=pt_i: DV.tensor_mul(
                        scr2["bf"][:, qlo:qlo + QBW],
                        PT[pt_i][:, qb * 512:qb * 512 + QBW],
                        st_s["bf"][:, qb * QBW:(qb + 1) * QBW])),
                        reads=[PS[other_lo + qb], st_s["buf"]], writes=[scr2_h[pa]])
                    em.op("dv", (lambda qh=qh, qb=qb, qlo=qlo: DV.tensor_add(
                        qr_ap(qh, qb * QBW, (qb + 1) * QBW),
                        qr_ap(qh, qb * QBW, (qb + 1) * QBW),
                        scr2["bf"][:, qlo:qlo + QBW])),
                        reads=[qr_buf(qh), scr2_h[pa]],
                        writes=[qr_buf(qh)])
            if qh + 2 < H:
                load_wq(qh + 2)
        if h == 0:
            load_wo(0)
            load_xm(0)
        if h == 1:
            load_wo(1)
            load_xm(1)
    for stage in range(6):
        tail_stage(H - 1, stage)

    dump("attn0", lambda: attn_ap(0, 0, Q), attn_s[0]["buf"], [128, Q])

    for b in pt_h:
        if b.last_w is not None:
            pt_s["buf"].readers.append(b.last_w)
        pt_s["buf"].readers.extend(b.readers)
    for i, b in enumerate(qr_bufs):
        sl = qr_s[i // 2]
        if b.last_w is not None:
            sl["buf"].readers.append(b.last_w)
        sl["buf"].readers.extend(b.readers)

    for sl in kr_s + vt_s + qr_s + [pt_s] + den_s + den_pl + xq_s + [ct_s, st_s] + wq_s:
        pool.give(sl)

    racc = [pool.take() for _ in range(DC)]
    w0_s = [pool.take() for _ in range((DC + 1) // 2)]
    ab_s = [pool.take(), pool.take()]
    bab = [Buf("ab0"), Buf("ab1")]
    x1b_s = []
    w1_s = []
    dg_s = []
    x1b_bufs = [Buf(f"x1b{dc}") for dc in range(DC)]

    def x1b_ap(dc, lo, hi):
        return x1b_s[dc // 2]["bf"][:, (dc % 2) * 1024 + lo:(dc % 2) * 1024 + hi]

    def w0_ap(dc, lo, hi):
        return w0_s[dc // 2]["bf"][:, (dc % 2) * NSB + lo:(dc % 2) * NSB + hi]

    def dg_ap(ncl, lo, hi):
        return dg_s[ncl // 2]["bf"][:, (ncl % 2) * 1024 + lo:(ncl % 2) * 1024 + hi]

    def load_w0(ns):
        for i in range((DC + 1) // 2):
            emits = []
            for dc in (2 * i, 2 * i + 1):
                if dc < DC:
                    emits.append(lambda ns=ns, dc=dc: GP.dma_start(
                        out=w0_ap(dc, 0, NSB),
                        in_=w0[dc * 128:(dc + 1) * 128, ns * NSB:(ns + 1) * NSB]))
            em.dma_multi("g", emits, w0_s[i]["buf"], f"d_w0_{i}")

    def load_w1(ns):
        for ncl in range(NCK):
            row = ns * NSB + ncl * 128
            em.dma_multi("g", [lambda row=row, ncl=ncl: GP.dma_start(
                out=w1_s[ncl]["bf"][:, 0:D], in_=w1[row:row + 128, :])],
                w1_s[ncl]["buf"], f"d_w1_{ncl}")

    load_w0(0)

    stat_groups = []
    for qb in range(NQB):
        stat_groups.append(em.group("pe", writes=[PS[4 + qb]]))
    for qb in range(NQB):
        stat_groups.append(em.group("pe", writes=[PS[6 + qb]]))

    def stats_add(stat_groups, dc, src_scr, src_bufs, which, last):
        for qb in range(NQB):
            stat_groups[which * NQB + qb].add(
                (lambda dc=dc, qb=qb, which=which, src_scr=src_scr: T.matmul(
                    PT[2 + which][0:1, qb * 512:qb * 512 + QBW], ones_col.ap(),
                    src_scr["bf"][:, (dc % 2) * 1024 + qb * QBW:
                                  (dc % 2) * 1024 + (qb + 1) * QBW],
                    start=(dc == 0), stop=last)),
                reads=[src_bufs[dc % 2], bones_c], last=last)

    for d in range(DC):
        par = d % 2
        st_halves = [PS[2 * par], PS[2 * par + 1]]
        grp = em.group("pe", writes=st_halves)
        for dvc in range(DC):
            for qb in range(NQB):
                grp.add((lambda d=d, qb=qb, dvc=dvc, par=par: T.matmul(
                    PT[par][:, qb * QBW:(qb + 1) * QBW],
                    wo_s[d % 2]["bf"][:, dvc * 128:(dvc + 1) * 128],
                    attn_ap(dvc, qb * QBW, (qb + 1) * QBW),
                    start=(dvc == 0), stop=(dvc == DC - 1))),
                    reads=[wo_s[d % 2]["buf"], attn_s[dvc // 2]["buf"]],
                    last=(qb == NQB - 1 and dvc == DC - 1))
        em.op("dv", (lambda d=d, par=par: DV.tensor_add(
            racc[d]["f32"][:, 0:Q], PT[par][:, 0:Q], xm_s[d % 2]["f32"][:, 0:Q])),
            reads=st_halves + [xm_s[d % 2]["buf"]], writes=[racc[d]["buf"]])
        if d == 0:
            dump("r1_0", lambda: racc[0]["f32"][:, 0:Q], racc[0]["buf"],
                 [128, Q], F32)
        if d + 2 < DC:
            load_wo(d + 2)
            load_xm(d + 2)
        em.op("sc", (lambda d=d, par=par: SC.activation(
            scr1["bf"][:, par * 1024:par * 1024 + Q], racc[d]["f32"][:, 0:Q], AF.Copy)),
            reads=[racc[d]["buf"]], writes=[scr1_h[par]])
        em.op("sc", (lambda par=par: SC.activation(
            scr2["bf"][:, par * 1024:par * 1024 + Q],
            scr1["bf"][:, par * 1024:par * 1024 + Q], AF.Square)),
            reads=[scr1_h[par]], writes=[scr2_h[par]])
        if d > 0:
            stats_add(stat_groups, d - 1, scr1, scr1_h, 0, last=False)
            stats_add(stat_groups, d - 1, scr2, scr2_h, 1, last=False)
    stats_add(stat_groups, DC - 1, scr1, scr1_h, 0, last=True)
    stats_add(stat_groups, DC - 1, scr2, scr2_h, 1, last=True)

    pool.give(wo_s[0])
    pool.give(wo_s[1])
    pool.give(xm_s[0])
    pool.give(xm_s[1])
    for sl in attn_s:
        pool.give(sl)
    x1b_s.extend(pool.take() for _ in range((DC + 1) // 2))
    w1_s.extend(pool.take() for _ in range(NCK))
    dg_s.extend(pool.take() for _ in range((NCK + 1) // 2))

    def ln_finalize():
        em.op("sc", (lambda: SC.activation(rowA["f32"][0:1, 0:Q], PT[2][0:1, 0:Q],
                                           AF.Copy, scale=1.0 / D)),
              reads=[PS[4], PS[5]], writes=[bA])
        em.op("sc", (lambda: SC.activation(rowB["f32"][0:1, 0:Q], PT[3][0:1, 0:Q],
                                           AF.Copy, scale=1.0 / D)),
              reads=[PS[6], PS[7]], writes=[bB])
        em.op("sc", (lambda: SC.activation(PT[0][0:1, 0:Q], rowA["f32"][0:1, 0:Q],
                                           AF.Square)),
              reads=[bA], writes=[PS[0], PS[1]])
        em.op("dv", (lambda: DV.tensor_sub(PT[1][0:1, 0:Q], rowB["f32"][0:1, 0:Q],
                                           PT[0][0:1, 0:Q])),
              reads=[bB, PS[0], PS[1]], writes=[PS[2], PS[3]])
        em.op("sc", (lambda: SC.activation(PT[0][0:1, 0:Q], PT[1][0:1, 0:Q],
                                           AF.Ln, bias=eps_r[0:1, 0:1])),
              reads=[PS[2], PS[3], beps], writes=[PS[0], PS[1]])
        em.op("sc", (lambda: SC.activation(rowB["f32"][0:1, 0:Q], PT[0][0:1, 0:Q],
                                           AF.Exp, scale=-0.5)),
              reads=[PS[0], PS[1]], writes=[bB])
        em.op("dv", (lambda: DV.scalar_tensor_tensor(
            rowA["f32"][0:1, 0:Q], rowA["f32"][0:1, 0:Q], -1.0,
            rowB["f32"][0:1, 0:Q], ALU.mult, ALU.mult)),
            reads=[bA, bB], writes=[bA])
        agrp = em.group("pe", writes=[PS[0], PS[1]])
        for qb in range(NQB):
            agrp.add((lambda qb=qb: T.matmul(
                PT[0][:, qb * QBW:(qb + 1) * QBW], ones_rf.ap(),
                rowB["f32"][0:1, qb * QBW:(qb + 1) * QBW], start=True, stop=True)),
                reads=[bones_rf, bB], last=(qb == NQB - 1))
        bgrp = em.group("pe", writes=[PS[2], PS[3]])
        for qb in range(NQB):
            bgrp.add((lambda qb=qb: T.matmul(
                PT[1][:, qb * QBW:(qb + 1) * QBW], ones_rf.ap(),
                rowA["f32"][0:1, qb * QBW:(qb + 1) * QBW], start=True, stop=True)),
                reads=[bones_rf, bA], last=(qb == NQB - 1))
        em.op("sc", (lambda: SC.activation(ab_s[0]["bf"][:, 0:Q], PT[0][:, 0:Q],
                                           AF.Copy)),
              reads=[PS[0], PS[1]], writes=[bab[0]])
        em.op("sc", (lambda: SC.activation(ab_s[1]["bf"][:, 0:Q], PT[1][:, 0:Q],
                                           AF.Copy)),
              reads=[PS[2], PS[3]], writes=[bab[1]])

    ln_finalize()

    NCK0 = min(4, NCK)
    f1_groups = [em.group("pe", writes=[PS[2 * i], PS[2 * i + 1]])
                 for i in range(NCK0)]
    for dc in range(DC):
        em.op("dv", (lambda dc=dc: DV.tensor_mul(
            racc[dc]["f32"][:, 0:Q], racc[dc]["f32"][:, 0:Q],
            ab_s[0]["bf"][:, 0:Q])),
            reads=[racc[dc]["buf"], bab[0]], writes=[racc[dc]["buf"]])
        em.op("dv", (lambda dc=dc: DV.tensor_add(
            racc[dc]["f32"][:, 0:Q], racc[dc]["f32"][:, 0:Q],
            ab_s[1]["bf"][:, 0:Q])),
            reads=[racc[dc]["buf"], bab[1]], writes=[racc[dc]["buf"]])
        em.op("sc", (lambda dc=dc: SC.activation(
            x1b_ap(dc, 0, Q), racc[dc]["f32"][:, 0:Q], AF.Copy)),
            reads=[racc[dc]["buf"]], writes=[x1b_bufs[dc]])
        for ncl in range(NCK0):
            for qb in range(NQB):
                f1_groups[ncl].add((lambda ncl=ncl, qb=qb, dc=dc: T.matmul(
                    PT[ncl][:, qb * QBW:(qb + 1) * QBW],
                    w0_ap(dc, ncl * 128, (ncl + 1) * 128),
                    x1b_ap(dc, qb * QBW, (qb + 1) * QBW),
                    start=(dc == 0), stop=(dc == DC - 1))),
                    reads=[w0_s[dc // 2]["buf"], x1b_bufs[dc]],
                    last=(dc == DC - 1 and qb == NQB - 1))
    for ncl in range(NCK0):
        em.op("sc", (lambda ncl=ncl: SC.activation(
            dg_ap(ncl, 0, Q), PT[ncl][:, 0:Q], AF.Gelu,
            bias=b0fm[:, ncl:ncl + 1])),
            reads=[PS[2 * ncl], PS[2 * ncl + 1], bb0],
            writes=[dg_s[ncl // 2]["buf"]])

    dump("x1_0", lambda: racc[0]["f32"][:, 0:Q], racc[0]["buf"], [128, Q], F32)

    load_w1(0)

    for ns in range(NS):
        for ncl in range(NCK):
            if ns == 0 and ncl < NCK0:
                continue
            par = ncl % 2
            st_halves = [PS[2 * par], PS[2 * par + 1]]
            grp = em.group("pe", writes=st_halves)
            for dc in range(DC):
                for qb in range(NQB):
                    grp.add((lambda ncl=ncl, qb=qb, dc=dc, par=par: T.matmul(
                        PT[par][:, qb * QBW:(qb + 1) * QBW],
                        w0_ap(dc, ncl * 128, (ncl + 1) * 128),
                        x1b_ap(dc, qb * QBW, (qb + 1) * QBW),
                        start=(dc == 0), stop=(dc == DC - 1))),
                        reads=[w0_s[dc // 2]["buf"], x1b_bufs[dc]],
                        last=(qb == NQB - 1 and dc == DC - 1))
            col = ns * NCK + ncl
            em.op("sc", (lambda ncl=ncl, par=par, col=col: SC.activation(
                dg_ap(ncl, 0, Q), PT[par][:, 0:Q], AF.Gelu,
                bias=b0fm[:, col:col + 1])),
                reads=st_halves + [bb0], writes=[dg_s[ncl // 2]["buf"]])
            if ns == 0 and ncl == 0:
                dump("dg0", lambda: dg_ap(0, 0, Q), dg_s[0]["buf"], [128, Q])
        if ns + 1 < NS:
            load_w0(ns + 1)
        for dc2 in range(DC):
            par = dc2 % 2
            av_halves = [PS[4 + 2 * par], PS[5 + 2 * par]]
            grp = em.group("pe", writes=av_halves)
            for ncl in range(NCK):
                for qb in range(NQB):
                    grp.add((lambda dc2=dc2, qb=qb, ncl=ncl, par=par: T.matmul(
                        PT[2 + par][:, qb * QBW:(qb + 1) * QBW],
                        w1_s[ncl]["bf"][:, dc2 * 128:(dc2 + 1) * 128],
                        dg_ap(ncl, qb * QBW, (qb + 1) * QBW),
                        start=(ncl == 0), stop=(ncl == NCK - 1))),
                        reads=[w1_s[ncl]["buf"], dg_s[ncl // 2]["buf"]],
                        last=(qb == NQB - 1 and ncl == NCK - 1))
            if ns < NS - 1:
                em.op("dv", (lambda dc2=dc2, par=par: DV.tensor_add(
                    racc[dc2]["f32"][:, 0:Q], racc[dc2]["f32"][:, 0:Q],
                    PT[2 + par][:, 0:Q])),
                    reads=av_halves + [racc[dc2]["buf"]], writes=[racc[dc2]["buf"]])
            else:
                em.op("dv", (lambda dc2=dc2, par=par: DV.scalar_tensor_tensor(
                    racc[dc2]["f32"][:, 0:Q], PT[2 + par][:, 0:Q],
                    b1fm[:, dc2:dc2 + 1], racc[dc2]["f32"][:, 0:Q],
                    ALU.add, ALU.add)),
                    reads=av_halves + [racc[dc2]["buf"], bb1],
                    writes=[racc[dc2]["buf"]])
        if ns + 1 < NS:
            load_w1(ns + 1)

    stat_groups = []
    for qb in range(NQB):
        stat_groups.append(em.group("pe", writes=[PS[4 + qb]]))
    for qb in range(NQB):
        stat_groups.append(em.group("pe", writes=[PS[6 + qb]]))
    for dc in range(DC):
        par = dc % 2
        em.op("sc", (lambda dc=dc, par=par: SC.activation(
            scr1["bf"][:, par * 1024:par * 1024 + Q], racc[dc]["f32"][:, 0:Q],
            AF.Copy)),
            reads=[racc[dc]["buf"]], writes=[scr1_h[par]])
        em.op("sc", (lambda par=par: SC.activation(
            scr2["bf"][:, par * 1024:par * 1024 + Q],
            scr1["bf"][:, par * 1024:par * 1024 + Q], AF.Square)),
            reads=[scr1_h[par]], writes=[scr2_h[par]])
        if dc > 0:
            stats_add(stat_groups, dc - 1, scr1, scr1_h, 0, last=False)
            stats_add(stat_groups, dc - 1, scr2, scr2_h, 1, last=False)
    stats_add(stat_groups, DC - 1, scr1, scr1_h, 0, last=True)
    stats_add(stat_groups, DC - 1, scr2, scr2_h, 1, last=True)

    ln_finalize()

    for dc in range(DC):
        eng = "dv"
        E = GP if eng == "g" else DV
        em.op(eng, (lambda E=E, dc=dc: E.tensor_mul(
            racc[dc]["f32"][:, 0:Q], racc[dc]["f32"][:, 0:Q],
            ab_s[0]["bf"][:, 0:Q])),
            reads=[racc[dc]["buf"], bab[0]], writes=[racc[dc]["buf"]])
        em.op(eng, (lambda E=E, dc=dc: E.tensor_add(
            racc[dc]["f32"][:, 0:Q], racc[dc]["f32"][:, 0:Q],
            ab_s[1]["bf"][:, 0:Q])),
            reads=[racc[dc]["buf"], bab[1]], writes=[racc[dc]["buf"]])
        em.op("s", (lambda dc=dc: SY.dma_start(
            out=outT[dc * 128:(dc + 1) * 128, :], in_=racc[dc]["f32"][:, 0:Q])),
            reads=[racc[dc]["buf"]], sem="d_out", amt=16)

    em.raw_wait("s", "d_out", 16 * DC)
    for i in range(dbg_n[0]):
        em.raw_wait("s", f"d_dbg{i}", 16)

    em.replay()
    return nc



def rope_tables(positions):
    half = 64
    freqs = (1.0 / 10000.0 ** (np.arange(half, dtype=np.float64) / half))
    ang = np.outer(freqs, positions.astype(np.float64))
    c = np.cos(ang)
    s = np.sin(ang)
    C = np.ascontiguousarray(np.vstack([c, c]).astype(ml_dtypes.bfloat16))
    Ssn = np.ascontiguousarray(np.vstack([-s, s]).astype(ml_dtypes.bfloat16))
    return C, Ssn


def perm_idx_heads(nheads):
    hd = 128
    order = np.concatenate([np.arange(0, hd, 2), np.arange(1, hd, 2)])
    return np.concatenate([h * hd + order for h in range(nheads)])


def prep_shared(cfg, W_q, W_k, W_v, W_o, W0, b0, W1, b1):
    D, H, G, N, DC = cfg["D"], cfg["H"], cfg["G"], cfg["N"], cfg["DC"]
    sh = {}
    Wqp = W_q[:, perm_idx_heads(H)]
    Wkp = W_k[:, perm_idx_heads(G)]
    sh["wq"] = np.ascontiguousarray(
        Wqp.reshape(DC, 128, H, 128).transpose(2, 1, 0, 3)).astype(np.float32)
    sh["wk"] = np.ascontiguousarray(
        Wkp.reshape(DC, 128, G, 128).transpose(2, 1, 0, 3)).astype(np.float32)
    sh["wv"] = np.ascontiguousarray(W_v).astype(np.float32)
    sh["wo"] = np.ascontiguousarray(
        W_o.reshape(DC, 128, DC, 128).transpose(2, 1, 0, 3)).astype(np.float32)
    sh["w0"] = np.ascontiguousarray(W0.T).astype(np.float32)
    sh["w1"] = np.ascontiguousarray(W1.T).astype(np.float32)
    sh["b0f"] = np.ascontiguousarray(b0.reshape(N // 128, 128).T).astype(np.float32)
    sh["b1f"] = np.ascontiguousarray(b1.reshape(DC, 128).T).astype(np.float32)
    sh["swm"] = np.roll(np.eye(128, dtype=np.float32), 64, axis=0).astype(
        ml_dtypes.bfloat16)
    return sh


def prep_core(cfg, shared, xb, half):
    Q = cfg["Q"]
    my = np.arange(half * Q, (half + 1) * Q)
    other = np.arange((1 - half) * Q, (2 - half) * Q)
    cols = np.concatenate([my, other])
    d = dict(shared)
    d["xT"] = np.ascontiguousarray(xb[cols, :].T).astype(np.float32)
    C, Ssn = rope_tables(cols)
    d["ctab"] = C
    d["stab"] = Ssn
    return d


_PROG_CACHE = {}


def _get_prog(key, cfg):
    if key not in _PROG_CACHE:
        _PROG_CACHE[key] = build_program(cfg)
    return _PROG_CACHE[key]


def kernel(L0_attn_x, L0_attn_W_q, L0_attn_W_k, L0_attn_W_v, L0_attn_W_o,
           L0_ffn_W0, L0_ffn_bias0, L0_ffn_W1, L0_ffn_bias1):
    cfg = CFG_FULL
    x = np.asarray(L0_attn_x, dtype=np.float32)
    W_q = np.asarray(L0_attn_W_q, dtype=np.float32)
    W_k = np.asarray(L0_attn_W_k, dtype=np.float32)
    W_v = np.asarray(L0_attn_W_v, dtype=np.float32)
    W_o = np.asarray(L0_attn_W_o, dtype=np.float32)
    W0 = np.asarray(L0_ffn_W0, dtype=np.float32)
    b0 = np.asarray(L0_ffn_bias0, dtype=np.float32)
    W1 = np.asarray(L0_ffn_W1, dtype=np.float32)
    b1 = np.asarray(L0_ffn_bias1, dtype=np.float32)

    shared = prep_shared(cfg, W_q, W_k, W_v, W_o, W0, b0, W1, b1)
    in_maps = [prep_core(cfg, shared, x[c // 2], c % 2) for c in range(8)]
    nc = _get_prog("full", cfg)
    res = run_bass_kernel_spmd(nc, in_maps, core_ids=list(range(8)))
    Q = cfg["Q"]
    out = np.empty((4, 2048, 2048), dtype=np.float32)
    for c in range(8):
        b, half = c // 2, c % 2
        out[b, half * Q:(half + 1) * Q, :] = np.asarray(res.results[c]["outT"]).T
    return out
